# version 03
# baseline (speedup 1.0000x reference)
import contextlib

import numpy as np
import ml_dtypes
import concourse.bass as bass
import concourse.mybir as mybir
from concourse.bass_utils import run_bass_kernel_spmd

F32 = mybir.dt.float32
BF16 = mybir.dt.bfloat16
I32 = mybir.dt.int32
U32 = mybir.dt.uint32
AF = mybir.ActivationFunctionType
ALU = mybir.AluOpType
AX = mybir.AxisListType.X
BF = ml_dtypes.bfloat16

NCORES = 8
T = 2048
D = 2048
E = 8
HF = 2816
CAP = 2048
NT = T // 128
KD = D // 128
EPS = 1e-5

MH = 64
MP = 64
MN = 128
CH = 256
NCH = T // CH
INTER = MH * MP
CONV_DIM = INTER + 2 * MN
OUT_IN = 2 * (INTER + MN) + MH

ENGS = ("sync", "vector", "scalar", "tensor", "gpsimd")
_PASSTHRU = {"wait_ge", "sem_inc", "nop", "Fori", "dma_start"}


class EW:

    def __init__(self, ctx, en, eng):
        self._ctx, self._en, self._eng = ctx, en, eng

    def wait(self, sem, val):
        self._eng.wait_ge(sem, val)

    @property
    def u(self):
        return self._eng

    def dma(self, out, in_, key=None, **kw):
        ctx, en = self._ctx, self._en
        inst = self._eng.dma_start(out=out, in_=in_, **kw)
        sem, rec = ctx.get_dsem(key, en)
        inst.then_inc(sem, 16)
        rec["cnt"] += 1
        return rec["cnt"]

    def idma(self, out, out_off, in_, in_off, key=None, **kw):
        ctx, en = self._ctx, self._en
        inst = self._eng.indirect_dma_start(out, out_off, in_, in_off, **kw)
        sem, rec = ctx.get_dsem(key, en)
        inst.then_inc(sem, 16)
        rec["cnt"] += 1
        return rec["cnt"]

    def __getattr__(self, name):
        if name.startswith("_"):
            raise AttributeError(name)
        fn = getattr(self._eng, name)
        if name in _PASSTHRU:
            return fn
        ctx, en = self._ctx, self._en

        def call(*a, **kw):
            self._eng.wait_ge(ctx.ch[en], ctx.ccnt[en])
            inst = fn(*a, **kw)
            inst.then_inc(ctx.ch[en], 1)
            ctx.ccnt[en] += 1
            return inst

        return call


class TE:

    def __init__(self, ctx, eng):
        self._ctx, self._eng = ctx, eng
        self._last = None

    def wait(self, sem, val):
        self._eng.wait_ge(sem, val)

    def matmul(self, *a, **kw):
        self._last = self._eng.matmul(*a, **kw)
        return self._last

    def transpose(self, *a, **kw):
        self._last = self._eng.transpose(*a, **kw)
        return self._last

    def mark(self):
        ctx = self._ctx
        assert self._last is not None
        self._last.then_inc(ctx.ch["tensor"], 1)
        ctx.ccnt["tensor"] += 1
        self._last = None
        return ctx.ccnt["tensor"]


class Ctx:
    def __init__(self, nc, stack):
        self.nc = nc
        self.bar = stack.enter_context(nc.semaphore("bar"))
        self.nbar = 0
        self.ch = {e: stack.enter_context(nc.semaphore(f"ch_{e}")) for e in ENGS}
        self.ccnt = {e: 0 for e in ENGS}
        self.block = None
        self._pool = [stack.enter_context(nc.semaphore(f"sp{i}")) for i in range(80)]
        self._pi = 0
        self.dsems = {}

    def sem(self):
        s = self._pool[self._pi]
        self._pi += 1
        return s

    def get_dsem(self, key, en=None):
        key = key if key is not None else f"_default_{en}"
        if key not in self.dsems:
            self.dsems[key] = (self.sem(), {"cnt": 0, "eng": en})
        sem, rec = self.dsems[key]
        if rec["eng"] is None:
            rec["eng"] = en
        assert en is None or rec["eng"] == en, \
            f"dma sem {key} used from {en} and {rec['eng']}"
        return sem, rec

    def dwait(self, w, key, cnt):
        sem, rec = self.get_dsem(key)
        w.wait(sem, 16 * cnt)

    def dtotal(self, key):
        return self.dsems[key][1]["cnt"] if key in self.dsems else 0

    def wrap(self, en, eng):
        return TE(self, eng) if en == "tensor" else EW(self, en, eng)

    def phase(self, fns):
        block = self.block
        self.c0 = dict(self.ccnt)
        self.d0 = {k: v[1]["cnt"] for k, v in self.dsems.items()}
        entry = self.nbar * len(ENGS)
        for en in ENGS:
            reg = getattr(block, en)

            def body(eng, en=en):
                w = self.wrap(en, eng)
                if entry:
                    eng.wait_ge(self.bar, entry)
                if en in fns:
                    fns[en](w)
                if self.ccnt[en]:
                    eng.wait_ge(self.ch[en], self.ccnt[en])
                for key, (sem, rec) in self.dsems.items():
                    if rec["eng"] == en and rec["cnt"]:
                        eng.wait_ge(sem, 16 * rec["cnt"])
                eng.sem_inc(self.bar, 1)

            reg(body)
        self.nbar += 1


def mk_nc():
    return bass.Bass("TRN2", target_bir_lowering=False, debug=False)


def emit_ln_router(nc, ctx, st, xsrc, xn_out, eidx_out, gate_out, wg_in, pre):
    sb = lambda n, s_, d: st.enter_context(nc.sbuf_tensor(pre + n, s_, d)).ap()
    xt = sb("xt", [128, NT // 2, D], F32)
    msum = sb("msum", [128, NT], F32)
    sq = sb("sq", [128, D], BF16)
    var = sb("var", [128, NT], F32)
    m_t = sb("m_t", [128, NT], F32)
    lnv = sb("lnv", [128, NT], F32)
    r_t = sb("r_t", [128, NT], F32)
    xnr = sb("xnr", [128, 4, D], BF16)
    xnf = sb("xnf", [128, KD, T], BF16)
    wgt = sb("wgt", [128, KD, E], F32)
    wgb = sb("wgb", [128, KD, E], BF16)
    lgs = sb("lgs", [128, NT, E], F32)
    mx8 = sb("mx8", [128, 8], F32)
    nmx = sb("nmx", [128, NT], F32)
    ix8 = sb("ix8", [128, 8], U32)
    eix = sb("eix", [128, NT], U32)
    gex = sb("gex", [128, NT, E], F32)
    gsum = sb("gsum", [128, NT], F32)
    gt = sb("gt", [128, NT], F32)
    epsc2 = sb("epsc2", [128, 1], F32)
    lg = st.enter_context(nc.psum_tensor(pre + "lg", [128, NT * E], F32)).ap()

    x0t = xsrc.rearrange("(n p) d -> p n d", p=128)
    xn1t = xn_out.rearrange("(n p) d -> p n d", p=128)
    HT = NT // 2
    ctx.phase({"sync": lambda s: s.dma(wgt, wg_in.rearrange("(k p) e -> p k e", p=128)),
               "gpsimd": lambda g: g.memset(epsc2, float(EPS))})
    for half in range(2):
        jo = half * HT
        ctx.phase({"sync": lambda s, jo=jo: s.dma(xt, x0t[:, jo:jo + HT, :])})

        def p1_vec(v, jo=jo):
            for j in range(HT):
                v.reduce_sum(msum[:, jo + j:jo + j + 1], xt[:, j, :], axis=AX)
        def p1_sca(s, jo=jo):
            for j in range(HT):
                s.activation(sq, xt[:, j, :], AF.Square, accum_out=var[:, jo + j:jo + j + 1])
        ctx.phase({"vector": p1_vec, "scalar": p1_sca})

        def p2_vec(v, jo=jo):
            sl = slice(jo, jo + HT)
            v.tensor_scalar(m_t[:, sl], msum[:, sl], 1.0 / D, None, ALU.mult)
            v.tensor_scalar(var[:, sl], var[:, sl], 1.0 / D, None, ALU.mult)
            v.tensor_tensor(msum[:, sl], m_t[:, sl], m_t[:, sl], ALU.mult)
            v.tensor_tensor(var[:, sl], var[:, sl], msum[:, sl], ALU.subtract)
        ctx.phase({"vector": p2_vec})
        def p3_sca(s, jo=jo):
            sl = slice(jo, jo + HT)
            s.activation(lnv[:, sl], var[:, sl], AF.Ln, bias=epsc2)
            s.activation(r_t[:, sl], lnv[:, sl], AF.Exp, scale=-0.5)
        ctx.phase({"scalar": p3_sca})

        def p4_vec(v, jo=jo):
            for j in range(HT):
                if j >= 4:
                    ctx.dwait(v, pre + f"xnr{j % 4}",
                              ctx.d0.get(pre + f"xnr{j % 4}", 0) + j // 4)
                v.tensor_scalar(xnr[:, j % 4, :], xt[:, j, :], m_t[:, jo + j:jo + j + 1],
                                r_t[:, jo + j:jo + j + 1], ALU.subtract, ALU.mult)
        def p4_sca(s, jo=jo):
            v0 = ctx.c0["vector"]
            for j in range(HT):
                s.wait(ctx.ch["vector"], v0 + j + 1)
                s.dma(xn1t[:, jo + j, :], xnr[:, j % 4, :], key=pre + f"xnr{j % 4}")
        ctx.phase({"vector": p4_vec, "scalar": p4_sca})

    ctx.phase({"vector": lambda v: v.tensor_copy(wgb, wgt)})

    def p5_sync(s):
        for k in range(KD):
            s.dma(xnf[:, k, :], xn_out[:, k * 128:(k + 1) * 128], transpose=True)
    ctx.phase({"sync": p5_sync})

    def p6_ten(t):
        for j in range(NT):
            for k in range(KD):
                t.matmul(lg[:, j * E:(j + 1) * E],
                         xnf[:, k, j * 128:(j + 1) * 128],
                         wgb[:, k, :], start=(k == 0), stop=(k == KD - 1))
        t.mark()
    ctx.phase({"tensor": p6_ten})

    def p7_vec(v):
        v.tensor_copy(lgs, lg.rearrange("p (n e) -> p n e", e=E))
        for j in range(NT):
            v.max(mx8, lgs[:, j, :])
            v.max_index(ix8, mx8, lgs[:, j, :])
            v.tensor_copy(eix[:, j:j + 1], ix8[:, 0:1])
            v.tensor_scalar(nmx[:, j:j + 1], mx8[:, 0:1], -1.0, None, ALU.mult)
    ctx.phase({"vector": p7_vec})

    def p8_sca(s):
        for j in range(NT):
            s.activation(gex[:, j, :], lgs[:, j, :], AF.Exp, bias=nmx[:, j:j + 1])
    ctx.phase({"scalar": p8_sca})
    def p9_vec(v):
        for j in range(NT):
            v.reduce_sum(gsum[:, j:j + 1], gex[:, j, :], axis=AX)
        v.reciprocal(gt, gsum)
    ctx.phase({"vector": p9_vec})

    def p10_sync(s):
        s.dma(eidx_out, eix)
        s.dma(gate_out, gt)
    ctx.phase({"sync": p10_sync})


def emit_compact(nc, ctx, st, eidx_d, xn_d, xcomp, lc_out, pre):
    sb = lambda n, s_, d: st.enter_context(nc.sbuf_tensor(pre + n, s_, d)).ap()
    eoqu = sb("eoqu", [128, NT], U32)
    eoqf = sb("eoqf", [128, NT], F32)
    ohq = sb("ohq", [128, E, NT], F32)
    cnts = sb("cnts", [1, E, NT], F32)
    csc = sb("csc", [1, E, NT], F32)
    cex = sb("cex", [1, E, NT], F32)
    tote = sb("tote", [1, E], F32)
    lcoff = sb("lcoff", [1, E], F32)
    onesT = sb("onesT", [1, NT], F32)
    onesE = sb("onesE", [1, E], F32)
    lincl = sb("lincl", [128, 128], F32)
    ones1 = sb("ones1", [1, 128], F32)
    onc = sb("onc", [128, 1], F32)
    inclq = sb("inclq", [128, E, NT], F32)
    lflat = sb("lflat", [128, NT], F32)
    tsc = sb("tsc", [128, NT], F32)
    lfi = sb("lfi", [128, NT], I32)
    xnrl = sb("xnrl", [128, 2, D], BF16)
    psK = st.enter_context(nc.psum_tensor(pre + "psK", [128, 256], F32)).ap()
    xn_t = xn_d.rearrange("(n p) d -> p n d", p=128)

    def k0_sy(sy):
        sy.dma(eoqu, eidx_d)
    def k0_g(g):
        g.memset(lincl, 1.0)
        g.affine_select(lincl, lincl, [[1, 128]], ALU.is_ge, 0.0,
                        base=0, channel_multiplier=-1)
        g.memset(ones1, 1.0)
        g.memset(onc, 1.0)
        g.memset(onesT, 1.0)
        g.memset(onesE, 1.0)
    ctx.phase({"sync": k0_sy, "gpsimd": k0_g})

    def k1_v(v):
        v.tensor_copy(eoqf, eoqu)
        for e in range(E):
            v.tensor_scalar(ohq[:, e, :], eoqf, float(e), None, ALU.is_equal)
    ctx.phase({"vector": k1_v})

    def k2_t(t):
        for e in range(E):
            t.matmul(psK[0:1, e * NT:(e + 1) * NT], onc, ohq[:, e, :],
                     start=True, stop=True)
        t.mark()
    ctx.phase({"tensor": k2_t})
    def k2b_v(v):
        v.tensor_copy(cnts, psK[0:1, 0:E * NT].rearrange("p (e n) -> p e n", e=E))
        for e in range(E):
            v.tensor_tensor_scan(csc[:, e, :], onesT, cnts[:, e, :], 0.0, ALU.mult, ALU.add)
            v.tensor_copy(tote[:, e:e + 1], csc[:, e, NT - 1:NT])
        v.tensor_tensor(cex, csc, cnts, ALU.subtract)
        v.tensor_tensor_scan(lcoff, onesE, tote, 0.0, ALU.mult, ALU.add)
        v.tensor_tensor(lcoff, lcoff, tote, ALU.subtract)
        for e in range(E):
            v.tensor_scalar_add(cex[:, e, :], cex[:, e, :], lcoff[:, e:e + 1])
    ctx.phase({"vector": k2b_v})

    def k3_t(t):
        for e in range(E):
            t.matmul(psK[:, e * NT:(e + 1) * NT], lincl, ohq[:, e, :], start=True, stop=False)
            t.matmul(psK[:, e * NT:(e + 1) * NT], ones1, cex[:, e, :], start=False, stop=True)
        t.mark()
    ctx.phase({"tensor": k3_t})
    def k3b_v(v):
        v.tensor_copy(inclq, psK[:, 0:E * NT].rearrange("p (e n) -> p e n", e=E))
        v.memset(lflat, 0.0)
        for e in range(E):
            v.tensor_scalar(tsc, inclq[:, e, :], -1.0, None, ALU.add)
            v.tensor_tensor(tsc, tsc, ohq[:, e, :], ALU.mult)
            v.tensor_tensor(lflat, lflat, tsc, ALU.add)
        v.tensor_copy(lfi, lflat)
    ctx.phase({"vector": k3b_v})
    ctx.phase({"sync": lambda s: s.dma(lc_out, tote)})

    def k4_sy(s):
        for j in range(NT):
            if j >= 2:
                kk = pre + f"ks{j % 2}"
                ctx.dwait(s, kk, ctx.d0.get(kk, 0) + j // 2)
            s.dma(xnrl[:, j % 2, :], xn_t[:, j, :], key=pre + f"xr{j % 2}")
    def k4_g(g):
        for j in range(NT):
            key = pre + f"xr{j % 2}"
            ctx.dwait(g, key, ctx.d0.get(key, 0) + j // 2 + 1)
            g.idma(xcomp, bass.IndirectOffsetOnAxis(lfi[:, j:j + 1], 0),
                   xnrl[:, j % 2, :], None,
                   bounds_check=T - 1, oob_is_err=False, key=pre + f"ks{j % 2}")
    ctx.phase({"sync": k4_sy, "gpsimd": k4_g})


def build_l1():
    nc = mk_nc()
    x0 = nc.dram_tensor("x0", [T, D], F32, kind="ExternalInput").ap()
    wg = nc.dram_tensor("wg", [D, E], F32, kind="ExternalInput").ap()
    xn1 = nc.dram_tensor("xn1", [T, D], BF16, kind="ExternalOutput").ap()
    eidx = nc.dram_tensor("eidx", [128, NT], U32, kind="ExternalOutput").ap()
    gate = nc.dram_tensor("gate", [128, NT], F32, kind="ExternalOutput").ap()
    xcomp = nc.dram_tensor("xcomp", [T, D], BF16, kind="ExternalOutput").ap()
    lc = nc.dram_tensor("lc", [1, E], F32, kind="ExternalOutput").ap()
    with contextlib.ExitStack() as st:
        ctx = Ctx(nc, st)
        ctx.block = st.enter_context(nc.Block())
        emit_ln_router(nc, ctx, st, x0, xn1, eidx, gate, wg, "r_")
        emit_compact(nc, ctx, st, eidx, xn1, xcomp, lc, "k_")
    return nc


HC = HF // 128


def build_ffn():
    nc = mk_nc()
    xin = nc.dram_tensor("xin", [T, D], F32, kind="ExternalInput").ap()
    xn = nc.dram_tensor("xn", [T, D], BF16, kind="ExternalInput").ap()
    eidx_pi = nc.dram_tensor("eidx_pi", [128, 128], U32, kind="ExternalInput").ap()
    selv = nc.dram_tensor("selv", [128, 1], F32, kind="ExternalInput").ap()
    eoq = nc.dram_tensor("eoq", [128, NT], U32, kind="ExternalInput").ap()
    gateq = nc.dram_tensor("gateq", [128, NT], F32, kind="ExternalInput").ap()
    w1r = nc.dram_tensor("w1r", [E, HC, 128, KD, 128], F32, kind="ExternalInput").ap()
    w3r = nc.dram_tensor("w3r", [E, HC, 128, KD, 128], F32, kind="ExternalInput").ap()
    w2r = nc.dram_tensor("w2r", [E, HC, 128, D], F32, kind="ExternalInput").ap()
    sw1r = nc.dram_tensor("sw1r", [HC, 128, KD, 128], F32, kind="ExternalInput").ap()
    sw3r = nc.dram_tensor("sw3r", [HC, 128, KD, 128], F32, kind="ExternalInput").ap()
    sw2r = nc.dram_tensor("sw2r", [HC, 128, D], F32, kind="ExternalInput").ap()
    xmid = nc.dram_tensor("xmid", [T, D], F32, kind="ExternalOutput").ap()

    with contextlib.ExitStack() as st:
        ctx = Ctx(nc, st)
        sb = lambda n, s_, d: st.enter_context(nc.sbuf_tensor(n, s_, d)).ap()
        ps = lambda n, s_, d: st.enter_context(nc.psum_tensor(n, s_, d)).ap()
        wbig = sb("wbig", [128, HC, D], BF16)
        hbuf = sb("hbuf", [128, HC, T], BF16)
        scr = sb("scr", [128, 2 * D], F32)
        accs = scr.rearrange("p (a b) -> p a b", a=2)
        _scrb = scr.bitcast(BF16)
        wr1 = _scrb[:, 0:2 * KD * 128].rearrange("p (s k h) -> p s k h", s=2, k=KD)
        wr3 = _scrb[:, 2 * KD * 128:4 * KD * 128].rearrange("p (s k h) -> p s k h", s=2, k=KD)
        sA = sb("sA", [128, T], BF16)
        efu = sb("efu", [128, 128], U32)
        ef = sb("ef", [128, 128], F32)
        ohg = sb("ohg", [128, 128], F32)
        rcg = sb("rcg", [128, E], F32)
        ooff = sb("ooff", [1, E], F32)
        eoqu = sb("eoqu", [128, NT], U32)
        eoqf = sb("eoqf", [128, NT], F32)
        ohq = sb("ohq", [128, E, NT], F32)
        cnts = sb("cnts", [1, E, NT], F32)
        csc = sb("csc", [1, E, NT], F32)
        cex = sb("cex", [1, E, NT], F32)
        onesT = sb("onesT", [1, NT], F32)
        gq5 = sb("gq5", [128, NT], F32)
        lincl = sb("lincl", [128, 128], F32)
        ones1 = sb("ones1", [1, 128], F32)
        onc = sb("onc", [128, 1], F32)
        inclq = sb("inclq", [128, E, NT], F32)
        masks = sb("masks", [128, E, NT], F32)
        tsc = sb("tsc", [128, NT], F32)
        selv_sb = sb("selv_sb", [128, 1], F32)
        psA = ps("psA", [128, D], F32)
        psB = ps("psB", [128, D], F32)
        xnf = wbig.rearrange("p c d -> p (c d)")[:, 0:KD * T].rearrange("p (k t) -> p k t", k=KD)
        w2res = wbig

        ctx.block = st.enter_context(nc.Block())
        xmid_t = xmid.rearrange("(n p) d -> p n d", p=128)
        xin_t = xin.rearrange("(n p) d -> p n d", p=128)

        def p0_sync(s):
            s.dma(efu, eidx_pi)
            s.dma(eoqu, eoq)
            s.dma(gq5, gateq)
            s.dma(selv_sb, selv)
        def p0_g(g):
            g.memset(lincl, 1.0)
            g.affine_select(lincl, lincl, [[1, 128]], ALU.is_ge, 0.0, base=0, channel_multiplier=-1)
            g.memset(ones1, 1.0)
            g.memset(onc, 1.0)
            g.memset(onesT, 1.0)
        ctx.phase({"sync": p0_sync, "gpsimd": p0_g})

        def p1_vec(v):
            v.tensor_copy(ef, efu)
            v.tensor_copy(eoqf, eoqu)
            v.tensor_scalar(gq5, gq5, 0.5, None, ALU.mult)
            for e in range(E):
                v.tensor_scalar(ohg, ef, float(e), None, ALU.is_equal)
                v.reduce_sum(rcg[:, e:e + 1], ohg, axis=AX)
            for e in range(E):
                v.tensor_scalar(ohq[:, e, :], eoqf, float(e), None, ALU.is_equal)
        ctx.phase({"vector": p1_vec})

        def p2_ten(t):
            t.matmul(psB[0:1, 0:E], selv_sb, rcg, start=True, stop=True)
            for e in range(E):
                t.matmul(psB[0:1, E + e * NT:E + (e + 1) * NT], onc, ohq[:, e, :],
                         start=True, stop=True)
            t.mark()
        ctx.phase({"tensor": p2_ten})
        def p2b_vec(v):
            v.tensor_copy(ooff, psB[0:1, 0:E])
            v.tensor_copy(cnts, psB[0:1, E:E + E * NT].rearrange("p (e n) -> p e n", e=E))
            for e in range(E):
                v.tensor_tensor_scan(csc[:, e, :], onesT, cnts[:, e, :], 0.0, ALU.mult, ALU.add)
            v.tensor_tensor(cex, csc, cnts, ALU.subtract)
            for e in range(E):
                v.tensor_scalar_add(cex[:, e, :], cex[:, e, :], ooff[:, e:e + 1])
        ctx.phase({"vector": p2b_vec})

        def p3_ten(t):
            for e in range(E):
                t.matmul(psB[:, e * NT:(e + 1) * NT], lincl, ohq[:, e, :], start=True, stop=False)
                t.matmul(psB[:, e * NT:(e + 1) * NT], ones1, cex[:, e, :], start=False, stop=True)
            t.mark()
        ctx.phase({"tensor": p3_ten})
        def p3b_vec(v):
            v.tensor_copy(inclq, psB[:, 0:E * NT].rearrange("p (e n) -> p e n", e=E))
            for e in range(E):
                v.tensor_scalar(tsc, inclq[:, e, :], float(CAP), None, ALU.is_le)
                v.tensor_tensor(tsc, tsc, ohq[:, e, :], ALU.mult)
                v.tensor_tensor(masks[:, e, :], tsc, gq5, ALU.mult)
        ctx.phase({"vector": p3b_vec})

        for pi in range(E + 1):
            shared = (pi == 0)
            e = pi - 1

            def pa_sync(s):
                for k in range(KD):
                    s.dma(xnf[:, k, :], xn[:, k * 128:(k + 1) * 128], transpose=True)
            ctx.phase({"sync": pa_sync})

            def pb_g(g, shared=shared, e=e):
                t0 = ctx.c0["tensor"]
                for hc in range(HC):
                    if hc >= 2:
                        g.wait_ge(ctx.ch["tensor"], t0 + hc - 1)
                    sl = hc % 2
                    if shared:
                        g.dma(wr1[:, sl, :, :], sw1r[hc], key=f"wr{sl}")
                        g.dma(wr3[:, sl, :, :], sw3r[hc], key=f"wr{sl}")
                    else:
                        g.dma(wr1[:, sl, :, :], w1r[e, hc], key=f"wr{sl}")
                        g.dma(wr3[:, sl, :, :], w3r[e, hc], key=f"wr{sl}")
            def pb_t(t):
                v0 = ctx.c0["vector"]
                s0 = ctx.c0["scalar"]
                for hc in range(HC):
                    key = f"wr{hc % 2}"
                    ctx.dwait(t, key, ctx.d0.get(key, 0) + 2 * (hc // 2 + 1))
                    if hc >= 1:
                        t.wait(ctx.ch["scalar"], s0 + hc)
                        t.wait(ctx.ch["vector"], v0 + hc)
                    sl = hc % 2
                    for kc in range(KD):
                        for nb in range(4):
                            cs = slice(nb * 512, (nb + 1) * 512)
                            t.matmul(psA[:, cs], wr1[:, sl, kc, :], xnf[:, kc, cs],
                                     start=(kc == 0), stop=(kc == KD - 1))
                    for kc in range(KD):
                        for nb in range(4):
                            cs = slice(nb * 512, (nb + 1) * 512)
                            t.matmul(psB[:, cs], wr3[:, sl, kc, :], xnf[:, kc, cs],
                                     start=(kc == 0), stop=(kc == KD - 1))
                    t.mark()
            def pb_s(s):
                t0 = ctx.c0["tensor"]
                v0 = ctx.c0["vector"]
                for hc in range(HC):
                    s.wait(ctx.ch["tensor"], t0 + hc + 1)
                    if hc >= 1:
                        s.wait(ctx.ch["vector"], v0 + hc)
                    s.activation(sA, psA[:], AF.Silu)
            def pb_v(v):
                t0 = ctx.c0["tensor"]
                s0 = ctx.c0["scalar"]
                for hc in range(HC):
                    v.wait(ctx.ch["tensor"], t0 + hc + 1)
                    v.wait(ctx.ch["scalar"], s0 + hc + 1)
                    v.tensor_tensor(hbuf[:, hc, :], sA, psB[:], ALU.mult)
            ctx.phase({"gpsimd": pb_g, "tensor": pb_t, "scalar": pb_s, "vector": pb_v})

            def pc_g(g, shared=shared, e=e):
                for hc in range(HC):
                    g.dma(w2res[:, hc, :], (sw2r[hc] if shared else w2r[e, hc]))
            ctx.phase({"gpsimd": pc_g})

            def pd_sync(s, pi=pi):
                src = xin_t if pi == 0 else xmid_t
                for sc in range(NT):
                    if sc >= 2:
                        key = f"ast{sc % 2}"
                        ctx.dwait(s, key, ctx.d0.get(key, 0) + sc // 2)
                    s.dma(accs[:, sc % 2, :], src[:, sc, :], key=f"acl{sc % 2}")
            def pd_t(t):
                v0 = ctx.c0["vector"]
                for sc in range(NT):
                    if sc >= 1:
                        t.wait(ctx.ch["vector"], v0 + sc)
                    for hc in range(HC):
                        for nb in range(4):
                            cs = slice(nb * 512, (nb + 1) * 512)
                            t.matmul(psA[:, cs], hbuf[:, hc, sc * 128:(sc + 1) * 128],
                                     w2res[:, hc, cs], start=(hc == 0), stop=(hc == HC - 1))
                    t.mark()
            def pd_v(v, pi=pi, e=e):
                t0 = ctx.c0["tensor"]
                for sc in range(NT):
                    v.wait(ctx.ch["tensor"], t0 + sc + 1)
                    key = f"acl{sc % 2}"
                    ctx.dwait(v, key, ctx.d0.get(key, 0) + sc // 2 + 1)
                    if pi == 0:
                        v.scalar_tensor_tensor(accs[:, sc % 2, :], psA[:], 0.5,
                                               accs[:, sc % 2, :], ALU.mult, ALU.add)
                    else:
                        v.scalar_tensor_tensor(accs[:, sc % 2, :], psA[:], masks[:, e, sc:sc + 1],
                                               accs[:, sc % 2, :], ALU.mult, ALU.add)
            def pd_s(s):
                v0 = ctx.c0["vector"]
                for sc in range(NT):
                    s.wait(ctx.ch["vector"], v0 + sc + 1)
                    s.dma(xmid_t[:, sc, :], accs[:, sc % 2, :], key=f"ast{sc % 2}")
            ctx.phase({"sync": pd_sync, "tensor": pd_t, "vector": pd_v, "scalar": pd_s})
    return nc


def build_ffn_exp():
    nc = mk_nc()
    xcompall = nc.dram_tensor("xcompall", [NCORES * T, D], BF16, kind="ExternalInput").ap()
    xnown = nc.dram_tensor("xnown", [T, D], BF16, kind="ExternalInput").ap()
    lcT = nc.dram_tensor("lcT", [E, NCORES], F32, kind="ExternalInput").ap()
    esel = nc.dram_tensor("esel", [128, 1], F32, kind="ExternalInput").ap()
    w1e = nc.dram_tensor("w1e", [HC, 128, KD, 128], F32, kind="ExternalInput").ap()
    w3e = nc.dram_tensor("w3e", [HC, 128, KD, 128], F32, kind="ExternalInput").ap()
    w2e = nc.dram_tensor("w2e", [HC, 128, D], F32, kind="ExternalInput").ap()
    sw1r = nc.dram_tensor("sw1r", [HC, 128, KD, 128], F32, kind="ExternalInput").ap()
    sw3r = nc.dram_tensor("sw3r", [HC, 128, KD, 128], F32, kind="ExternalInput").ap()
    sw2r = nc.dram_tensor("sw2r", [HC, 128, D], F32, kind="ExternalInput").ap()
    oe_o = nc.dram_tensor("oe_o", [CAP, D], BF16, kind="ExternalOutput").ap()
    shro = nc.dram_tensor("shro", [T, D], BF16, kind="ExternalOutput").ap()
    xd = nc.dram_tensor("xd", [CAP, D], BF16, kind="Internal").ap()
    w2b = nc.dram_tensor("w2b", [HC, 128, D], BF16, kind="Internal").ap()
    sw2b = nc.dram_tensor("sw2b", [HC, 128, D], BF16, kind="Internal").ap()

    with contextlib.ExitStack() as st:
        ctx = Ctx(nc, st)
        sb = lambda n, s_, d: st.enter_context(nc.sbuf_tensor(n, s_, d)).ap()
        ps = lambda n, s_, d: st.enter_context(nc.psum_tensor(n, s_, d)).ap()
        ctx.block = st.enter_context(nc.Block())

        castsem = ctx.sem()
        def _cast_sec(g):
            g.dma_start(out=sw2b, in_=sw2r).then_inc(castsem, 16)
            g.dma_start(out=w2b, in_=w2e).then_inc(castsem, 16)
        ctx.block.gpsimd(_cast_sec)

        with contextlib.ExitStack() as st2:
            sb2 = lambda n, s_, d: st2.enter_context(nc.sbuf_tensor(n, s_, d)).ap()
            lcts = sb2("lcts", [E, NCORES], F32)
            esl = sb2("esl", [128, 1], F32)
            iop = sb2("iop", [E, 1], F32)
            ohc = sb2("ohc", [E, 1], F32)
            ltc = sb2("ltc", [E, 1], F32)
            lcrow = sb2("lcrow", [1, NCORES], F32)
            lofr = sb2("lofr", [1, NCORES], F32)
            gexr = sb2("gexr", [1, NCORES], F32)
            oner = sb2("oner", [1, NCORES], F32)
            ones1 = sb2("ones1", [1, 128], F32)
            LCB = sb2("LCB", [128, NCORES], F32)
            LOB = sb2("LOB", [128, NCORES], F32)
            GB = sb2("GB", [128, NCORES], F32)
            Sio = sb2("Sio", [128, NT], F32)
            uu = sb2("uu", [128, NT], F32)
            tt1 = sb2("tt1", [128, NT], F32)
            vt = sb2("vt", [128, NT], F32)
            idxf = sb2("idxf", [128, NT], F32)
            idxi = sb2("idxi", [128, NT], I32)
            xg = sb2("xg", [128, 2, D], BF16)
            psD = st2.enter_context(nc.psum_tensor("psD", [128, 64], F32)).ap()
            xd_t = xd.rearrange("(n p) d -> p n d", p=128)

            def d0_sy(sy):
                sy.dma(lcts, lcT)
                sy.dma(esl, esel)
            def d0_g(g):
                g.iota(iop, [[0, 1]], channel_multiplier=1,
                       allow_small_or_imprecise_dtypes=True)
                g.iota(Sio, [[128, NT]], channel_multiplier=1,
                       allow_small_or_imprecise_dtypes=True)
                g.memset(ones1, 1.0)
                g.memset(oner, 1.0)
            ctx.phase({"sync": d0_sy, "gpsimd": d0_g})

            def d1_v(v):
                v.tensor_scalar(ohc, iop, esl[0:E, :], None, ALU.is_equal)
                v.tensor_scalar(ltc, iop, esl[0:E, :], None, ALU.is_lt)
            ctx.phase({"vector": d1_v})
            def d2_t(t):
                t.matmul(psD[0:1, 0:NCORES], ohc, lcts, start=True, stop=True)
                t.matmul(psD[0:1, NCORES:2 * NCORES], ltc, lcts, start=True, stop=True)
                t.mark()
            ctx.phase({"tensor": d2_t})
            def d3_v(v):
                v.tensor_copy(lcrow, psD[0:1, 0:NCORES])
                v.tensor_copy(lofr, psD[0:1, NCORES:2 * NCORES])
                v.tensor_tensor_scan(gexr, oner, lcrow, 0.0, ALU.mult, ALU.add)
                v.tensor_tensor(gexr, gexr, lcrow, ALU.subtract)
            ctx.phase({"vector": d3_v})
            def d4_t(t):
                t.matmul(psD[:, 2 * NCORES:3 * NCORES], ones1, lcrow, start=True, stop=True)
                t.matmul(psD[:, 3 * NCORES:4 * NCORES], ones1, lofr, start=True, stop=True)
                t.matmul(psD[:, 4 * NCORES:5 * NCORES], ones1, gexr, start=True, stop=True)
                t.mark()
            ctx.phase({"tensor": d4_t})
            def d5_v(v):
                v.tensor_copy(LCB, psD[:, 2 * NCORES:3 * NCORES])
                v.tensor_copy(LOB, psD[:, 3 * NCORES:4 * NCORES])
                v.tensor_copy(GB, psD[:, 4 * NCORES:5 * NCORES])
                v.memset(idxf, 0.0)
                for k in range(NCORES):
                    v.tensor_scalar(uu, Sio, GB[:, k:k + 1], None, ALU.subtract)
                    v.tensor_scalar(tt1, uu, 0.0, None, ALU.is_ge)
                    v.tensor_scalar(vt, uu, LCB[:, k:k + 1], None, ALU.is_lt)
                    v.tensor_tensor(tt1, tt1, vt, ALU.mult)
                    v.tensor_scalar(vt, uu, LOB[:, k:k + 1], None, ALU.add)
                    v.tensor_scalar(vt, vt, float(k * T), None, ALU.add)
                    v.tensor_tensor(vt, vt, tt1, ALU.mult)
                    v.tensor_tensor(idxf, idxf, vt, ALU.add)
                v.tensor_copy(idxi, idxf)
            ctx.phase({"vector": d5_v})

            def d6_g(g):
                for n in range(NT):
                    if n >= 2:
                        key = f"xds{n % 2}"
                        ctx.dwait(g, key, ctx.d0.get(key, 0) + n // 2)
                    g.idma(xg[:, n % 2, :], None, xcompall,
                           bass.IndirectOffsetOnAxis(idxi[:, n:n + 1], 0),
                           bounds_check=NCORES * T - 1, oob_is_err=False,
                           key=f"xgl{n % 2}")
            def d6_sy(s):
                for n in range(NT):
                    key = f"xgl{n % 2}"
                    ctx.dwait(s, key, ctx.d0.get(key, 0) + n // 2 + 1)
                    s.dma(xd_t[:, n, :], xg[:, n % 2, :], key=f"xds{n % 2}")
            ctx.phase({"gpsimd": d6_g, "sync": d6_sy})

        wbig = sb("wbig", [128, HC, D], BF16)
        hbuf = sb("hbuf", [128, HC, T], BF16)
        scr = sb("scr", [128, 2 * D], F32)
        _scrb = scr.bitcast(BF16)
        wr1 = _scrb[:, 0:2 * KD * 128].rearrange("p (s k h) -> p s k h", s=2, k=KD)
        wr3 = _scrb[:, 2 * KD * 128:4 * KD * 128].rearrange("p (s k h) -> p s k h", s=2, k=KD)
        oer = _scrb[:, 0:2 * D].rearrange("p (s d) -> p s d", s=2)
        sA = sb("sA", [128, T], BF16)
        psA = ps("psA", [128, D], F32)
        psB = ps("psB", [128, D], F32)
        xnf = wbig.rearrange("p c d -> p (c d)")[:, 0:KD * T].rearrange("p (k t) -> p k t", k=KD)
        w2res = wbig

        for pi in range(2):
            src = xnown if pi == 0 else xd
            wa, wb = (sw1r, sw3r) if pi == 0 else (w1e, w3e)
            wc = sw2b if pi == 0 else w2b
            dst = shro if pi == 0 else oe_o
            dst_t = dst.rearrange("(n p) d -> p n d", p=128)

            def pa_sync(s, src=src):
                for k in range(KD):
                    s.dma(xnf[:, k, :], src[:, k * 128:(k + 1) * 128], transpose=True)
            ctx.phase({"sync": pa_sync})

            def pb_g(g, wa=wa, wb=wb):
                t0 = ctx.c0["tensor"]
                for hc in range(HC):
                    if hc >= 2:
                        g.wait_ge(ctx.ch["tensor"], t0 + hc - 1)
                    sl = hc % 2
                    g.dma(wr1[:, sl, :, :], wa[hc], key=f"wr{sl}")
                    g.dma(wr3[:, sl, :, :], wb[hc], key=f"wr{sl}")
            def pb_t(t):
                v0 = ctx.c0["vector"]
                s0 = ctx.c0["scalar"]
                for hc in range(HC):
                    key = f"wr{hc % 2}"
                    ctx.dwait(t, key, ctx.d0.get(key, 0) + 2 * (hc // 2 + 1))
                    if hc >= 1:
                        t.wait(ctx.ch["scalar"], s0 + hc)
                        t.wait(ctx.ch["vector"], v0 + hc)
                    sl = hc % 2
                    for kc in range(KD):
                        for nb in range(4):
                            cs = slice(nb * 512, (nb + 1) * 512)
                            t.matmul(psA[:, cs], wr1[:, sl, kc, :], xnf[:, kc, cs],
                                     start=(kc == 0), stop=(kc == KD - 1))
                    for kc in range(KD):
                        for nb in range(4):
                            cs = slice(nb * 512, (nb + 1) * 512)
                            t.matmul(psB[:, cs], wr3[:, sl, kc, :], xnf[:, kc, cs],
                                     start=(kc == 0), stop=(kc == KD - 1))
                    t.mark()
            def pb_s(s):
                t0 = ctx.c0["tensor"]
                v0 = ctx.c0["vector"]
                for hc in range(HC):
                    s.wait(ctx.ch["tensor"], t0 + hc + 1)
                    if hc >= 1:
                        s.wait(ctx.ch["vector"], v0 + hc)
                    s.activation(sA, psA[:], AF.Silu)
            def pb_v(v):
                t0 = ctx.c0["tensor"]
                s0 = ctx.c0["scalar"]
                for hc in range(HC):
                    v.wait(ctx.ch["tensor"], t0 + hc + 1)
                    v.wait(ctx.ch["scalar"], s0 + hc + 1)
                    v.tensor_tensor(hbuf[:, hc, :], sA, psB[:], ALU.mult)
            ctx.phase({"gpsimd": pb_g, "tensor": pb_t, "scalar": pb_s, "vector": pb_v})

            def pc_g(g, wc=wc, pi=pi):
                if pi == 0:
                    g.wait_ge(castsem, 32)
                for hc in range(HC):
                    g.dma(w2res[:, hc, :], wc[hc])
            ctx.phase({"gpsimd": pc_g})

            def pd_t(t):
                v0 = ctx.c0["vector"]
                for sc in range(NT):
                    if sc >= 1:
                        t.wait(ctx.ch["vector"], v0 + sc)
                    for hc in range(HC):
                        for nb in range(4):
                            cs = slice(nb * 512, (nb + 1) * 512)
                            t.matmul(psA[:, cs], hbuf[:, hc, sc * 128:(sc + 1) * 128],
                                     w2res[:, hc, cs], start=(hc == 0), stop=(hc == HC - 1))
                    t.mark()
            def pd_v(v):
                t0 = ctx.c0["tensor"]
                for sc in range(NT):
                    v.wait(ctx.ch["tensor"], t0 + sc + 1)
                    if sc >= 2:
                        key = f"oes{sc % 2}"
                        ctx.dwait(v, key, ctx.d0.get(key, 0) + sc // 2)
                    v.tensor_copy(oer[:, sc % 2, :], psA[:])
            def pd_s(s, dst_t=dst_t):
                v0 = ctx.c0["vector"]
                for sc in range(NT):
                    s.wait(ctx.ch["vector"], v0 + sc + 1)
                    s.dma(dst_t[:, sc, :], oer[:, sc % 2, :], key=f"oes{sc % 2}")
            ctx.phase({"tensor": pd_t, "vector": pd_v, "scalar": pd_s})
    return nc



def emit_combine(nc, ctx, st, eidx_pi, selvlb, eoq17, gq17, oeall,
                 xb_blocks, sh_blocks, out_blocks, pre):
    sb = lambda n, s_, d: st.enter_context(nc.sbuf_tensor(pre + n, s_, d)).ap()
    NB = len(xb_blocks)
    efu = sb("efu", [128, 128], U32)
    ef = sb("ef", [128, 128], F32)
    ohg = sb("ohg", [128, 128], F32)
    rcg = sb("rcg", [128, E], F32)
    ooff = sb("ooff", [1, E], F32)
    eoqu = sb("eoqu", [128, NB], U32)
    eoqf = sb("eoqf", [128, NB], F32)
    gq = sb("gq", [128, NB], F32)
    selv = sb("selv", [128, 1], F32)
    ohq = sb("ohq", [128, E, NB], F32)
    cnts = sb("cnts", [1, E, NB], F32)
    csc = sb("csc", [1, E, NB], F32)
    cex = sb("cex", [1, E, NB], F32)
    onesT = sb("onesT", [1, NB], F32)
    lincl = sb("lincl", [128, 128], F32)
    ones1 = sb("ones1", [1, 128], F32)
    onc = sb("onc", [128, 1], F32)
    inclq = sb("inclq", [128, E, NB], F32)
    flatq = sb("flatq", [128, NB], F32)
    keepq = sb("keepq", [128, NB], F32)
    gk = sb("gk", [128, NB], F32)
    idxq = sb("idxq", [128, NB], I32)
    tsc = sb("tsc", [128, NB], F32)
    oeg = sb("oeg", [128, 2, D], BF16)
    xbt = sb("xbt", [128, 2, D], F32)
    sht = sb("sht", [128, 2, D], BF16)
    otr = sb("otr", [128, 2, D], F32)
    psC = st.enter_context(nc.psum_tensor(pre + "psC", [128, 256], F32)).ap()

    def c0_sy(sy):
        sy.dma(efu, eidx_pi)
        sy.dma(eoqu, eoq17)
        sy.dma(gq, gq17)
        sy.dma(selv, selvlb)
    def c0_g(g):
        g.memset(lincl, 1.0)
        g.affine_select(lincl, lincl, [[1, 128]], ALU.is_ge, 0.0,
                        base=0, channel_multiplier=-1)
        g.memset(ones1, 1.0)
        g.memset(onc, 1.0)
        g.memset(onesT, 1.0)
    ctx.phase({"sync": c0_sy, "gpsimd": c0_g})

    def c1_v(v):
        v.tensor_copy(ef, efu)
        v.tensor_copy(eoqf, eoqu)
        v.tensor_scalar(gq, gq, 0.5, None, ALU.mult)
        for e in range(E):
            v.tensor_scalar(ohg, ef, float(e), None, ALU.is_equal)
            v.reduce_sum(rcg[:, e:e + 1], ohg, axis=AX)
        for e in range(E):
            v.tensor_scalar(ohq[:, e, :], eoqf, float(e), None, ALU.is_equal)
    ctx.phase({"vector": c1_v})

    def c2_t(t):
        t.matmul(psC[0:1, 0:E], selv, rcg, start=True, stop=True)
        for e in range(E):
            t.matmul(psC[0:1, E + e * NB:E + (e + 1) * NB], onc, ohq[:, e, :],
                     start=True, stop=True)
        t.mark()
    ctx.phase({"tensor": c2_t})
    def c2b_v(v):
        v.tensor_copy(ooff, psC[0:1, 0:E])
        v.tensor_copy(cnts, psC[0:1, E:E + E * NB].rearrange("p (e n) -> p e n", e=E))
        for e in range(E):
            v.tensor_tensor_scan(csc[:, e, :], onesT, cnts[:, e, :], 0.0, ALU.mult, ALU.add)
        v.tensor_tensor(cex, csc, cnts, ALU.subtract)
        for e in range(E):
            v.tensor_scalar_add(cex[:, e, :], cex[:, e, :], ooff[:, e:e + 1])
    ctx.phase({"vector": c2b_v})

    def c3_t(t):
        for e in range(E):
            t.matmul(psC[:, e * NB:(e + 1) * NB], lincl, ohq[:, e, :], start=True, stop=False)
            t.matmul(psC[:, e * NB:(e + 1) * NB], ones1, cex[:, e, :], start=False, stop=True)
        t.mark()
    ctx.phase({"tensor": c3_t})
    def c3b_v(v):
        v.tensor_copy(inclq, psC[:, 0:E * NB].rearrange("p (e n) -> p e n", e=E))
        v.memset(flatq, 0.0)
        v.memset(keepq, 0.0)
        for e in range(E):
            v.tensor_scalar(tsc, inclq[:, e, :], float(CAP), None, ALU.is_le)
            v.tensor_tensor(tsc, tsc, ohq[:, e, :], ALU.mult)
            v.tensor_tensor(keepq, keepq, tsc, ALU.add)
            v.tensor_scalar(tsc, inclq[:, e, :], float(e * CAP) - 1.0, None, ALU.add)
            v.tensor_tensor(tsc, tsc, ohq[:, e, :], ALU.mult)
            v.tensor_tensor(flatq, flatq, tsc, ALU.add)
        v.tensor_scalar_min(flatq, flatq, float(E * CAP - 1))
        v.tensor_tensor(gk, gq, keepq, ALU.mult)
        v.tensor_copy(idxq, flatq)
    ctx.phase({"vector": c3b_v})

    def cl_g(g):
        v0 = ctx.c0["vector"]
        for j in range(NB):
            sl = j % 2
            if j >= 2:
                g.wait_ge(ctx.ch["vector"], v0 + 2 * (j - 1))
            g.idma(oeg[:, sl, :], None, oeall,
                   bass.IndirectOffsetOnAxis(idxq[:, j:j + 1], 0),
                   key=pre + f"oeg{sl}")
    def cl_sy(sy):
        v0 = ctx.c0["vector"]
        for j in range(NB):
            sl = j % 2
            if j >= 2:
                sy.wait(ctx.ch["vector"], v0 + 2 * (j - 1))
            sy.dma(xbt[:, sl, :], xb_blocks[j], key=pre + f"xb{sl}")
            sy.dma(sht[:, sl, :], sh_blocks[j], key=pre + f"sh{sl}")
    def cl_v(v):
        for j in range(NB):
            sl = j % 2
            n = j // 2 + 1
            ctx.dwait(v, pre + f"oeg{sl}", ctx.d0.get(pre + f"oeg{sl}", 0) + n)
            ctx.dwait(v, pre + f"xb{sl}", ctx.d0.get(pre + f"xb{sl}", 0) + n)
            ctx.dwait(v, pre + f"sh{sl}", ctx.d0.get(pre + f"sh{sl}", 0) + n)
            if j >= 2:
                ctx.dwait(v, pre + f"ot{sl}", ctx.d0.get(pre + f"ot{sl}", 0) + j // 2)
            v.scalar_tensor_tensor(otr[:, sl, :], sht[:, sl, :], 0.5,
                                   xbt[:, sl, :], ALU.mult, ALU.add)
            v.scalar_tensor_tensor(otr[:, sl, :], oeg[:, sl, :], gk[:, j:j + 1],
                                   otr[:, sl, :], ALU.mult, ALU.add)
    def cl_s(s_):
        v0 = ctx.c0["vector"]
        for j in range(NB):
            sl = j % 2
            s_.wait(ctx.ch["vector"], v0 + 2 * (j + 1))
            s_.dma(out_blocks[j], otr[:, sl, :], key=pre + f"ot{sl}")
    ctx.phase({"gpsimd": cl_g, "sync": cl_sy, "vector": cl_v, "scalar": cl_s})


def build_final():
    nc = mk_nc()
    x2 = nc.dram_tensor("x2", [T, D], F32, kind="ExternalInput").ap()
    shro = nc.dram_tensor("shro", [T, D], BF16, kind="ExternalInput").ap()
    oeall = nc.dram_tensor("oeall", [E * CAP, D], BF16, kind="ExternalInput").ap()
    eidx_pi = nc.dram_tensor("eidx_pi", [128, 128], U32, kind="ExternalInput").ap()
    selvlb = nc.dram_tensor("selvlb", [128, 1], F32, kind="ExternalInput").ap()
    eoq = nc.dram_tensor("eoq", [128, NT], U32, kind="ExternalInput").ap()
    gateq = nc.dram_tensor("gateq", [128, NT], F32, kind="ExternalInput").ap()
    out = nc.dram_tensor("out", [T, D], F32, kind="ExternalOutput").ap()
    with contextlib.ExitStack() as st:
        ctx = Ctx(nc, st)
        ctx.block = st.enter_context(nc.Block())
        x2t = x2.rearrange("(n p) d -> p n d", p=128)
        sht = shro.rearrange("(n p) d -> p n d", p=128)
        outt = out.rearrange("(n p) d -> p n d", p=128)
        emit_combine(nc, ctx, st, eidx_pi, selvlb, eoq, gateq, oeall,
                     [x2t[:, j, :] for j in range(NT)],
                     [sht[:, j, :] for j in range(NT)],
                     [outt[:, j, :] for j in range(NT)], "cb_")
    return nc


def ffn_exp_in_maps(xn_sl, xcomp_sl, lc_sl, wf):
    xcompall = np.concatenate(xcomp_sl, axis=0)
    lcT = np.ascontiguousarray(np.concatenate(lc_sl, axis=0).T)
    maps = []
    for c in range(NCORES):
        maps.append(dict(
            xcompall=xcompall, xnown=xn_sl[c], lcT=lcT,
            esel=np.full((128, 1), float(c), np.float32),
            w1e=wf["w1r"][c], w3e=wf["w3r"][c], w2e=wf["w2r"][c],
            sw1r=wf["sw1r"], sw3r=wf["sw3r"], sw2r=wf["sw2r"]))
    return maps


def final_in_maps(x2_sl, oe_sl, shro_sl, eidx_c, gate_c):
    oeall = np.concatenate(oe_sl, axis=0)
    eidx_pi = np.ascontiguousarray(
        np.concatenate([e.T for e in eidx_c], axis=0)).astype(np.uint32)
    maps = []
    for c in range(NCORES):
        maps.append(dict(
            x2=x2_sl[c], shro=shro_sl[c], oeall=oeall, eidx_pi=eidx_pi,
            selvlb=(np.arange(128) < c * NT).astype(np.float32).reshape(128, 1),
            eoq=eidx_c[c], gateq=gate_c[c]))
    return maps


TPAD = 2176
NTP = TPAD // 128
CCW = 67
CCI = CONV_DIM // 128
NB5 = [(0, 512), (512, 1024), (1024, 1536), (1536, 2048), (2048, 2176)]


def build_l3():
    nc = mk_nc()
    xb = nc.dram_tensor("xb", [T, D], F32, kind="ExternalInput").ap()
    xblb = nc.dram_tensor("xblb", [128, D], F32, kind="ExternalInput").ap()
    shro = nc.dram_tensor("shro", [T, D], BF16, kind="ExternalInput").ap()
    shlb = nc.dram_tensor("shlb", [128, D], BF16, kind="ExternalInput").ap()
    oeall = nc.dram_tensor("oeall", [NCORES * T, D], BF16, kind="ExternalInput").ap()
    eidx_pi = nc.dram_tensor("eidx_pi", [128, 128], U32, kind="ExternalInput").ap()
    selvlb = nc.dram_tensor("selvlb", [128, 1], F32, kind="ExternalInput").ap()
    eoq17 = nc.dram_tensor("eoq17", [128, NTP], U32, kind="ExternalInput").ap()
    gq17 = nc.dram_tensor("gq17", [128, NTP], F32, kind="ExternalInput").ap()
    winr = nc.dram_tensor("winr", [CCW, 128, KD, 128], F32, kind="ExternalInput").ap()
    convwr = nc.dram_tensor("convwr", [128, CCI, 4], F32, kind="ExternalInput").ap()
    convbr = nc.dram_tensor("convbr", [128, CCI], F32, kind="ExternalInput").ap()
    dtbr = nc.dram_tensor("dtbr", [MH, 1], F32, kind="ExternalInput").ap()
    alogr = nc.dram_tensor("alogr", [MH, 1], F32, kind="ExternalInput").ap()
    z_o = nc.dram_tensor("z_o", [INTER, T], BF16, kind="ExternalOutput").ap()
    xfm_o = nc.dram_tensor("xfm_o", [INTER, T], BF16, kind="ExternalOutput").ap()
    bfm_o = nc.dram_tensor("bfm_o", [MN, T], BF16, kind="ExternalOutput").ap()
    cfm_o = nc.dram_tensor("cfm_o", [MN, T], BF16, kind="ExternalOutput").ap()
    dtf_o = nc.dram_tensor("dtf_o", [MH, T], F32, kind="ExternalOutput").ap()
    acsf_o = nc.dram_tensor("acsf_o", [MH, T], F32, kind="ExternalOutput").ap()
    nacsf_o = nc.dram_tensor("nacsf_o", [MH, T], F32, kind="ExternalOutput").ap()
    ce_o = nc.dram_tensor("ce_o", [MH, NCH], F32, kind="ExternalOutput").ap()
    stn_o = nc.dram_tensor("stn_o", [NCH, MN, INTER], BF16, kind="ExternalOutput").ap()
    xmid_o = nc.dram_tensor("xmid_o", [T, D], F32, kind="ExternalOutput").ap()
    xn2 = nc.dram_tensor("xn2", [TPAD, D], BF16, kind="Internal").ap()
    ddec = nc.dram_tensor("ddec", [MH, T], BF16, kind="Internal").ap()
    xlb_d = nc.dram_tensor("xlb_d", [128, D], F32, kind="Internal").ap()

    with contextlib.ExitStack() as st:
        ctx = Ctx(nc, st)
        sb = lambda n, s_, d: st.enter_context(nc.sbuf_tensor(n, s_, d)).ap()
        ctx.block = st.enter_context(nc.Block())

        with contextlib.ExitStack() as stC:
            xbt_ = xb.rearrange("(n p) d -> p n d", p=128)
            sht_ = shro.rearrange("(n p) d -> p n d", p=128)
            xmt_ = xmid_o.rearrange("(n p) d -> p n d", p=128)
            emit_combine(nc, ctx, stC, eidx_pi, selvlb, eoq17, gq17, oeall,
                         [xblb] + [xbt_[:, j, :] for j in range(NT)],
                         [shlb] + [sht_[:, j, :] for j in range(NT)],
                         [xlb_d] + [xmt_[:, j, :] for j in range(NT)], "cb_")

        cw = sb("cw", [128, CCI, 4], F32)
        cb = sb("cb", [128, CCI], F32)
        dtb_t = sb("dtb_t", [MH, 1], F32)
        nega = sb("nega", [MH, 1], F32)
        dtft = sb("dtft", [MH, T], F32)
        acst = sb("acst", [MH, T], F32)
        tmp64 = sb("tmp64", [MH, T], F32)
        cet = sb("cet", [MH, NCH], F32)
        onc64 = sb("onc64", [MH, 1], F32)
        ones_b = sb("ones_b", [128, 128], BF16)
        xn2t = xn2.rearrange("(n p) d -> p n d", p=128)

        def p0(sy):
            sy.dma(cw, convwr)
            sy.dma(cb, convbr)
            sy.dma(dtb_t, dtbr)
            sy.dma(nega, alogr)
        def p0g(g):
            g.memset(onc64, 1.0)
            g.memset(ones_b, 1.0)
        ctx.phase({"sync": p0, "gpsimd": p0g})
        ctx.phase({"scalar": lambda s_: s_.activation(nega, nega, AF.Exp)})
        ctx.phase({"vector": lambda v: v.tensor_scalar(nega, nega, -1.0, None, ALU.mult)})

        with nc.sbuf_tensor("lnx", [128, NTP, D], F32) as _a, \
             nc.sbuf_tensor("lsq", [128, D], BF16) as _b, \
             nc.sbuf_tensor("lms", [128, NTP], F32) as _c, \
             nc.sbuf_tensor("lvr", [128, NTP], F32) as _d, \
             nc.sbuf_tensor("lm", [128, NTP], F32) as _e, \
             nc.sbuf_tensor("llv", [128, NTP], F32) as _f, \
             nc.sbuf_tensor("lr", [128, NTP], F32) as _g, \
             nc.sbuf_tensor("lnr", [128, 4, D], BF16) as _h, \
             nc.sbuf_tensor("lneps", [128, 1], F32) as _i:
            lnx, lsq, lms, lvr, lm, llv, lr, lnr, lneps = (
                _a.ap(), _b.ap(), _c.ap(), _d.ap(), _e.ap(), _f.ap(), _g.ap(), _h.ap(), _i.ap())

            def q0_g(g):
                g.memset(lnx[:, NTP - 1, :], 0.0)
                g.memset(lneps, float(EPS))
            def q0_sy(sy):
                sy.dma(lnx[0:3, 0, :], xlb_d[125:128, :])
                sy.dma(lnx[3:128, 0, :], xmid_o[0:125])
                sy.dma(lnx[:, 1:NTP - 1, :],
                       xmid_o[125:2045].rearrange("(n p) d -> p n d", p=128))
            ctx.phase({"gpsimd": q0_g, "sync": q0_sy})
            ctx.phase({"sync": lambda sy: sy.dma(lnx[0:3, NTP - 1, :], xmid_o[2045:2048])})

            def q1_v(v):
                for j in range(NTP):
                    v.reduce_sum(lms[:, j:j + 1], lnx[:, j, :], axis=AX)
            def q1_s(s_):
                for j in range(NTP):
                    s_.activation(lsq, lnx[:, j, :], AF.Square, accum_out=lvr[:, j:j + 1])
            ctx.phase({"vector": q1_v, "scalar": q1_s})

            def q2_v(v):
                v.tensor_scalar(lm, lms, 1.0 / D, None, ALU.mult)
                v.tensor_scalar(lvr, lvr, 1.0 / D, None, ALU.mult)
                v.tensor_tensor(lms, lm, lm, ALU.mult)
                v.tensor_tensor(lvr, lvr, lms, ALU.subtract)
            ctx.phase({"vector": q2_v})
            def q3_s(s_):
                s_.activation(llv, lvr, AF.Ln, bias=lneps)
                s_.activation(lr, llv, AF.Exp, scale=-0.5)
            ctx.phase({"scalar": q3_s})

            def q4_v(v):
                for j in range(NTP):
                    if j >= 4:
                        ctx.dwait(v, f"l3nr{j % 4}", j // 4)
                    v.tensor_scalar(lnr[:, j % 4, :], lnx[:, j, :], lm[:, j:j + 1],
                                    lr[:, j:j + 1], ALU.subtract, ALU.mult)
            def q4_s(s_):
                v0 = ctx.c0["vector"]
                for j in range(NTP):
                    s_.wait(ctx.ch["vector"], v0 + j + 1)
                    s_.dma(xn2t[:, j, :], lnr[:, j % 4, :], key=f"l3nr{j % 4}")
            ctx.phase({"vector": q4_v, "scalar": q4_s})

        with nc.sbuf_tensor("xn2f", [128, KD, TPAD], BF16) as _a, \
             nc.sbuf_tensor("wrin", [128, 2, KD, 128], BF16) as _b, \
             nc.sbuf_tensor("evb", [128, 2, TPAD], BF16) as _c, \
             nc.sbuf_tensor("xbcp", [128, TPAD], BF16) as _d, \
             nc.sbuf_tensor("cdg", [128, 4, 128], BF16) as _e, \
             nc.psum_tensor("psW", [128, TPAD], F32) as _p:
            xn2f, wrin, evb, xbcp, cdg, psW = (
                _a.ap(), _b.ap(), _c.ap(), _d.ap(), _e.ap(), _p.ap())
            psC = psW[:, 0:2048]

            def r0_sy(sy):
                for k in range(KD):
                    sy.dma(xn2f[:, k, :], xn2[:, k * 128:(k + 1) * 128], transpose=True)
            ctx.phase({"sync": r0_sy})

            def win_mms(t, sl):
                for kc in range(KD):
                    for (a, b) in NB5:
                        t.matmul(psW[:, a:b], wrin[:, sl, kc, :], xn2f[:, kc, a:b],
                                 start=(kc == 0), stop=(kc == KD - 1))
                t.mark()

            ctx.phase({"gpsimd": lambda g: g.dma(wrin[:, 0, :, :], winr[66])})
            ctx.phase({"tensor": lambda t: win_mms(t, 0)})
            dtraw = psW[0:64, 3:2051]
            ctx.phase({"vector": lambda v: v.tensor_scalar_add(tmp64, dtraw, dtb_t)})
            with nc.sbuf_tensor("spa", [MH, T], F32) as _sa, \
                 nc.sbuf_tensor("spl", [MH, T], F32) as _sl2, \
                 nc.sbuf_tensor("dect", [MH, T], F32) as _dc, \
                 nc.sbuf_tensor("ddecb", [MH, T], BF16) as _db, \
                 nc.sbuf_tensor("onec", [MH, 1], F32) as _oc:
                spa, spl, dect, ddecb, onec = _sa.ap(), _sl2.ap(), _dc.ap(), _db.ap(), _oc.ap()
                ctx.phase({"gpsimd": lambda g: g.memset(onec, 1.0)})
                def s1_s(s_):
                    s_.activation(spa, tmp64, AF.Abs)
                    s_.activation(spa, spa, AF.Exp, scale=-1.0)
                    s_.activation(spl, spa, AF.Ln, bias=onec)
                ctx.phase({"scalar": s1_s})
                def s2_v(v):
                    v.tensor_scalar_max(dtft, tmp64, 0.0)
                    v.tensor_tensor(dtft, dtft, spl, ALU.add)
                    v.tensor_scalar_min(dtft, dtft, 100.0)
                    v.tensor_scalar(tmp64, dtft, nega, None, ALU.mult)
                    for c in range(NCH):
                        cs = slice(c * CH, (c + 1) * CH)
                        v.tensor_tensor_scan(acst[:, cs],
                                             onc64.broadcast_to((MH, 1, CH))[:, 0, :],
                                             tmp64[:, cs], 0.0, ALU.mult, ALU.add)
                    v.tensor_copy(cet, acst.rearrange("h (c l) -> h c l", l=CH)[:, :, CH - 1])
                ctx.phase({"vector": s2_v})
                def s3_s(s_):
                    for c in range(NCH):
                        cs = slice(c * CH, (c + 1) * CH)
                        s_.activation(dect[:, cs], acst[:, cs], AF.Exp, scale=-1.0,
                                      bias=cet[:, c:c + 1])
                ctx.phase({"scalar": s3_s})
                def s4_v(v):
                    v.tensor_tensor(dect, dtft, dect, ALU.mult)
                    v.tensor_copy(ddecb, dect)
                    v.tensor_scalar(tmp64, acst, -1.0, None, ALU.mult)
                ctx.phase({"vector": s4_v})
                def s5_sy(sy):
                    sy.dma(dtf_o, dtft)
                    sy.dma(acsf_o, acst)
                    sy.dma(nacsf_o, tmp64)
                    sy.dma(ce_o, cet)
                    sy.dma(ddec, ddecb)
                ctx.phase({"sync": s5_sy})

            with nc.sbuf_tensor("cvt", [128, 2, 2048], F32) as _cv:
                cvt = _cv.ap()

                def ci_g(g):
                    t0 = ctx.c0["tensor"]
                    for ci in range(CCI):
                        sl = ci % 2
                        if ci >= 2:
                            g.wait_ge(ctx.ch["tensor"], t0 + ci - 1)
                        g.dma(wrin[:, sl, :, :], winr[32 + ci], key=f"wri{sl}")
                def ci_t(t):
                    v0 = ctx.c0["vector"]
                    for ci in range(CCI):
                        sl = ci % 2
                        key = f"wri{sl}"
                        ctx.dwait(t, key, ctx.d0.get(key, 0) + ci // 2 + 1)
                        if ci >= 1:
                            t.wait(ctx.ch["vector"], v0 + ci)
                        win_mms(t, sl)
                def ci_v(v):
                    t0 = ctx.c0["tensor"]
                    s0 = ctx.c0["scalar"]
                    for ci in range(CCI):
                        sl = ci % 2
                        v.wait(ctx.ch["tensor"], t0 + ci + 1)
                        if ci >= 2:
                            v.wait(ctx.ch["scalar"], s0 + ci - 1)
                        v.u.tensor_scalar(cvt[:, sl, :], psW[:, 0:2048],
                                          cw[:, ci, 0:1], None, ALU.mult)
                        for k in range(1, 4):
                            op = v.u.scalar_tensor_tensor if k < 3 \
                                else v.scalar_tensor_tensor
                            op(cvt[:, sl, :], psW[:, k:k + 2048], cw[:, ci, k:k + 1],
                               cvt[:, sl, :], ALU.mult, ALU.add)
                def ci_s(s_):
                    v0 = ctx.c0["vector"]
                    for ci in range(CCI):
                        sl = ci % 2
                        s_.wait(ctx.ch["vector"], v0 + ci + 1)
                        if ci >= 2:
                            key = f"evs{sl}"
                            ctx.dwait(s_, key, ctx.d0.get(key, 0) + ci // 2)
                        s_.activation(evb[:, sl, 0:2048], cvt[:, sl, :], AF.Silu,
                                      bias=cb[:, ci:ci + 1])
                def ci_sy(sy):
                    s0 = ctx.c0["scalar"]
                    for ci in range(CCI):
                        sl = ci % 2
                        sy.wait_ge(ctx.ch["scalar"], s0 + ci + 1)
                        if ci < 32:
                            sy.dma(xfm_o[ci * 128:(ci + 1) * 128, :],
                                   evb[:, sl, 0:2048], key=f"evs{sl}")
                        elif ci == 32:
                            sy.dma(bfm_o, evb[:, sl, 0:2048], key=f"evs{sl}")
                        else:
                            sy.dma(cfm_o, evb[:, sl, 0:2048], key=f"evs{sl}")
                ctx.phase({"gpsimd": ci_g, "tensor": ci_t, "vector": ci_v,
                           "scalar": ci_s, "sync": ci_sy})

            def cz_g(g):
                t0 = ctx.c0["tensor"]
                for cz in range(32):
                    sl = cz % 2
                    if cz >= 2:
                        g.wait_ge(ctx.ch["tensor"], t0 + cz - 1)
                    g.dma(wrin[:, sl, :, :], winr[cz], key=f"wri{sl}")
            def cz_t(t):
                v0 = ctx.c0["vector"]
                for cz in range(32):
                    sl = cz % 2
                    key = f"wri{sl}"
                    ctx.dwait(t, key, ctx.d0.get(key, 0) + cz // 2 + 1)
                    if cz >= 1:
                        t.wait(ctx.ch["vector"], v0 + cz)
                    win_mms(t, sl)
            def cz_v(v):
                t0 = ctx.c0["tensor"]
                for cz in range(32):
                    sl = cz % 2
                    v.wait(ctx.ch["tensor"], t0 + cz + 1)
                    if cz >= 2:
                        key = f"zvs{sl}"
                        ctx.dwait(v, key, ctx.d0.get(key, 0) + cz // 2)
                    v.tensor_copy(evb[:, sl, 0:2048], psW[:, 3:2051])
            def cz_sy(sy):
                v0 = ctx.c0["vector"]
                for cz in range(32):
                    sl = cz % 2
                    sy.wait_ge(ctx.ch["vector"], v0 + cz + 1)
                    sy.dma(z_o[cz * 128:(cz + 1) * 128, :], evb[:, sl, 0:2048],
                           key=f"zvs{sl}")
            ctx.phase({"gpsimd": cz_g, "tensor": cz_t, "vector": cz_v,
                       "sync": cz_sy})

        with nc.sbuf_tensor("xtok", [128, 2, INTER], BF16) as _a, \
             nc.sbuf_tensor("xdd", [128, 2, INTER], BF16) as _b, \
             nc.sbuf_tensor("ddtok", [128, 2, MH], BF16) as _c, \
             nc.sbuf_tensor("btok", [128, 2, MN], BF16) as _d, \
             nc.sbuf_tensor("stnb", [128, INTER], BF16) as _e, \
             nc.psum_tensor("psSt", [128, INTER], F32) as _p:
            xtok, xdd, ddtok, btok, stnb, psSt = (
                _a.ap(), _b.ap(), _c.ap(), _d.ap(), _e.ap(), _p.ap())
            for c in range(NCH):
                def w0(sy, c=c):
                    for sh in range(2):
                        j = 2 * c + sh
                        sy.dma(xtok[:, sh, :], xfm_o[:, j * 128:(j + 1) * 128], transpose=True)
                        sy.dma(ddtok[:, sh, :], ddec[:, j * 128:(j + 1) * 128], transpose=True)
                        sy.dma(btok[:, sh, :], bfm_o[:, j * 128:(j + 1) * 128], transpose=True)
                ctx.phase({"sync": w0})
                def w1(v, c=c):
                    for sh in range(2):
                        v.tensor_tensor(xdd[:, sh, :].rearrange("p (h q) -> p h q", q=MP),
                                        xtok[:, sh, :].rearrange("p (h q) -> p h q", q=MP),
                                        ddtok[:, sh, :].broadcast_to((128, MH, MP)),
                                        ALU.mult)
                ctx.phase({"vector": w1})
                def w2(t, c=c):
                    for nb in range(INTER // 512):
                        for sh in range(2):
                            t.matmul(psSt[:, nb * 512:(nb + 1) * 512], btok[:, sh, :],
                                     xdd[:, sh, nb * 512:(nb + 1) * 512],
                                     start=(sh == 0), stop=(sh == 1))
                    t.mark()
                ctx.phase({"tensor": w2})
                ctx.phase({"vector": lambda v: v.tensor_copy(stnb, psSt)})
                ctx.phase({"sync": lambda sy, c=c: sy.dma(stn_o[c], stnb)})
    return nc


NG8 = 8
_L4_UPTO = 9


def build_l4():
    nc = mk_nc()
    xm = nc.dram_tensor("xm", [T, D], F32, kind="ExternalInput").ap()
    zfm = nc.dram_tensor("zfm", [INTER, T], BF16, kind="ExternalInput").ap()
    xfm = nc.dram_tensor("xfm", [INTER, T], BF16, kind="ExternalInput").ap()
    xtk = nc.dram_tensor("xtk", [T, INTER], BF16, kind="ExternalInput").ap()
    bfm = nc.dram_tensor("bfm", [MN, T], BF16, kind="ExternalInput").ap()
    cfm = nc.dram_tensor("cfm", [MN, T], BF16, kind="ExternalInput").ap()
    dtt = nc.dram_tensor("dtt", [T, MH], F32, kind="ExternalInput").ap()
    acsf = nc.dram_tensor("acsf", [MH, T], F32, kind="ExternalInput").ap()
    nactk = nc.dram_tensor("nactk", [T, MH], F32, kind="ExternalInput").ap()
    ce16t = nc.dram_tensor("ce16t", [16, MH], F32, kind="ExternalInput").ap()
    msk16 = nc.dram_tensor("msk16", [16, 1], F32, kind="ExternalInput").ap()
    st16 = nc.dram_tensor("st16", [MH, 16, MP, MN], BF16, kind="ExternalInput").ap()
    dexp = nc.dram_tensor("dexp", [128, 32], F32, kind="ExternalInput").ap()
    nrmw = nc.dram_tensor("nrmw", [128, 32], F32, kind="ExternalInput").ap()
    wout = nc.dram_tensor("wout", [INTER, D], F32, kind="ExternalInput").ap()
    wg = nc.dram_tensor("wg", [D, E], F32, kind="ExternalInput").ap()
    x2_o = nc.dram_tensor("x2_o", [T, D], F32, kind="ExternalOutput").ap()
    xn3_o = nc.dram_tensor("xn3_o", [T, D], BF16, kind="ExternalOutput").ap()
    eidx = nc.dram_tensor("eidx", [128, NT], U32, kind="ExternalOutput").ap()
    gate = nc.dram_tensor("gate", [128, NT], F32, kind="ExternalOutput").ap()
    xcomp = nc.dram_tensor("xcomp", [T, D], BF16, kind="ExternalOutput").ap()
    lc = nc.dram_tensor("lc", [1, E], F32, kind="ExternalOutput").ap()
    wbd_d = nc.dram_tensor("wbd_d", [NG8, 128, MH], BF16, kind="Internal").ap()
    wob = nc.dram_tensor("wob", [INTER, D], BF16, kind="Internal").ap()
    yraw = nc.dram_tensor("yraw", [32, NCH, 128, CH], BF16, kind="Internal").ap()
    yf_d = nc.dram_tensor("yf_d", [32, 128, T], BF16, kind="Internal").ap()

    with contextlib.ExitStack() as st:
        ctx = Ctx(nc, st)
        sb = lambda n, s_, d: st.enter_context(nc.sbuf_tensor(n, s_, d)).ap()
        ctx.block = st.enter_context(nc.Block())

        castsem = ctx.sem()
        def _cast_sec(g):
            g.dma_start(out=wob, in_=wout).then_inc(castsem, 16)
        ctx.block.gpsimd(_cast_sec)

        nrmq = sb("nrmq", [128, 32], F32)
        ones_b = sb("ones_b", [128, 128], BF16)
        one1 = sb("one1", [1, 128], F32)
        onc128 = sb("onc128", [128, 1], BF16)
        rb = sb("rb", [128, D], F32)
        epsc = sb("epsc", [128, 1], F32)
        idf64 = sb("idf64", [MH, MH], F32)

        def p0(sy):
            sy.dma(nrmq, nrmw)
        def p0g(g):
            g.memset(ones_b, 1.0)
            g.memset(one1, 1.0)
            g.memset(onc128, 1.0)
            g.memset(epsc, float(EPS))
            g.memset(idf64, 0.0)
            g.affine_select(idf64, idf64, [[-1, MH]], ALU.not_equal, 1.0,
                            base=0, channel_multiplier=1)
        ctx.phase({"sync": p0, "gpsimd": p0g})

        with nc.sbuf_tensor("si", [128, MH, NCH, MP], BF16) as _si, \
             nc.sbuf_tensor("dexq", [128, 32], F32) as _dx, \
             nc.sbuf_tensor("ddg", [128, 32, 128], BF16) as _dd, \
             nc.sbuf_tensor("dtq", [128, NT, MH], F32) as _dq, \
             nc.sbuf_tensor("naq", [128, NT, MH], F32) as _nq, \
             nc.sbuf_tensor("acs_s", [MH, T], F32) as _ac:
            si, dexq, ddg = _si.ap(), _dx.ap(), _dd.ap()
            dtq, naq, acs_s = _dq.ap(), _nq.ap(), _ac.ap()

            def ab0_sy(sy):
                sy.dma(dexq, dexp)
                sy.dma(dtq, dtt.rearrange("(n p) h -> p n h", p=128))
                sy.dma(naq, nactk.rearrange("(n p) h -> p n h", p=128))
                sy.dma(acs_s, acsf)
            ctx.phase({"sync": ab0_sy})
            def p1_v(v):
                for h2 in range(32):
                    v.tensor_scalar(ddg[:, h2, :], ones_b, dexq[:, h2:h2 + 1], None, ALU.mult)
            ctx.phase({"vector": p1_v})
            def p1_g(g):
                for h2 in range(32):
                    g.affine_select(ddg[:, h2, :], ddg[:, h2, :], [[-1, 128]],
                                    ALU.is_equal, 0.0, base=0, channel_multiplier=1)
            ctx.phase({"gpsimd": p1_g})

            with contextlib.ExitStack() as stA:
                sbA = lambda n, s_, d: stA.enter_context(nc.sbuf_tensor(n, s_, d)).ap()
                psA_ = lambda n, s_, d: stA.enter_context(nc.psum_tensor(n, s_, d)).ap()
                ce16s = sbA("ce16s", [16, MH], F32)
                m16 = sbA("m16", [16, 1], F32)
                umask = sbA("umask", [16, NCH], F32)
                rhsu = sbA("rhsu", [16, MH, NCH], F32)
                pcsr = sbA("pcsr", [1, NCH * MH], F32)
                lin16 = sbA("lin16", [16, 16], F32)
                s16 = sbA("s16", [16, MH], F32)
                argt = sbA("argt", [16, MH, NCH], F32)
                mzs = sbA("mzs", [16, MH, NCH], F32)
                wtb = sbA("wtb", [16, MH, NCH], BF16)
                on16 = sbA("on16", [16, 1], F32)
                zrow = sbA("zrow", [128, NG8 * MH], BF16)
                wbd = sbA("wbd", [128, NG8, MH], BF16)
                stg = sbA("stg", [128, 2, 8192], BF16)
                psSm = psA_("psSm", [16, 1600], F32)
                psq = psA_("psq", [128, 16, MH], F32)
                def a0_sy(sy):
                    sy.dma(ce16s, ce16t)
                    sy.dma(m16, msk16)
                def a0_g(g):
                    g.memset(on16, 1.0)
                    g.memset(zrow, 0.0)
                    g.memset(umask, 1.0)
                    g.affine_select(umask, umask, [[1, NCH]], ALU.is_ge, 0.0,
                                    base=7, channel_multiplier=-1)
                    g.memset(mzs, 1.0)
                    g.affine_select(mzs, mzs, [[0, MH], [1, NCH]], ALU.is_ge, 0.0,
                                    base=7, channel_multiplier=-1)
                    g.memset(lin16, 1.0)
                    g.affine_select(lin16, lin16, [[1, 16]], ALU.is_ge, 0.0,
                                    base=0, channel_multiplier=-1)
                ctx.phase({"sync": a0_sy, "gpsimd": a0_g})
                if _L4_UPTO <= 0.2:
                    return nc
                def az_sy(sy):
                    for gg in range(NG8):
                        sy.dma(wbd_d[gg], zrow[:, 0:MH])
                ctx.phase({"sync": az_sy})

                def a1_v(v):
                    v.tensor_tensor(rhsu,
                                    umask.rearrange("j (o z) -> j o z", o=1)
                                         .broadcast_to((16, MH, NCH)),
                                    ce16s.rearrange("j h -> j h").broadcast_to((16, MH, NCH)),
                                    ALU.mult)
                ctx.phase({"vector": a1_v})
                if _L4_UPTO <= 0.3:
                    return nc
                def a2_t(t):
                    t.matmul(psSm[0:1, 0:512], on16,
                             rhsu.rearrange("j h z -> j (h z)"), start=True, stop=True)
                    t.matmul(psSm[:, 512:512 + MH], lin16, ce16s, start=True, stop=True)
                    t.mark()
                ctx.phase({"tensor": a2_t})
                def a3_v(v):
                    v.tensor_copy(pcsr, psSm[0:1, 0:512])
                    v.tensor_copy(s16, psSm[:, 512:512 + MH])
                ctx.phase({"vector": a3_v})
                def a4_t(t):
                    t.matmul(psSm[:, 1024:1536], one1[0:1, 0:16], pcsr, start=True, stop=True)
                    t.mark()
                ctx.phase({"tensor": a4_t})
                if _L4_UPTO <= 0.4:
                    return nc
                def a5_v(v):
                    v.tensor_tensor(argt,
                                    psSm[:, 1024:1536].rearrange("c (h z) -> c h z", h=MH),
                                    s16.broadcast_to((16, MH, NCH)), ALU.subtract)
                    v.tensor_scalar_min(argt, argt, 0.0)
                ctx.phase({"vector": a5_v})
                ctx.phase({"scalar": lambda s_: s_.activation(argt, argt, AF.Exp)})
                def a7_v(v):
                    v.tensor_tensor(argt, argt, mzs, ALU.mult)
                    v.tensor_scalar(wtb, argt, m16, None, ALU.mult)
                ctx.phase({"vector": a7_v})
                if _L4_UPTO <= 0.5:
                    return nc
                def a8_g(g):
                    for gg in range(NG8):
                        for hh in range(NG8):
                            g.dma(wbd_d[gg, hh * 16:(hh + 1) * 16, hh * 8:(hh + 1) * 8],
                                  wtb[:, gg * 8 + hh, :])
                ctx.phase({"gpsimd": a8_g})
                ctx.phase({"sync": lambda sy: sy.dma(wbd, wbd_d.rearrange("g p z -> p g z"))})
                if _L4_UPTO <= 0.6:
                    return nc

                def rec_sy(sy):
                    v0 = ctx.c0["vector"]
                    for gg in range(NG8):
                        if gg >= 2:
                            sy.wait_ge(ctx.ch["vector"], v0 + (gg - 1) * 8)
                        sy.dma(stg[:, gg % 2, :], st16[gg * 8:(gg + 1) * 8]
                               .rearrange("h c p n -> (h c) (p n)"), key=f"stg{gg % 2}")
                def rec_t(t):
                    v0 = ctx.c0["vector"]
                    for gg in range(NG8):
                        ctx.dwait(t, f"stg{gg % 2}", gg // 2 + 1)
                        for cb in range(8):
                            kb = gg * 8 + cb
                            if kb >= 2:
                                t.wait(ctx.ch["vector"], v0 + kb - 1)
                            for ci in range(8):
                                ch = cb * 8 + ci
                                t.matmul(psq[:, (kb % 2) * 8 + ci, :],
                                         stg[:, gg % 2, ch * 128:(ch + 1) * 128],
                                         wbd[:, gg, :], start=True, stop=True)
                            t.mark()
                def rec_v(v):
                    t0 = ctx.c0["tensor"]
                    for gg in range(NG8):
                        for cb in range(8):
                            kb = gg * 8 + cb
                            v.wait(ctx.ch["tensor"], t0 + kb + 1)
                            for ci in range(8):
                                ch = cb * 8 + ci
                                dst = si[:, gg * 8:(gg + 1) * 8, :, ch] \
                                    .rearrange("n h z -> n (h z)")
                                if ci < 7:
                                    v.u.tensor_copy(dst, psq[:, (kb % 2) * 8 + ci, :])
                                else:
                                    v.tensor_copy(dst, psq[:, (kb % 2) * 8 + ci, :])
                ctx.phase({"sync": rec_sy, "tensor": rec_t, "vector": rec_v})

            if _L4_UPTO <= 1:
                return nc
            with contextlib.ExitStack() as stB:
                sbB = lambda n, s_, d: stB.enter_context(nc.sbuf_tensor(n, s_, d)).ap()
                psB_ = lambda n, s_, d: stB.enter_context(nc.psum_tensor(n, s_, d)).ap()
                xfh = sbB("xfh", [128, 32, 1024], BF16)
                gg2 = sbB("gg2", [128, 2, CH], BF16)
                bch = sbB("bch", [128, 2, 128], BF16)
                cch = sbB("cch", [128, CH], BF16)
                xdt = sbB("xdt", [128, 2, INTER], BF16)
                stt = sbB("stt", [128, 2, 2, 2, CH], BF16)
                sdt = sbB("sdt", [128, 2, 2, CH], BF16)
                esb = sbB("esb", [128, 2, 2, 2, CH], BF16)
                estin = sbB("estin", [128, 2, 2, 2, CH], F32)
                csd = sbB("csd", [128, 2, 2, CH], BF16)
                yrb = sbB("yrb", [128, 2, CH], BF16)
                psY = psB_("psY", [128, 2, CH], F32)
                psrb = psB_("psrb", [128, 2, 2, CH], F32)
                psG = psB_("psG", [128, 2, CH], F32)
                for half in range(2):
                    ctx.phase({"sync": lambda sy, half=half: sy.dma(
                        xfh, xfm[:, half * 1024:(half + 1) * 1024]
                        .rearrange("(b p) t -> p b t", p=128))})
                    for cc_ in range(4):
                        c = half * 4 + cc_
                        def cp_sy(sy, c=c):
                            for sh in range(2):
                                j = 2 * c + sh
                                sy.dma(xdt[:, sh, :], xtk[j * 128:(j + 1) * 128, :])
                                sy.dma(bch[:, sh, :],
                                       bfm[:, c * CH + sh * 128:c * CH + (sh + 1) * 128])
                            sy.dma(cch, cfm[:, c * CH:(c + 1) * CH])
                        ctx.phase({"sync": cp_sy})
                        def cp_t(t, c=c):
                            for sh in range(2):
                                t.matmul(psG[:, sh, :], bch[:, sh, :], cch,
                                         start=True, stop=True)
                            t.mark()
                        def cp_v(v, c=c):
                            for sh in range(2):
                                v.tensor_tensor(
                                    xdt[:, sh, :].rearrange("p (h q) -> p h q", q=MP),
                                    xdt[:, sh, :].rearrange("p (h q) -> p h q", q=MP),
                                    dtq[:, 2 * c + sh, :].rearrange("p (h o) -> p h o", o=1)
                                       .broadcast_to((128, MH, MP)), ALU.mult)
                        ctx.phase({"tensor": cp_t, "vector": cp_v})
                        def cp_v2(v):
                            for sh in range(2):
                                v.tensor_copy(gg2[:, sh, :], psG[:, sh, :])
                        ctx.phase({"vector": cp_v2})
                        def cp_g(g):
                            for sh in range(2):
                                g.affine_select(gg2[:, sh, :], gg2[:, sh, :], [[1, CH]],
                                                ALU.is_ge, 0.0, base=-sh * 128,
                                                channel_multiplier=-1)
                        ctx.phase({"gpsimd": cp_g})

                        def yl_t(t, c=c, cc_=cc_):
                            v0 = ctx.c0["vector"]
                            s0 = ctx.c0["scalar"]
                            for h2 in range(32):
                                s2 = h2 % 2
                                if h2 >= 2:
                                    t.wait(ctx.ch["vector"], v0 + 4 * (h2 - 2) + 1)
                                    t.wait(ctx.ch["scalar"], s0 + 2 * (h2 - 2) + 2)
                                for hp in range(2):
                                    sel = idf64[:, 2 * h2 + hp:2 * h2 + hp + 1] \
                                        .broadcast_to((MH, 1, 128))[:, 0, :]
                                    t.matmul(psrb[:, s2, hp, :], sel,
                                             acs_s[:, c * CH:(c + 1) * CH],
                                             start=True, stop=True)
                                t.mark()
                                t.wait(ctx.ch["vector"], v0 + 4 * h2 + 3)
                                if h2 >= 2:
                                    t.wait(ctx.ch["vector"], v0 + 4 * (h2 - 2) + 4)
                                for hp in range(2):
                                    h = 2 * h2 + hp
                                    po = psY[hp * 64:(hp + 1) * 64, s2, :]
                                    tp = (0, hp * 64)
                                    t.matmul(po, ddg[:, h2, hp * 64:(hp + 1) * 64],
                                             xfh[:, h2, cc_ * 256:(cc_ + 1) * 256],
                                             start=True, stop=False, tile_position=tp)
                                    for sh in range(2):
                                        t.matmul(po, xdt[:, sh, h * MP:(h + 1) * MP],
                                                 stt[:, s2, hp, sh, :], start=False,
                                                 stop=False, tile_position=tp)
                                    t.matmul(po, si[:, h, c, :], csd[:, s2, hp, :],
                                             start=False, stop=True, tile_position=tp)
                                t.mark()
                        def yl_s(s_, c=c):
                            t0 = ctx.c0["tensor"]
                            v0 = ctx.c0["vector"]
                            for h2 in range(32):
                                s2 = h2 % 2
                                s_.wait(ctx.ch["vector"], v0 + 4 * h2 + 1)
                                for hp in range(2):
                                    for sh in range(2):
                                        op = s_.u.activation if (hp, sh) != (1, 1) \
                                            else s_.activation
                                        op(esb[:, s2, hp, sh, :],
                                           estin[:, s2, hp, sh, :], AF.Exp)
                                s_.wait(ctx.ch["tensor"], t0 + 2 * h2 + 1)
                                if h2 >= 2:
                                    s_.wait(ctx.ch["vector"], v0 + 4 * (h2 - 2) + 3)
                                s_.u.activation(sdt[:, s2, 0, :], psrb[:, s2, 0, :], AF.Exp)
                                s_.activation(sdt[:, s2, 1, :], psrb[:, s2, 1, :], AF.Exp)
                        def yl_v(v, c=c):
                            t0 = ctx.c0["tensor"]
                            s0 = ctx.c0["scalar"]
                            for h2 in range(32):
                                s2 = h2 % 2
                                v.wait(ctx.ch["tensor"], t0 + 2 * h2 + 1)
                                if h2 >= 2:
                                    v.wait(ctx.ch["scalar"], s0 + 2 * (h2 - 2) + 1)
                                for hp in range(2):
                                    for sh in range(2):
                                        op = v.u.tensor_scalar if (hp, sh) != (1, 1) \
                                            else v.tensor_scalar
                                        op(estin[:, s2, hp, sh, :], psrb[:, s2, hp, :],
                                           naq[:, 2 * c + sh,
                                               2 * h2 + hp:2 * h2 + hp + 1],
                                           0.0, ALU.add, ALU.min)
                                v.wait(ctx.ch["scalar"], s0 + 2 * h2 + 1)
                                for hp in range(2):
                                    for sh in range(2):
                                        op = v.u.tensor_tensor if (hp, sh) != (1, 1) \
                                            else v.tensor_tensor
                                        op(stt[:, s2, hp, sh, :], esb[:, s2, hp, sh, :],
                                           gg2[:, sh, :], ALU.mult)
                                v.wait(ctx.ch["scalar"], s0 + 2 * h2 + 2)
                                v.u.tensor_tensor(csd[:, s2, 0, :], cch,
                                                  sdt[:, s2, 0, :], ALU.mult)
                                v.tensor_tensor(csd[:, s2, 1, :], cch,
                                                sdt[:, s2, 1, :], ALU.mult)
                                v.wait(ctx.ch["tensor"], t0 + 2 * h2 + 2)
                                if h2 >= 2:
                                    key = f"yrb{s2}"
                                    ctx.dwait(v, key, ctx.d0.get(key, 0) + h2 // 2)
                                v.tensor_copy(yrb[:, s2, :], psY[:, s2, :])
                        def yl_g(g, c=c):
                            v0 = ctx.c0["vector"]
                            for h2 in range(32):
                                g.wait_ge(ctx.ch["vector"], v0 + 4 * h2 + 4)
                                g.dma(yraw[h2, c], yrb[:, h2 % 2, :], key=f"yrb{h2 % 2}")
                        ctx.phase({"tensor": yl_t, "scalar": yl_s, "vector": yl_v,
                                   "gpsimd": yl_g})

        if _L4_UPTO <= 2:
            return nc
        with contextlib.ExitStack() as stB2:
            sbC = lambda n, s_, d: stB2.enter_context(nc.sbuf_tensor(n, s_, d)).ap()
            zbl = sbC("zbl", [128, 2, T], BF16)
            szb = sbC("szb", [128, T], BF16)
            ybl = sbC("ybl", [128, 2, T], BF16)
            yfb = sbC("yfb", [128, 2, T], BF16)
            sqb = sbC("sqb", [128, 2, T], BF16)
            psV = stB2.enter_context(nc.psum_tensor("psV", [1, D], F32)).ap()
            def b2_sy(sy):
                s0 = ctx.c0["scalar"]
                v0 = ctx.c0["vector"]
                for h2 in range(32):
                    if h2 >= 2:
                        sy.wait_ge(ctx.ch["scalar"], s0 + 2 * (h2 - 1))
                        sy.wait_ge(ctx.ch["vector"], v0 + (h2 - 1))
                    sy.dma(zbl[:, h2 % 2, :], zfm[h2 * 128:(h2 + 1) * 128, :],
                           key=f"zl{h2 % 2}")
                    sy.dma(ybl[:, h2 % 2, :].rearrange("p (c l) -> p c l", l=CH),
                           yraw[h2].rearrange("c p l -> p c l"), key=f"zl{h2 % 2}")
            def b2_s(s_):
                v0 = ctx.c0["vector"]
                t0 = ctx.c0["tensor"]
                for h2 in range(32):
                    ctx.dwait(s_, f"zl{h2 % 2}", 2 * (h2 // 2 + 1))
                    s_.activation(szb, zbl[:, h2 % 2, :], AF.Silu)
                    s_.wait(ctx.ch["vector"], v0 + h2 + 1)
                    if h2 >= 2:
                        s_.wait(ctx.ch["tensor"], t0 + h2 - 1)
                    s_.activation(sqb[:, h2 % 2, :], yfb[:, h2 % 2, :], AF.Square)
            def b2_v(v):
                s0 = ctx.c0["scalar"]
                for h2 in range(32):
                    v.wait(ctx.ch["scalar"], s0 + 2 * h2 + 1)
                    if h2 >= 2:
                        ctx.dwait(v, f"yfs{h2 % 2}", h2 // 2)
                    v.tensor_tensor(yfb[:, h2 % 2, :], ybl[:, h2 % 2, :], szb, ALU.mult)
            def b2_t(t):
                s0 = ctx.c0["scalar"]
                for h2 in range(32):
                    t.wait(ctx.ch["scalar"], s0 + 2 * h2 + 2)
                    for nb in range(4):
                        cs = slice(nb * 512, (nb + 1) * 512)
                        t.matmul(psV[0:1, cs], onc128, sqb[:, h2 % 2, cs],
                                 start=(h2 == 0), stop=(h2 == 31))
                    t.mark()
            def b2_g(g):
                v0 = ctx.c0["vector"]
                for h2 in range(32):
                    g.wait_ge(ctx.ch["vector"], v0 + h2 + 1)
                    g.dma(yf_d[h2], yfb[:, h2 % 2, :], key=f"yfs{h2 % 2}")
            ctx.phase({"sync": b2_sy, "scalar": b2_s, "vector": b2_v,
                       "tensor": b2_t, "gpsimd": b2_g})

            with contextlib.ExitStack() as stRS:
                vrow = stRS.enter_context(nc.sbuf_tensor("vrow", [1, D], F32)).ap()
                psRB = stRS.enter_context(nc.psum_tensor("psRB", [128, D], F32)).ap()
                ctx.phase({"vector": lambda v: v.tensor_scalar(
                    vrow, psV, 1.0 / INTER, None, ALU.mult)})
                def rs_s(s_):
                    s_.activation(vrow, vrow, AF.Ln, bias=epsc[0:1, :])
                    s_.activation(vrow, vrow, AF.Exp, scale=-0.5)
                ctx.phase({"scalar": rs_s})
                def rs_t(t):
                    for nb in range(4):
                        cs = slice(nb * 512, (nb + 1) * 512)
                        t.matmul(psRB[:, cs], one1, vrow[:, cs], start=True, stop=True)
                    t.mark()
                ctx.phase({"tensor": rs_t})
                ctx.phase({"vector": lambda v: v.tensor_copy(rb, psRB)})

        if _L4_UPTO <= 3:
            return nc
        with contextlib.ExitStack() as stC:
            sbD = lambda n, s_, d: stC.enter_context(nc.sbuf_tensor(n, s_, d)).ap()
            wos = sbD("wos", [128, 32, 1024], BF16)
            ynh = sbD("ynh", [128, 32, 1024], BF16)
            yfl = sbD("yfl", [128, 2, 1024], BF16)
            xmt = sbD("xmt", [128, 2, 1024], F32)
            x2r = sbD("x2r", [128, 2, 1024], F32)
            psO = stC.enter_context(nc.psum_tensor("psO", [128, 1024], F32)).ap()
            x2t_d = x2_o.rearrange("(n p) d -> p n d", p=128)
            xmt_d = xm.rearrange("(n p) d -> p n d", p=128)

            for half in range(2):
                def c1_sy(sy, half=half):
                    v0 = ctx.c0["vector"]
                    for h2 in range(32):
                        if h2 >= 2:
                            sy.wait_ge(ctx.ch["vector"], v0 + 2 * (h2 - 1))
                        sy.dma(yfl[:, h2 % 2, :],
                               yf_d[h2, :, half * 1024:(half + 1) * 1024],
                               key=f"yfl{h2 % 2}")
                def c1_v(v, half=half):
                    for h2 in range(32):
                        key = f"yfl{h2 % 2}"
                        ctx.dwait(v, key, ctx.d0.get(key, 0) + h2 // 2 + 1)
                        v.tensor_tensor(yfl[:, h2 % 2, :], yfl[:, h2 % 2, :],
                                        rb[:, half * 1024:(half + 1) * 1024], ALU.mult)
                        v.tensor_scalar(ynh[:, h2, :], yfl[:, h2 % 2, :],
                                        nrmq[:, h2:h2 + 1], None, ALU.mult)
                ctx.phase({"sync": c1_sy, "vector": c1_v})

                for dhalf in range(2):
                    def wl_g(g, dhalf=dhalf):
                        g.wait_ge(castsem, 16)
                        for h2 in range(32):
                            g.dma(wos[:, h2, :],
                                  wob[h2 * 128:(h2 + 1) * 128,
                                      dhalf * 1024:(dhalf + 1) * 1024])
                    ctx.phase({"gpsimd": wl_g})

                    def c2_sy(sy, half=half, dhalf=dhalf):
                        v0 = ctx.c0["vector"]
                        for tc_ in range(NT // 2):
                            tc = half * (NT // 2) + tc_
                            if tc_ >= 2:
                                sy.wait_ge(ctx.ch["vector"], v0 + tc_ - 1)
                            sy.dma(xmt[:, tc_ % 2, :],
                                   xmt_d[:, tc, dhalf * 1024:(dhalf + 1) * 1024],
                                   key=f"xmt{tc_ % 2}")
                    def c2_t(t, half=half):
                        v0 = ctx.c0["vector"]
                        for tc_ in range(NT // 2):
                            if tc_ >= 1:
                                t.wait(ctx.ch["vector"], v0 + tc_)
                            for h2 in range(32):
                                for nb in range(2):
                                    cs = slice(nb * 512, (nb + 1) * 512)
                                    t.matmul(psO[:, cs],
                                             ynh[:, h2, tc_ * 128:(tc_ + 1) * 128],
                                             wos[:, h2, cs],
                                             start=(h2 == 0), stop=(h2 == 31))
                            t.mark()
                    def c2_v(v, half=half):
                        t0 = ctx.c0["tensor"]
                        for tc_ in range(NT // 2):
                            v.wait(ctx.ch["tensor"], t0 + tc_ + 1)
                            key = f"xmt{tc_ % 2}"
                            ctx.dwait(v, key, ctx.d0.get(key, 0) + tc_ // 2 + 1)
                            if tc_ >= 2:
                                key2 = f"x2s{tc_ % 2}"
                                ctx.dwait(v, key2, ctx.d0.get(key2, 0) + tc_ // 2)
                            v.tensor_tensor(x2r[:, tc_ % 2, :], psO, xmt[:, tc_ % 2, :],
                                            ALU.add)
                    def c2_s(s_, half=half, dhalf=dhalf):
                        v0 = ctx.c0["vector"]
                        for tc_ in range(NT // 2):
                            tc = half * (NT // 2) + tc_
                            s_.wait(ctx.ch["vector"], v0 + tc_ + 1)
                            s_.dma(x2t_d[:, tc, dhalf * 1024:(dhalf + 1) * 1024],
                                   x2r[:, tc_ % 2, :], key=f"x2s{tc_ % 2}")
                    ctx.phase({"sync": c2_sy, "tensor": c2_t, "vector": c2_v,
                               "scalar": c2_s})

        if _L4_UPTO <= 4:
            return nc
        emit_ln_router(nc, ctx, st, x2_o, xn3_o, eidx, gate, wg, "q_")
        with contextlib.ExitStack() as stK:
            emit_compact(nc, ctx, stK, eidx, xn3_o, xcomp, lc, "k2_")
    return nc


def prep_ffn_arrays(inputs):
    W1 = np.ascontiguousarray(inputs["L0_ffn_W1"], dtype=np.float32)
    W3 = np.ascontiguousarray(inputs["L0_ffn_W3"], dtype=np.float32)
    W2 = np.ascontiguousarray(inputs["L0_ffn_W2"], dtype=np.float32)
    sw1 = np.ascontiguousarray(inputs["L0_ffn_sw1"], dtype=np.float32)
    sw3 = np.ascontiguousarray(inputs["L0_ffn_sw3"], dtype=np.float32)
    sw2 = np.ascontiguousarray(inputs["L0_ffn_sw2"], dtype=np.float32)
    w1r = np.ascontiguousarray(W1.reshape(E, KD, 128, HC, 128).transpose(0, 3, 2, 1, 4))
    w3r = np.ascontiguousarray(W3.reshape(E, KD, 128, HC, 128).transpose(0, 3, 2, 1, 4))
    w2r = np.ascontiguousarray(W2.reshape(E, HC, 128, D))
    sw1r = np.ascontiguousarray(sw1.reshape(KD, 128, HC, 128).transpose(2, 1, 0, 3))
    sw3r = np.ascontiguousarray(sw3.reshape(KD, 128, HC, 128).transpose(2, 1, 0, 3))
    sw2r = np.ascontiguousarray(sw2.reshape(HC, 128, D))
    return dict(w1r=w1r, w3r=w3r, w2r=w2r, sw1r=sw1r, sw3r=sw3r, sw2r=sw2r)


def ffn_in_maps(x_sl, xn_sl, eidx_cores, gate_cores, wf):
    eidx_pi = np.ascontiguousarray(
        np.concatenate([eidx_cores[c].T for c in range(NCORES)], axis=0))
    maps = []
    for c in range(NCORES):
        selv = (np.arange(128) < c * NT).astype(np.float32).reshape(128, 1)
        maps.append(dict(
            xin=x_sl[c], xn=xn_sl[c], eidx_pi=eidx_pi, selv=selv,
            eoq=np.ascontiguousarray(eidx_cores[c]),
            gateq=np.ascontiguousarray(gate_cores[c]), **wf))
    return maps


_CACHE = {}


def _prog(name, builder):
    if name not in _CACHE:
        _CACHE[name] = builder()
    return _CACHE[name]


def _run(nc, maps):
    return run_bass_kernel_spmd(nc, maps, core_ids=list(range(NCORES)))


def kernel(**inputs):
    f32 = np.float32
    ca = np.ascontiguousarray
    x = ca(inputs["L0_attn_hidden_states"], dtype=f32)
    xf = x.reshape(NCORES * T, D)
    x_sl = [ca(xf[c * T:(c + 1) * T]) for c in range(NCORES)]
    wg = ca(inputs["L0_ffn_Wg"], dtype=f32)
    wf = prep_ffn_arrays(inputs)

    W_in = ca(inputs["L0_attn_W_in"], dtype=f32)
    winp = np.zeros((D, CCW * 128), f32)
    winp[:, :OUT_IN] = W_in
    winr = ca(winp.reshape(KD, 128, CCW, 128).transpose(2, 1, 0, 3))
    conv_w = ca(inputs["L0_attn_conv_w"], dtype=f32)
    conv_b = ca(inputs["L0_attn_conv_b"], dtype=f32)
    convwr = ca(conv_w.T.reshape(CCI, 128, 4).transpose(1, 0, 2))
    convbr = ca(conv_b.reshape(CCI, 128).T)
    dtbr = ca(inputs["L0_attn_dt_bias"], dtype=f32).reshape(MH, 1)
    alogr = ca(inputs["L0_attn_A_log"], dtype=f32).reshape(MH, 1)
    dexp = ca(np.repeat(ca(inputs["L0_attn_D"], dtype=f32), MP).reshape(32, 128).T)
    nrmw = ca(ca(inputs["L0_attn_norm_weight"], dtype=f32).reshape(32, 128).T)
    wout = ca(inputs["L0_attn_W_out"], dtype=f32)

    r1 = _run(_prog("l1", build_l1),
              [{"x0": x_sl[c], "wg": wg} for c in range(NCORES)])
    xn1 = [r1.results[c]["xn1"] for c in range(NCORES)]
    ei1 = [r1.results[c]["eidx"] for c in range(NCORES)]
    ga1 = [r1.results[c]["gate"] for c in range(NCORES)]
    xc1 = [r1.results[c]["xcomp"] for c in range(NCORES)]
    lc1 = [r1.results[c]["lc"] for c in range(NCORES)]

    r2 = _run(_prog("fexp", build_ffn_exp), ffn_exp_in_maps(xn1, xc1, lc1, wf))
    oe1 = [r2.results[c]["oe_o"] for c in range(NCORES)]
    sh1 = [r2.results[c]["shro"] for c in range(NCORES)]

    oeall1 = np.concatenate(oe1, axis=0)
    eip1 = ca(np.concatenate([e.T for e in ei1], axis=0)).astype(np.uint32)
    z128f = np.zeros((128, D), f32)
    z128b = np.zeros((128, D), BF)
    ncol_u = np.full((128, 1), 255, np.uint32)
    zcol_f = np.zeros((128, 1), f32)
    maps3 = []
    for c in range(NCORES):
        odd = c % 2 == 1
        xblb = ca(xf[c * T - 128:c * T]) if odd else z128f
        shlb = ca(sh1[c - 1][T - 128:]) if odd else z128b
        lbq = ca(ei1[c - 1][:, NT - 1:NT]) if odd else ncol_u
        lbg = ca(ga1[c - 1][:, NT - 1:NT]) if odd else zcol_f
        selv = (np.arange(128) < (c * NT - 1 if odd else c * NT)
                ).astype(f32).reshape(128, 1)
        maps3.append(dict(
            xb=x_sl[c], xblb=xblb, shro=sh1[c], shlb=shlb, oeall=oeall1,
            eidx_pi=eip1, selvlb=selv,
            eoq17=ca(np.concatenate([lbq, ei1[c]], axis=1)),
            gq17=ca(np.concatenate([lbg, ga1[c]], axis=1)),
            winr=winr, convwr=convwr, convbr=convbr, dtbr=dtbr, alogr=alogr))
    r3 = _run(_prog("l3", build_l3), maps3)
    xmid = [r3.results[c]["xmid_o"] for c in range(NCORES)]

    maps4 = []
    zst = np.zeros((MH, NCH, MP, MN), BF)
    zce = np.zeros((NCH, MH), f32)
    mk_even = np.concatenate([np.zeros((8, 1)), np.ones((8, 1))]).astype(f32)
    mk_odd = np.ones((16, 1), f32)
    for c in range(NCORES):
        o = r3.results[c]
        own_st = ca(o["stn_o"].reshape(NCH, MN, MH, MP).transpose(2, 0, 3, 1))
        own_ce = ca(o["ce_o"].T)
        if c % 2 == 1:
            po = r3.results[c - 1]
            par_st = ca(po["stn_o"].reshape(NCH, MN, MH, MP).transpose(2, 0, 3, 1))
            par_ce = ca(po["ce_o"].T)
            msk = mk_odd
        else:
            par_st, par_ce, msk = zst.transpose(0, 1, 2, 3), zce, mk_even
        st16 = ca(np.concatenate([par_st, own_st], axis=1))
        ce16 = ca(np.concatenate([par_ce, own_ce], axis=0))
        maps4.append(dict(
            xm=xmid[c], zfm=o["z_o"], xfm=o["xfm_o"],
            xtk=ca(o["xfm_o"].T), bfm=o["bfm_o"], cfm=o["cfm_o"],
            dtt=ca(o["dtf_o"].T), acsf=o["acsf_o"], nactk=ca(o["nacsf_o"].T),
            ce16t=ce16, msk16=msk, st16=st16, dexp=dexp, nrmw=nrmw,
            wout=wout, wg=wg))
    r4 = _run(_prog("l4", build_l4), maps4)
    x2 = [r4.results[c]["x2_o"] for c in range(NCORES)]
    xn3 = [r4.results[c]["xn3_o"] for c in range(NCORES)]
    ei2 = [r4.results[c]["eidx"] for c in range(NCORES)]
    ga2 = [r4.results[c]["gate"] for c in range(NCORES)]
    xc2 = [r4.results[c]["xcomp"] for c in range(NCORES)]
    lc2 = [r4.results[c]["lc"] for c in range(NCORES)]

    r5 = _run(_prog("fexp", build_ffn_exp), ffn_exp_in_maps(xn3, xc2, lc2, wf))
    oe2 = [r5.results[c]["oe_o"] for c in range(NCORES)]
    sh2 = [r5.results[c]["shro"] for c in range(NCORES)]

    r6 = _run(_prog("fin", build_final), final_in_maps(x2, oe2, sh2, ei2, ga2))
    out = np.concatenate([r6.results[c]["out"] for c in range(NCORES)], axis=0)
    return out.reshape(4, 4096, 2048)


# version 04
# speedup vs baseline: 1.1599x; 1.1599x over previous
import contextlib

import numpy as np
import ml_dtypes
import concourse.bass as bass
import concourse.mybir as mybir
from concourse.bass_utils import run_bass_kernel_spmd

F32 = mybir.dt.float32
BF16 = mybir.dt.bfloat16
I32 = mybir.dt.int32
U32 = mybir.dt.uint32
AF = mybir.ActivationFunctionType
ALU = mybir.AluOpType
AX = mybir.AxisListType.X
BF = ml_dtypes.bfloat16

NCORES = 8
T = 2048
D = 2048
E = 8
HF = 2816
CAP = 2048
NT = T // 128
KD = D // 128
EPS = 1e-5

MH = 64
MP = 64
MN = 128
CH = 256
NCH = T // CH
INTER = MH * MP
CONV_DIM = INTER + 2 * MN
OUT_IN = 2 * (INTER + MN) + MH

ENGS = ("sync", "vector", "scalar", "tensor", "gpsimd")
_PASSTHRU = {"wait_ge", "sem_inc", "nop", "Fori", "dma_start"}


class EW:

    def __init__(self, ctx, en, eng):
        self._ctx, self._en, self._eng = ctx, en, eng

    def wait(self, sem, val):
        self._eng.wait_ge(sem, val)

    @property
    def u(self):
        return self._eng

    def dma(self, out, in_, key=None, **kw):
        ctx, en = self._ctx, self._en
        inst = self._eng.dma_start(out=out, in_=in_, **kw)
        sem, rec = ctx.get_dsem(key, en)
        inst.then_inc(sem, 16)
        rec["cnt"] += 1
        return rec["cnt"]

    def idma(self, out, out_off, in_, in_off, key=None, **kw):
        ctx, en = self._ctx, self._en
        inst = self._eng.indirect_dma_start(out, out_off, in_, in_off, **kw)
        sem, rec = ctx.get_dsem(key, en)
        inst.then_inc(sem, 16)
        rec["cnt"] += 1
        return rec["cnt"]

    def __getattr__(self, name):
        if name.startswith("_"):
            raise AttributeError(name)
        fn = getattr(self._eng, name)
        if name in _PASSTHRU:
            return fn
        ctx, en = self._ctx, self._en

        def call(*a, **kw):
            self._eng.wait_ge(ctx.ch[en], ctx.ccnt[en])
            inst = fn(*a, **kw)
            inst.then_inc(ctx.ch[en], 1)
            ctx.ccnt[en] += 1
            return inst

        return call


class TE:

    def __init__(self, ctx, eng):
        self._ctx, self._eng = ctx, eng
        self._last = None

    def wait(self, sem, val):
        self._eng.wait_ge(sem, val)

    def matmul(self, *a, **kw):
        self._last = self._eng.matmul(*a, **kw)
        return self._last

    def transpose(self, *a, **kw):
        self._last = self._eng.transpose(*a, **kw)
        return self._last

    def mark(self):
        ctx = self._ctx
        assert self._last is not None
        self._last.then_inc(ctx.ch["tensor"], 1)
        ctx.ccnt["tensor"] += 1
        self._last = None
        return ctx.ccnt["tensor"]


class Ctx:
    def __init__(self, nc, stack):
        self.nc = nc
        self.bar = stack.enter_context(nc.semaphore("bar"))
        self.nbar = 0
        self.ch = {e: stack.enter_context(nc.semaphore(f"ch_{e}")) for e in ENGS}
        self.ccnt = {e: 0 for e in ENGS}
        self.block = None
        self._pool = [stack.enter_context(nc.semaphore(f"sp{i}")) for i in range(80)]
        self._pi = 0
        self.dsems = {}

    def sem(self):
        s = self._pool[self._pi]
        self._pi += 1
        return s

    def get_dsem(self, key, en=None):
        key = key if key is not None else f"_default_{en}"
        if key not in self.dsems:
            self.dsems[key] = (self.sem(), {"cnt": 0, "eng": en})
        sem, rec = self.dsems[key]
        if rec["eng"] is None:
            rec["eng"] = en
        assert en is None or rec["eng"] == en, \
            f"dma sem {key} used from {en} and {rec['eng']}"
        return sem, rec

    def dwait(self, w, key, cnt):
        sem, rec = self.get_dsem(key)
        w.wait(sem, 16 * cnt)

    def dtotal(self, key):
        return self.dsems[key][1]["cnt"] if key in self.dsems else 0

    def wrap(self, en, eng):
        return TE(self, eng) if en == "tensor" else EW(self, en, eng)

    def phase(self, fns):
        block = self.block
        self.c0 = dict(self.ccnt)
        self.d0 = {k: v[1]["cnt"] for k, v in self.dsems.items()}
        entry = self.nbar * len(ENGS)
        for en in ENGS:
            reg = getattr(block, en)

            def body(eng, en=en):
                w = self.wrap(en, eng)
                if entry:
                    eng.wait_ge(self.bar, entry)
                if en in fns:
                    fns[en](w)
                if self.ccnt[en]:
                    eng.wait_ge(self.ch[en], self.ccnt[en])
                for key, (sem, rec) in self.dsems.items():
                    if rec["eng"] == en and rec["cnt"]:
                        eng.wait_ge(sem, 16 * rec["cnt"])
                eng.sem_inc(self.bar, 1)

            reg(body)
        self.nbar += 1


def mk_nc():
    return bass.Bass("TRN2", target_bir_lowering=False, debug=False)


def emit_ln_router(nc, ctx, st, xsrc, xn_out, eidx_out, gate_out, wg_in, pre):
    sb = lambda n, s_, d: st.enter_context(nc.sbuf_tensor(pre + n, s_, d)).ap()
    xt = sb("xt", [128, NT // 2, D], F32)
    msum = sb("msum", [128, NT], F32)
    sq = sb("sq", [128, D], BF16)
    var = sb("var", [128, NT], F32)
    m_t = sb("m_t", [128, NT], F32)
    lnv = sb("lnv", [128, NT], F32)
    r_t = sb("r_t", [128, NT], F32)
    xnr = sb("xnr", [128, 4, D], BF16)
    xnf = sb("xnf", [128, KD, T], BF16)
    wgt = sb("wgt", [128, KD, E], F32)
    wgb = sb("wgb", [128, KD, E], BF16)
    lgs = sb("lgs", [128, NT, E], F32)
    mx8 = sb("mx8", [128, 8], F32)
    nmx = sb("nmx", [128, NT], F32)
    ix8 = sb("ix8", [128, 8], U32)
    eix = sb("eix", [128, NT], U32)
    gex = sb("gex", [128, NT, E], F32)
    gsum = sb("gsum", [128, NT], F32)
    gt = sb("gt", [128, NT], F32)
    epsc2 = sb("epsc2", [128, 1], F32)
    lg = st.enter_context(nc.psum_tensor(pre + "lg", [128, NT * E], F32)).ap()

    x0t = xsrc.rearrange("(n p) d -> p n d", p=128)
    xn1t = xn_out.rearrange("(n p) d -> p n d", p=128)
    HT = NT // 2
    ctx.phase({"sync": lambda s: s.dma(wgt, wg_in.rearrange("(k p) e -> p k e", p=128)),
               "gpsimd": lambda g: g.memset(epsc2, float(EPS))})
    for half in range(2):
        jo = half * HT
        ctx.phase({"sync": lambda s, jo=jo: s.dma(xt, x0t[:, jo:jo + HT, :])})

        def p1_vec(v, jo=jo):
            for j in range(HT):
                v.reduce_sum(msum[:, jo + j:jo + j + 1], xt[:, j, :], axis=AX)
        def p1_sca(s, jo=jo):
            for j in range(HT):
                s.activation(sq, xt[:, j, :], AF.Square, accum_out=var[:, jo + j:jo + j + 1])
        ctx.phase({"vector": p1_vec, "scalar": p1_sca})

        def p2_vec(v, jo=jo):
            sl = slice(jo, jo + HT)
            v.tensor_scalar(m_t[:, sl], msum[:, sl], 1.0 / D, None, ALU.mult)
            v.tensor_scalar(var[:, sl], var[:, sl], 1.0 / D, None, ALU.mult)
            v.tensor_tensor(msum[:, sl], m_t[:, sl], m_t[:, sl], ALU.mult)
            v.tensor_tensor(var[:, sl], var[:, sl], msum[:, sl], ALU.subtract)
        ctx.phase({"vector": p2_vec})
        def p3_sca(s, jo=jo):
            sl = slice(jo, jo + HT)
            s.activation(lnv[:, sl], var[:, sl], AF.Ln, bias=epsc2)
            s.activation(r_t[:, sl], lnv[:, sl], AF.Exp, scale=-0.5)
        ctx.phase({"scalar": p3_sca})

        def p4_vec(v, jo=jo):
            for j in range(HT):
                if j >= 4:
                    ctx.dwait(v, pre + f"xnr{j % 4}",
                              ctx.d0.get(pre + f"xnr{j % 4}", 0) + j // 4)
                v.tensor_scalar(xnr[:, j % 4, :], xt[:, j, :], m_t[:, jo + j:jo + j + 1],
                                r_t[:, jo + j:jo + j + 1], ALU.subtract, ALU.mult)
        def p4_sca(s, jo=jo):
            v0 = ctx.c0["vector"]
            for j in range(HT):
                s.wait(ctx.ch["vector"], v0 + j + 1)
                s.dma(xn1t[:, jo + j, :], xnr[:, j % 4, :], key=pre + f"xnr{j % 4}")
        ctx.phase({"vector": p4_vec, "scalar": p4_sca})

    ctx.phase({"vector": lambda v: v.tensor_copy(wgb, wgt)})

    def p5_sync(s):
        for k in range(KD):
            s.dma(xnf[:, k, :], xn_out[:, k * 128:(k + 1) * 128], transpose=True)
    ctx.phase({"sync": p5_sync})

    def p6_ten(t):
        for j in range(NT):
            for k in range(KD):
                t.matmul(lg[:, j * E:(j + 1) * E],
                         xnf[:, k, j * 128:(j + 1) * 128],
                         wgb[:, k, :], start=(k == 0), stop=(k == KD - 1))
        t.mark()
    ctx.phase({"tensor": p6_ten})

    def p7_vec(v):
        v.tensor_copy(lgs, lg.rearrange("p (n e) -> p n e", e=E))
        for j in range(NT):
            v.max(mx8, lgs[:, j, :])
            v.max_index(ix8, mx8, lgs[:, j, :])
            v.tensor_copy(eix[:, j:j + 1], ix8[:, 0:1])
            v.tensor_scalar(nmx[:, j:j + 1], mx8[:, 0:1], -1.0, None, ALU.mult)
    ctx.phase({"vector": p7_vec})

    def p8_sca(s):
        for j in range(NT):
            s.activation(gex[:, j, :], lgs[:, j, :], AF.Exp, bias=nmx[:, j:j + 1])
    ctx.phase({"scalar": p8_sca})
    def p9_vec(v):
        for j in range(NT):
            v.reduce_sum(gsum[:, j:j + 1], gex[:, j, :], axis=AX)
        v.reciprocal(gt, gsum)
    ctx.phase({"vector": p9_vec})

    def p10_sync(s):
        s.dma(eidx_out, eix)
        s.dma(gate_out, gt)
    ctx.phase({"sync": p10_sync})


def emit_compact(nc, ctx, st, eidx_d, xn_d, xcomp, lc_out, pre):
    sb = lambda n, s_, d: st.enter_context(nc.sbuf_tensor(pre + n, s_, d)).ap()
    eoqu = sb("eoqu", [128, NT], U32)
    eoqf = sb("eoqf", [128, NT], F32)
    ohq = sb("ohq", [128, E, NT], F32)
    cnts = sb("cnts", [1, E, NT], F32)
    csc = sb("csc", [1, E, NT], F32)
    cex = sb("cex", [1, E, NT], F32)
    tote = sb("tote", [1, E], F32)
    lcoff = sb("lcoff", [1, E], F32)
    onesT = sb("onesT", [1, NT], F32)
    onesE = sb("onesE", [1, E], F32)
    lincl = sb("lincl", [128, 128], F32)
    ones1 = sb("ones1", [1, 128], F32)
    onc = sb("onc", [128, 1], F32)
    inclq = sb("inclq", [128, E, NT], F32)
    lflat = sb("lflat", [128, NT], F32)
    tsc = sb("tsc", [128, NT], F32)
    lfi = sb("lfi", [128, NT], I32)
    xnrl = sb("xnrl", [128, 2, D], BF16)
    psK = st.enter_context(nc.psum_tensor(pre + "psK", [128, 256], F32)).ap()
    xn_t = xn_d.rearrange("(n p) d -> p n d", p=128)

    def k0_sy(sy):
        sy.dma(eoqu, eidx_d)
    def k0_g(g):
        g.memset(lincl, 1.0)
        g.affine_select(lincl, lincl, [[1, 128]], ALU.is_ge, 0.0,
                        base=0, channel_multiplier=-1)
        g.memset(ones1, 1.0)
        g.memset(onc, 1.0)
        g.memset(onesT, 1.0)
        g.memset(onesE, 1.0)
    ctx.phase({"sync": k0_sy, "gpsimd": k0_g})

    def k1_v(v):
        v.tensor_copy(eoqf, eoqu)
        for e in range(E):
            v.tensor_scalar(ohq[:, e, :], eoqf, float(e), None, ALU.is_equal)
    ctx.phase({"vector": k1_v})

    def k2_t(t):
        for e in range(E):
            t.matmul(psK[0:1, e * NT:(e + 1) * NT], onc, ohq[:, e, :],
                     start=True, stop=True)
        t.mark()
    ctx.phase({"tensor": k2_t})
    def k2b_v(v):
        v.tensor_copy(cnts, psK[0:1, 0:E * NT].rearrange("p (e n) -> p e n", e=E))
        for e in range(E):
            v.tensor_tensor_scan(csc[:, e, :], onesT, cnts[:, e, :], 0.0, ALU.mult, ALU.add)
            v.tensor_copy(tote[:, e:e + 1], csc[:, e, NT - 1:NT])
        v.tensor_tensor(cex, csc, cnts, ALU.subtract)
        v.tensor_tensor_scan(lcoff, onesE, tote, 0.0, ALU.mult, ALU.add)
        v.tensor_tensor(lcoff, lcoff, tote, ALU.subtract)
        for e in range(E):
            v.tensor_scalar_add(cex[:, e, :], cex[:, e, :], lcoff[:, e:e + 1])
    ctx.phase({"vector": k2b_v})

    def k3_t(t):
        for e in range(E):
            t.matmul(psK[:, e * NT:(e + 1) * NT], lincl, ohq[:, e, :], start=True, stop=False)
            t.matmul(psK[:, e * NT:(e + 1) * NT], ones1, cex[:, e, :], start=False, stop=True)
        t.mark()
    ctx.phase({"tensor": k3_t})
    def k3b_v(v):
        v.tensor_copy(inclq, psK[:, 0:E * NT].rearrange("p (e n) -> p e n", e=E))
        v.memset(lflat, 0.0)
        for e in range(E):
            v.tensor_scalar(tsc, inclq[:, e, :], -1.0, None, ALU.add)
            v.tensor_tensor(tsc, tsc, ohq[:, e, :], ALU.mult)
            v.tensor_tensor(lflat, lflat, tsc, ALU.add)
        v.tensor_copy(lfi, lflat)
    ctx.phase({"vector": k3b_v})
    ctx.phase({"sync": lambda s: s.dma(lc_out, tote)})

    def k4_sy(s):
        for j in range(NT):
            if j >= 2:
                kk = pre + f"ks{j % 2}"
                ctx.dwait(s, kk, ctx.d0.get(kk, 0) + j // 2)
            s.dma(xnrl[:, j % 2, :], xn_t[:, j, :], key=pre + f"xr{j % 2}")
    def k4_g(g):
        for j in range(NT):
            key = pre + f"xr{j % 2}"
            ctx.dwait(g, key, ctx.d0.get(key, 0) + j // 2 + 1)
            g.idma(xcomp, bass.IndirectOffsetOnAxis(lfi[:, j:j + 1], 0),
                   xnrl[:, j % 2, :], None,
                   bounds_check=T - 1, oob_is_err=False, key=pre + f"ks{j % 2}")
    ctx.phase({"sync": k4_sy, "gpsimd": k4_g})


def build_l1():
    nc = mk_nc()
    x0 = nc.dram_tensor("x0", [T, D], F32, kind="ExternalInput").ap()
    wg = nc.dram_tensor("wg", [D, E], F32, kind="ExternalInput").ap()
    xn1 = nc.dram_tensor("xn1", [T, D], BF16, kind="ExternalOutput").ap()
    eidx = nc.dram_tensor("eidx", [128, NT], U32, kind="ExternalOutput").ap()
    gate = nc.dram_tensor("gate", [128, NT], F32, kind="ExternalOutput").ap()
    xcomp = nc.dram_tensor("xcomp", [T, D], BF16, kind="ExternalOutput").ap()
    lc = nc.dram_tensor("lc", [1, E], F32, kind="ExternalOutput").ap()
    with contextlib.ExitStack() as st:
        ctx = Ctx(nc, st)
        ctx.block = st.enter_context(nc.Block())
        emit_ln_router(nc, ctx, st, x0, xn1, eidx, gate, wg, "r_")
        emit_compact(nc, ctx, st, eidx, xn1, xcomp, lc, "k_")
    return nc


HC = HF // 128


def build_ffn():
    nc = mk_nc()
    xin = nc.dram_tensor("xin", [T, D], F32, kind="ExternalInput").ap()
    xn = nc.dram_tensor("xn", [T, D], BF16, kind="ExternalInput").ap()
    eidx_pi = nc.dram_tensor("eidx_pi", [128, 128], U32, kind="ExternalInput").ap()
    selv = nc.dram_tensor("selv", [128, 1], F32, kind="ExternalInput").ap()
    eoq = nc.dram_tensor("eoq", [128, NT], U32, kind="ExternalInput").ap()
    gateq = nc.dram_tensor("gateq", [128, NT], F32, kind="ExternalInput").ap()
    w1r = nc.dram_tensor("w1r", [E, HC, 128, KD, 128], F32, kind="ExternalInput").ap()
    w3r = nc.dram_tensor("w3r", [E, HC, 128, KD, 128], F32, kind="ExternalInput").ap()
    w2r = nc.dram_tensor("w2r", [E, HC, 128, D], F32, kind="ExternalInput").ap()
    sw1r = nc.dram_tensor("sw1r", [HC, 128, KD, 128], F32, kind="ExternalInput").ap()
    sw3r = nc.dram_tensor("sw3r", [HC, 128, KD, 128], F32, kind="ExternalInput").ap()
    sw2r = nc.dram_tensor("sw2r", [HC, 128, D], F32, kind="ExternalInput").ap()
    xmid = nc.dram_tensor("xmid", [T, D], F32, kind="ExternalOutput").ap()

    with contextlib.ExitStack() as st:
        ctx = Ctx(nc, st)
        sb = lambda n, s_, d: st.enter_context(nc.sbuf_tensor(n, s_, d)).ap()
        ps = lambda n, s_, d: st.enter_context(nc.psum_tensor(n, s_, d)).ap()
        wbig = sb("wbig", [128, HC, D], BF16)
        hbuf = sb("hbuf", [128, HC, T], BF16)
        scr = sb("scr", [128, 2 * D], F32)
        accs = scr.rearrange("p (a b) -> p a b", a=2)
        _scrb = scr.bitcast(BF16)
        wr1 = _scrb[:, 0:2 * KD * 128].rearrange("p (s k h) -> p s k h", s=2, k=KD)
        wr3 = _scrb[:, 2 * KD * 128:4 * KD * 128].rearrange("p (s k h) -> p s k h", s=2, k=KD)
        sA = sb("sA", [128, T], BF16)
        efu = sb("efu", [128, 128], U32)
        ef = sb("ef", [128, 128], F32)
        ohg = sb("ohg", [128, 128], F32)
        rcg = sb("rcg", [128, E], F32)
        ooff = sb("ooff", [1, E], F32)
        eoqu = sb("eoqu", [128, NT], U32)
        eoqf = sb("eoqf", [128, NT], F32)
        ohq = sb("ohq", [128, E, NT], F32)
        cnts = sb("cnts", [1, E, NT], F32)
        csc = sb("csc", [1, E, NT], F32)
        cex = sb("cex", [1, E, NT], F32)
        onesT = sb("onesT", [1, NT], F32)
        gq5 = sb("gq5", [128, NT], F32)
        lincl = sb("lincl", [128, 128], F32)
        ones1 = sb("ones1", [1, 128], F32)
        onc = sb("onc", [128, 1], F32)
        inclq = sb("inclq", [128, E, NT], F32)
        masks = sb("masks", [128, E, NT], F32)
        tsc = sb("tsc", [128, NT], F32)
        selv_sb = sb("selv_sb", [128, 1], F32)
        psA = ps("psA", [128, D], F32)
        psB = ps("psB", [128, D], F32)
        xnf = wbig.rearrange("p c d -> p (c d)")[:, 0:KD * T].rearrange("p (k t) -> p k t", k=KD)
        w2res = wbig

        ctx.block = st.enter_context(nc.Block())
        xmid_t = xmid.rearrange("(n p) d -> p n d", p=128)
        xin_t = xin.rearrange("(n p) d -> p n d", p=128)

        def p0_sync(s):
            s.dma(efu, eidx_pi)
            s.dma(eoqu, eoq)
            s.dma(gq5, gateq)
            s.dma(selv_sb, selv)
        def p0_g(g):
            g.memset(lincl, 1.0)
            g.affine_select(lincl, lincl, [[1, 128]], ALU.is_ge, 0.0, base=0, channel_multiplier=-1)
            g.memset(ones1, 1.0)
            g.memset(onc, 1.0)
            g.memset(onesT, 1.0)
        ctx.phase({"sync": p0_sync, "gpsimd": p0_g})

        def p1_vec(v):
            v.tensor_copy(ef, efu)
            v.tensor_copy(eoqf, eoqu)
            v.tensor_scalar(gq5, gq5, 0.5, None, ALU.mult)
            for e in range(E):
                v.tensor_scalar(ohg, ef, float(e), None, ALU.is_equal)
                v.reduce_sum(rcg[:, e:e + 1], ohg, axis=AX)
            for e in range(E):
                v.tensor_scalar(ohq[:, e, :], eoqf, float(e), None, ALU.is_equal)
        ctx.phase({"vector": p1_vec})

        def p2_ten(t):
            t.matmul(psB[0:1, 0:E], selv_sb, rcg, start=True, stop=True)
            for e in range(E):
                t.matmul(psB[0:1, E + e * NT:E + (e + 1) * NT], onc, ohq[:, e, :],
                         start=True, stop=True)
            t.mark()
        ctx.phase({"tensor": p2_ten})
        def p2b_vec(v):
            v.tensor_copy(ooff, psB[0:1, 0:E])
            v.tensor_copy(cnts, psB[0:1, E:E + E * NT].rearrange("p (e n) -> p e n", e=E))
            for e in range(E):
                v.tensor_tensor_scan(csc[:, e, :], onesT, cnts[:, e, :], 0.0, ALU.mult, ALU.add)
            v.tensor_tensor(cex, csc, cnts, ALU.subtract)
            for e in range(E):
                v.tensor_scalar_add(cex[:, e, :], cex[:, e, :], ooff[:, e:e + 1])
        ctx.phase({"vector": p2b_vec})

        def p3_ten(t):
            for e in range(E):
                t.matmul(psB[:, e * NT:(e + 1) * NT], lincl, ohq[:, e, :], start=True, stop=False)
                t.matmul(psB[:, e * NT:(e + 1) * NT], ones1, cex[:, e, :], start=False, stop=True)
            t.mark()
        ctx.phase({"tensor": p3_ten})
        def p3b_vec(v):
            v.tensor_copy(inclq, psB[:, 0:E * NT].rearrange("p (e n) -> p e n", e=E))
            for e in range(E):
                v.tensor_scalar(tsc, inclq[:, e, :], float(CAP), None, ALU.is_le)
                v.tensor_tensor(tsc, tsc, ohq[:, e, :], ALU.mult)
                v.tensor_tensor(masks[:, e, :], tsc, gq5, ALU.mult)
        ctx.phase({"vector": p3b_vec})

        for pi in range(E + 1):
            shared = (pi == 0)
            e = pi - 1

            def pa_sync(s):
                for k in range(KD):
                    s.dma(xnf[:, k, :], xn[:, k * 128:(k + 1) * 128], transpose=True)
            ctx.phase({"sync": pa_sync})

            def pb_g(g, shared=shared, e=e):
                t0 = ctx.c0["tensor"]
                for hc in range(HC):
                    if hc >= 2:
                        g.wait_ge(ctx.ch["tensor"], t0 + hc - 1)
                    sl = hc % 2
                    if shared:
                        g.dma(wr1[:, sl, :, :], sw1r[hc], key=f"wr{sl}")
                        g.dma(wr3[:, sl, :, :], sw3r[hc], key=f"wr{sl}")
                    else:
                        g.dma(wr1[:, sl, :, :], w1r[e, hc], key=f"wr{sl}")
                        g.dma(wr3[:, sl, :, :], w3r[e, hc], key=f"wr{sl}")
            def pb_t(t):
                v0 = ctx.c0["vector"]
                s0 = ctx.c0["scalar"]
                for hc in range(HC):
                    key = f"wr{hc % 2}"
                    ctx.dwait(t, key, ctx.d0.get(key, 0) + 2 * (hc // 2 + 1))
                    if hc >= 1:
                        t.wait(ctx.ch["scalar"], s0 + hc)
                        t.wait(ctx.ch["vector"], v0 + hc)
                    sl = hc % 2
                    for kc in range(KD):
                        for nb in range(4):
                            cs = slice(nb * 512, (nb + 1) * 512)
                            t.matmul(psA[:, cs], wr1[:, sl, kc, :], xnf[:, kc, cs],
                                     start=(kc == 0), stop=(kc == KD - 1))
                    for kc in range(KD):
                        for nb in range(4):
                            cs = slice(nb * 512, (nb + 1) * 512)
                            t.matmul(psB[:, cs], wr3[:, sl, kc, :], xnf[:, kc, cs],
                                     start=(kc == 0), stop=(kc == KD - 1))
                    t.mark()
            def pb_s(s):
                t0 = ctx.c0["tensor"]
                v0 = ctx.c0["vector"]
                for hc in range(HC):
                    s.wait(ctx.ch["tensor"], t0 + hc + 1)
                    if hc >= 1:
                        s.wait(ctx.ch["vector"], v0 + hc)
                    s.activation(sA, psA[:], AF.Silu)
            def pb_v(v):
                t0 = ctx.c0["tensor"]
                s0 = ctx.c0["scalar"]
                for hc in range(HC):
                    v.wait(ctx.ch["tensor"], t0 + hc + 1)
                    v.wait(ctx.ch["scalar"], s0 + hc + 1)
                    v.tensor_tensor(hbuf[:, hc, :], sA, psB[:], ALU.mult)
            ctx.phase({"gpsimd": pb_g, "tensor": pb_t, "scalar": pb_s, "vector": pb_v})

            def pc_g(g, shared=shared, e=e):
                for hc in range(HC):
                    g.dma(w2res[:, hc, :], (sw2r[hc] if shared else w2r[e, hc]))
            ctx.phase({"gpsimd": pc_g})

            def pd_sync(s, pi=pi):
                src = xin_t if pi == 0 else xmid_t
                for sc in range(NT):
                    if sc >= 2:
                        key = f"ast{sc % 2}"
                        ctx.dwait(s, key, ctx.d0.get(key, 0) + sc // 2)
                    s.dma(accs[:, sc % 2, :], src[:, sc, :], key=f"acl{sc % 2}")
            def pd_t(t):
                v0 = ctx.c0["vector"]
                for sc in range(NT):
                    if sc >= 1:
                        t.wait(ctx.ch["vector"], v0 + sc)
                    for hc in range(HC):
                        for nb in range(4):
                            cs = slice(nb * 512, (nb + 1) * 512)
                            t.matmul(psA[:, cs], hbuf[:, hc, sc * 128:(sc + 1) * 128],
                                     w2res[:, hc, cs], start=(hc == 0), stop=(hc == HC - 1))
                    t.mark()
            def pd_v(v, pi=pi, e=e):
                t0 = ctx.c0["tensor"]
                for sc in range(NT):
                    v.wait(ctx.ch["tensor"], t0 + sc + 1)
                    key = f"acl{sc % 2}"
                    ctx.dwait(v, key, ctx.d0.get(key, 0) + sc // 2 + 1)
                    if pi == 0:
                        v.scalar_tensor_tensor(accs[:, sc % 2, :], psA[:], 0.5,
                                               accs[:, sc % 2, :], ALU.mult, ALU.add)
                    else:
                        v.scalar_tensor_tensor(accs[:, sc % 2, :], psA[:], masks[:, e, sc:sc + 1],
                                               accs[:, sc % 2, :], ALU.mult, ALU.add)
            def pd_s(s):
                v0 = ctx.c0["vector"]
                for sc in range(NT):
                    s.wait(ctx.ch["vector"], v0 + sc + 1)
                    s.dma(xmid_t[:, sc, :], accs[:, sc % 2, :], key=f"ast{sc % 2}")
            ctx.phase({"sync": pd_sync, "tensor": pd_t, "vector": pd_v, "scalar": pd_s})
    return nc


def build_ffn_exp():
    nc = mk_nc()
    xcompall = nc.dram_tensor("xcompall", [NCORES * T, D], BF16, kind="ExternalInput").ap()
    xnown = nc.dram_tensor("xnown", [T, D], BF16, kind="ExternalInput").ap()
    lcT = nc.dram_tensor("lcT", [E, NCORES], F32, kind="ExternalInput").ap()
    esel = nc.dram_tensor("esel", [128, 1], F32, kind="ExternalInput").ap()
    w1e = nc.dram_tensor("w1e", [HC, 128, KD, 128], F32, kind="ExternalInput").ap()
    w3e = nc.dram_tensor("w3e", [HC, 128, KD, 128], F32, kind="ExternalInput").ap()
    w2e = nc.dram_tensor("w2e", [HC, 128, D], F32, kind="ExternalInput").ap()
    sw1r = nc.dram_tensor("sw1r", [HC, 128, KD, 128], F32, kind="ExternalInput").ap()
    sw3r = nc.dram_tensor("sw3r", [HC, 128, KD, 128], F32, kind="ExternalInput").ap()
    sw2r = nc.dram_tensor("sw2r", [HC, 128, D], F32, kind="ExternalInput").ap()
    oe_o = nc.dram_tensor("oe_o", [CAP, D], BF16, kind="ExternalOutput").ap()
    shro = nc.dram_tensor("shro", [T, D], BF16, kind="ExternalOutput").ap()
    xd = nc.dram_tensor("xd", [CAP, D], BF16, kind="Internal").ap()
    w2b = nc.dram_tensor("w2b", [HC, 128, D], BF16, kind="Internal").ap()
    sw2b = nc.dram_tensor("sw2b", [HC, 128, D], BF16, kind="Internal").ap()

    with contextlib.ExitStack() as st:
        ctx = Ctx(nc, st)
        sb = lambda n, s_, d: st.enter_context(nc.sbuf_tensor(n, s_, d)).ap()
        ps = lambda n, s_, d: st.enter_context(nc.psum_tensor(n, s_, d)).ap()
        ctx.block = st.enter_context(nc.Block())

        castsem = ctx.sem()
        def _cast_sec(g):
            g.dma_start(out=sw2b, in_=sw2r).then_inc(castsem, 16)
            g.dma_start(out=w2b, in_=w2e).then_inc(castsem, 16)
        ctx.block.gpsimd(_cast_sec)

        with contextlib.ExitStack() as st2:
            sb2 = lambda n, s_, d: st2.enter_context(nc.sbuf_tensor(n, s_, d)).ap()
            lcts = sb2("lcts", [E, NCORES], F32)
            esl = sb2("esl", [128, 1], F32)
            iop = sb2("iop", [E, 1], F32)
            ohc = sb2("ohc", [E, 1], F32)
            ltc = sb2("ltc", [E, 1], F32)
            lcrow = sb2("lcrow", [1, NCORES], F32)
            lofr = sb2("lofr", [1, NCORES], F32)
            gexr = sb2("gexr", [1, NCORES], F32)
            oner = sb2("oner", [1, NCORES], F32)
            ones1 = sb2("ones1", [1, 128], F32)
            LCB = sb2("LCB", [128, NCORES], F32)
            LOB = sb2("LOB", [128, NCORES], F32)
            GB = sb2("GB", [128, NCORES], F32)
            Sio = sb2("Sio", [128, NT], F32)
            uu = sb2("uu", [128, NT], F32)
            tt1 = sb2("tt1", [128, NT], F32)
            vt = sb2("vt", [128, NT], F32)
            idxf = sb2("idxf", [128, NT], F32)
            idxi = sb2("idxi", [128, NT], I32)
            xg = sb2("xg", [128, 2, D], BF16)
            psD = st2.enter_context(nc.psum_tensor("psD", [128, 64], F32)).ap()
            xd_t = xd.rearrange("(n p) d -> p n d", p=128)

            def d0_sy(sy):
                sy.dma(lcts, lcT)
                sy.dma(esl, esel)
            def d0_g(g):
                g.iota(iop, [[0, 1]], channel_multiplier=1,
                       allow_small_or_imprecise_dtypes=True)
                g.iota(Sio, [[128, NT]], channel_multiplier=1,
                       allow_small_or_imprecise_dtypes=True)
                g.memset(ones1, 1.0)
                g.memset(oner, 1.0)
            ctx.phase({"sync": d0_sy, "gpsimd": d0_g})

            def d1_v(v):
                v.tensor_scalar(ohc, iop, esl[0:E, :], None, ALU.is_equal)
                v.tensor_scalar(ltc, iop, esl[0:E, :], None, ALU.is_lt)
            ctx.phase({"vector": d1_v})
            def d2_t(t):
                t.matmul(psD[0:1, 0:NCORES], ohc, lcts, start=True, stop=True)
                t.matmul(psD[0:1, NCORES:2 * NCORES], ltc, lcts, start=True, stop=True)
                t.mark()
            ctx.phase({"tensor": d2_t})
            def d3_v(v):
                v.tensor_copy(lcrow, psD[0:1, 0:NCORES])
                v.tensor_copy(lofr, psD[0:1, NCORES:2 * NCORES])
                v.tensor_tensor_scan(gexr, oner, lcrow, 0.0, ALU.mult, ALU.add)
                v.tensor_tensor(gexr, gexr, lcrow, ALU.subtract)
            ctx.phase({"vector": d3_v})
            def d4_t(t):
                t.matmul(psD[:, 2 * NCORES:3 * NCORES], ones1, lcrow, start=True, stop=True)
                t.matmul(psD[:, 3 * NCORES:4 * NCORES], ones1, lofr, start=True, stop=True)
                t.matmul(psD[:, 4 * NCORES:5 * NCORES], ones1, gexr, start=True, stop=True)
                t.mark()
            ctx.phase({"tensor": d4_t})
            def d5_v(v):
                v.tensor_copy(LCB, psD[:, 2 * NCORES:3 * NCORES])
                v.tensor_copy(LOB, psD[:, 3 * NCORES:4 * NCORES])
                v.tensor_copy(GB, psD[:, 4 * NCORES:5 * NCORES])
                v.memset(idxf, 0.0)
                for k in range(NCORES):
                    v.tensor_scalar(uu, Sio, GB[:, k:k + 1], None, ALU.subtract)
                    v.tensor_scalar(tt1, uu, 0.0, None, ALU.is_ge)
                    v.tensor_scalar(vt, uu, LCB[:, k:k + 1], None, ALU.is_lt)
                    v.tensor_tensor(tt1, tt1, vt, ALU.mult)
                    v.tensor_scalar(vt, uu, LOB[:, k:k + 1], None, ALU.add)
                    v.tensor_scalar(vt, vt, float(k * T), None, ALU.add)
                    v.tensor_tensor(vt, vt, tt1, ALU.mult)
                    v.tensor_tensor(idxf, idxf, vt, ALU.add)
                v.tensor_copy(idxi, idxf)
            ctx.phase({"vector": d5_v})

            def d6_g(g):
                for n in range(NT):
                    if n >= 2:
                        key = f"xds{n % 2}"
                        ctx.dwait(g, key, ctx.d0.get(key, 0) + n // 2)
                    g.idma(xg[:, n % 2, :], None, xcompall,
                           bass.IndirectOffsetOnAxis(idxi[:, n:n + 1], 0),
                           bounds_check=NCORES * T - 1, oob_is_err=False,
                           key=f"xgl{n % 2}")
            def d6_sy(s):
                for n in range(NT):
                    key = f"xgl{n % 2}"
                    ctx.dwait(s, key, ctx.d0.get(key, 0) + n // 2 + 1)
                    s.dma(xd_t[:, n, :], xg[:, n % 2, :], key=f"xds{n % 2}")
            ctx.phase({"gpsimd": d6_g, "sync": d6_sy})

        wbig = sb("wbig", [128, HC, D], BF16)
        hbuf = sb("hbuf", [128, HC, T], BF16)
        scr = sb("scr", [128, 2 * D], F32)
        _scrb = scr.bitcast(BF16)
        wr1 = _scrb[:, 0:2 * KD * 128].rearrange("p (s k h) -> p s k h", s=2, k=KD)
        wr3 = _scrb[:, 2 * KD * 128:4 * KD * 128].rearrange("p (s k h) -> p s k h", s=2, k=KD)
        oer = _scrb[:, 0:2 * D].rearrange("p (s d) -> p s d", s=2)
        sA = sb("sA", [128, T], BF16)
        psA = ps("psA", [128, D], F32)
        psB = ps("psB", [128, D], F32)
        xnf = wbig.rearrange("p c d -> p (c d)")[:, 0:KD * T].rearrange("p (k t) -> p k t", k=KD)
        w2res = wbig

        for pi in range(2):
            src = xnown if pi == 0 else xd
            wa, wb = (sw1r, sw3r) if pi == 0 else (w1e, w3e)
            wc = sw2b if pi == 0 else w2b
            dst = shro if pi == 0 else oe_o
            dst_t = dst.rearrange("(n p) d -> p n d", p=128)

            def pa_sync(s, src=src):
                for k in range(KD):
                    s.dma(xnf[:, k, :], src[:, k * 128:(k + 1) * 128], transpose=True)
            ctx.phase({"sync": pa_sync})

            def pb_g(g, wa=wa, wb=wb):
                t0 = ctx.c0["tensor"]
                for hc in range(HC):
                    if hc >= 2:
                        g.wait_ge(ctx.ch["tensor"], t0 + hc - 1)
                    sl = hc % 2
                    g.dma(wr1[:, sl, :, :], wa[hc], key=f"wr{sl}")
                    g.dma(wr3[:, sl, :, :], wb[hc], key=f"wr{sl}")
            def pb_t(t):
                v0 = ctx.c0["vector"]
                s0 = ctx.c0["scalar"]
                for hc in range(HC):
                    key = f"wr{hc % 2}"
                    ctx.dwait(t, key, ctx.d0.get(key, 0) + 2 * (hc // 2 + 1))
                    if hc >= 1:
                        t.wait(ctx.ch["scalar"], s0 + hc)
                        t.wait(ctx.ch["vector"], v0 + hc)
                    sl = hc % 2
                    for kc in range(KD):
                        for nb in range(4):
                            cs = slice(nb * 512, (nb + 1) * 512)
                            t.matmul(psA[:, cs], wr1[:, sl, kc, :], xnf[:, kc, cs],
                                     start=(kc == 0), stop=(kc == KD - 1))
                    for kc in range(KD):
                        for nb in range(4):
                            cs = slice(nb * 512, (nb + 1) * 512)
                            t.matmul(psB[:, cs], wr3[:, sl, kc, :], xnf[:, kc, cs],
                                     start=(kc == 0), stop=(kc == KD - 1))
                    t.mark()
            def pb_s(s):
                t0 = ctx.c0["tensor"]
                v0 = ctx.c0["vector"]
                for hc in range(HC):
                    s.wait(ctx.ch["tensor"], t0 + hc + 1)
                    if hc >= 1:
                        s.wait(ctx.ch["vector"], v0 + hc)
                    s.activation(sA, psA[:], AF.Silu)
            def pb_v(v):
                t0 = ctx.c0["tensor"]
                s0 = ctx.c0["scalar"]
                for hc in range(HC):
                    v.wait(ctx.ch["tensor"], t0 + hc + 1)
                    v.wait(ctx.ch["scalar"], s0 + hc + 1)
                    v.tensor_tensor(hbuf[:, hc, :], sA, psB[:], ALU.mult)
            ctx.phase({"gpsimd": pb_g, "tensor": pb_t, "scalar": pb_s, "vector": pb_v})

            def pc_g(g, wc=wc, pi=pi):
                if pi == 0:
                    g.wait_ge(castsem, 32)
                for hc in range(HC):
                    g.dma(w2res[:, hc, :], wc[hc])
            ctx.phase({"gpsimd": pc_g})

            def pd_t(t):
                v0 = ctx.c0["vector"]
                for sc in range(NT):
                    if sc >= 1:
                        t.wait(ctx.ch["vector"], v0 + sc)
                    for hc in range(HC):
                        for nb in range(4):
                            cs = slice(nb * 512, (nb + 1) * 512)
                            t.matmul(psA[:, cs], hbuf[:, hc, sc * 128:(sc + 1) * 128],
                                     w2res[:, hc, cs], start=(hc == 0), stop=(hc == HC - 1))
                    t.mark()
            def pd_v(v):
                t0 = ctx.c0["tensor"]
                for sc in range(NT):
                    v.wait(ctx.ch["tensor"], t0 + sc + 1)
                    if sc >= 2:
                        key = f"oes{sc % 2}"
                        ctx.dwait(v, key, ctx.d0.get(key, 0) + sc // 2)
                    v.tensor_copy(oer[:, sc % 2, :], psA[:])
            def pd_s(s, dst_t=dst_t):
                v0 = ctx.c0["vector"]
                for sc in range(NT):
                    s.wait(ctx.ch["vector"], v0 + sc + 1)
                    s.dma(dst_t[:, sc, :], oer[:, sc % 2, :], key=f"oes{sc % 2}")
            ctx.phase({"tensor": pd_t, "vector": pd_v, "scalar": pd_s})
    return nc



def emit_combine(nc, ctx, st, eidx_pi, selvlb, eoq17, gq17, oeall,
                 xb_blocks, sh_blocks, out_blocks, pre):
    sb = lambda n, s_, d: st.enter_context(nc.sbuf_tensor(pre + n, s_, d)).ap()
    NB = len(xb_blocks)
    efu = sb("efu", [128, 128], U32)
    ef = sb("ef", [128, 128], F32)
    ohg = sb("ohg", [128, 128], F32)
    rcg = sb("rcg", [128, E], F32)
    ooff = sb("ooff", [1, E], F32)
    eoqu = sb("eoqu", [128, NB], U32)
    eoqf = sb("eoqf", [128, NB], F32)
    gq = sb("gq", [128, NB], F32)
    selv = sb("selv", [128, 1], F32)
    ohq = sb("ohq", [128, E, NB], F32)
    cnts = sb("cnts", [1, E, NB], F32)
    csc = sb("csc", [1, E, NB], F32)
    cex = sb("cex", [1, E, NB], F32)
    onesT = sb("onesT", [1, NB], F32)
    lincl = sb("lincl", [128, 128], F32)
    ones1 = sb("ones1", [1, 128], F32)
    onc = sb("onc", [128, 1], F32)
    inclq = sb("inclq", [128, E, NB], F32)
    flatq = sb("flatq", [128, NB], F32)
    keepq = sb("keepq", [128, NB], F32)
    gk = sb("gk", [128, NB], F32)
    idxq = sb("idxq", [128, NB], I32)
    tsc = sb("tsc", [128, NB], F32)
    oeg = sb("oeg", [128, 2, D], BF16)
    xbt = sb("xbt", [128, 2, D], F32)
    sht = sb("sht", [128, 2, D], BF16)
    otr = sb("otr", [128, 2, D], F32)
    psC = st.enter_context(nc.psum_tensor(pre + "psC", [128, 256], F32)).ap()

    def c0_sy(sy):
        sy.dma(efu, eidx_pi)
        sy.dma(eoqu, eoq17)
        sy.dma(gq, gq17)
        sy.dma(selv, selvlb)
    def c0_g(g):
        g.memset(lincl, 1.0)
        g.affine_select(lincl, lincl, [[1, 128]], ALU.is_ge, 0.0,
                        base=0, channel_multiplier=-1)
        g.memset(ones1, 1.0)
        g.memset(onc, 1.0)
        g.memset(onesT, 1.0)
    ctx.phase({"sync": c0_sy, "gpsimd": c0_g})

    def c1_v(v):
        v.tensor_copy(ef, efu)
        v.tensor_copy(eoqf, eoqu)
        v.tensor_scalar(gq, gq, 0.5, None, ALU.mult)
        for e in range(E):
            v.tensor_scalar(ohg, ef, float(e), None, ALU.is_equal)
            v.reduce_sum(rcg[:, e:e + 1], ohg, axis=AX)
        for e in range(E):
            v.tensor_scalar(ohq[:, e, :], eoqf, float(e), None, ALU.is_equal)
    ctx.phase({"vector": c1_v})

    def c2_t(t):
        t.matmul(psC[0:1, 0:E], selv, rcg, start=True, stop=True)
        for e in range(E):
            t.matmul(psC[0:1, E + e * NB:E + (e + 1) * NB], onc, ohq[:, e, :],
                     start=True, stop=True)
        t.mark()
    ctx.phase({"tensor": c2_t})
    def c2b_v(v):
        v.tensor_copy(ooff, psC[0:1, 0:E])
        v.tensor_copy(cnts, psC[0:1, E:E + E * NB].rearrange("p (e n) -> p e n", e=E))
        for e in range(E):
            v.tensor_tensor_scan(csc[:, e, :], onesT, cnts[:, e, :], 0.0, ALU.mult, ALU.add)
        v.tensor_tensor(cex, csc, cnts, ALU.subtract)
        for e in range(E):
            v.tensor_scalar_add(cex[:, e, :], cex[:, e, :], ooff[:, e:e + 1])
    ctx.phase({"vector": c2b_v})

    def c3_t(t):
        for e in range(E):
            t.matmul(psC[:, e * NB:(e + 1) * NB], lincl, ohq[:, e, :], start=True, stop=False)
            t.matmul(psC[:, e * NB:(e + 1) * NB], ones1, cex[:, e, :], start=False, stop=True)
        t.mark()
    ctx.phase({"tensor": c3_t})
    def c3b_v(v):
        v.tensor_copy(inclq, psC[:, 0:E * NB].rearrange("p (e n) -> p e n", e=E))
        v.memset(flatq, 0.0)
        v.memset(keepq, 0.0)
        for e in range(E):
            v.tensor_scalar(tsc, inclq[:, e, :], float(CAP), None, ALU.is_le)
            v.tensor_tensor(tsc, tsc, ohq[:, e, :], ALU.mult)
            v.tensor_tensor(keepq, keepq, tsc, ALU.add)
            v.tensor_scalar(tsc, inclq[:, e, :], float(e * CAP) - 1.0, None, ALU.add)
            v.tensor_tensor(tsc, tsc, ohq[:, e, :], ALU.mult)
            v.tensor_tensor(flatq, flatq, tsc, ALU.add)
        v.tensor_scalar_min(flatq, flatq, float(E * CAP - 1))
        v.tensor_tensor(gk, gq, keepq, ALU.mult)
        v.tensor_copy(idxq, flatq)
    ctx.phase({"vector": c3b_v})

    def cl_g(g):
        v0 = ctx.c0["vector"]
        for j in range(NB):
            sl = j % 2
            if j >= 2:
                g.wait_ge(ctx.ch["vector"], v0 + 2 * (j - 1))
            g.idma(oeg[:, sl, :], None, oeall,
                   bass.IndirectOffsetOnAxis(idxq[:, j:j + 1], 0),
                   key=pre + f"oeg{sl}")
    def cl_sy(sy):
        v0 = ctx.c0["vector"]
        for j in range(NB):
            sl = j % 2
            if j >= 2:
                sy.wait(ctx.ch["vector"], v0 + 2 * (j - 1))
            sy.dma(xbt[:, sl, :], xb_blocks[j], key=pre + f"xb{sl}")
            sy.dma(sht[:, sl, :], sh_blocks[j], key=pre + f"sh{sl}")
    def cl_v(v):
        for j in range(NB):
            sl = j % 2
            n = j // 2 + 1
            ctx.dwait(v, pre + f"oeg{sl}", ctx.d0.get(pre + f"oeg{sl}", 0) + n)
            ctx.dwait(v, pre + f"xb{sl}", ctx.d0.get(pre + f"xb{sl}", 0) + n)
            ctx.dwait(v, pre + f"sh{sl}", ctx.d0.get(pre + f"sh{sl}", 0) + n)
            if j >= 2:
                ctx.dwait(v, pre + f"ot{sl}", ctx.d0.get(pre + f"ot{sl}", 0) + j // 2)
            v.scalar_tensor_tensor(otr[:, sl, :], sht[:, sl, :], 0.5,
                                   xbt[:, sl, :], ALU.mult, ALU.add)
            v.scalar_tensor_tensor(otr[:, sl, :], oeg[:, sl, :], gk[:, j:j + 1],
                                   otr[:, sl, :], ALU.mult, ALU.add)
    def cl_s(s_):
        v0 = ctx.c0["vector"]
        for j in range(NB):
            sl = j % 2
            s_.wait(ctx.ch["vector"], v0 + 2 * (j + 1))
            s_.dma(out_blocks[j], otr[:, sl, :], key=pre + f"ot{sl}")
    ctx.phase({"gpsimd": cl_g, "sync": cl_sy, "vector": cl_v, "scalar": cl_s})


def build_final():
    nc = mk_nc()
    x2 = nc.dram_tensor("x2", [T, D], F32, kind="ExternalInput").ap()
    shro = nc.dram_tensor("shro", [T, D], BF16, kind="ExternalInput").ap()
    oeall = nc.dram_tensor("oeall", [E * CAP, D], BF16, kind="ExternalInput").ap()
    eidx_pi = nc.dram_tensor("eidx_pi", [128, 128], U32, kind="ExternalInput").ap()
    selvlb = nc.dram_tensor("selvlb", [128, 1], F32, kind="ExternalInput").ap()
    eoq = nc.dram_tensor("eoq", [128, NT], U32, kind="ExternalInput").ap()
    gateq = nc.dram_tensor("gateq", [128, NT], F32, kind="ExternalInput").ap()
    out = nc.dram_tensor("out", [T, D], F32, kind="ExternalOutput").ap()
    with contextlib.ExitStack() as st:
        ctx = Ctx(nc, st)
        ctx.block = st.enter_context(nc.Block())
        x2t = x2.rearrange("(n p) d -> p n d", p=128)
        sht = shro.rearrange("(n p) d -> p n d", p=128)
        outt = out.rearrange("(n p) d -> p n d", p=128)
        emit_combine(nc, ctx, st, eidx_pi, selvlb, eoq, gateq, oeall,
                     [x2t[:, j, :] for j in range(NT)],
                     [sht[:, j, :] for j in range(NT)],
                     [outt[:, j, :] for j in range(NT)], "cb_")
    return nc


def ffn_exp_in_maps(xn_sl, xcomp_sl, lc_sl, wf):
    xcompall = np.concatenate(xcomp_sl, axis=0)
    lcT = np.ascontiguousarray(np.concatenate(lc_sl, axis=0).T)
    maps = []
    for c in range(NCORES):
        maps.append(dict(
            xcompall=xcompall, xnown=xn_sl[c], lcT=lcT,
            esel=np.full((128, 1), float(c), np.float32),
            w1e=wf["w1r"][c], w3e=wf["w3r"][c], w2e=wf["w2r"][c],
            sw1r=wf["sw1r"], sw3r=wf["sw3r"], sw2r=wf["sw2r"]))
    return maps


def final_in_maps(x2_sl, oe_sl, shro_sl, eidx_c, gate_c):
    oeall = np.concatenate(oe_sl, axis=0)
    eidx_pi = np.ascontiguousarray(
        np.concatenate([e.T for e in eidx_c], axis=0)).astype(np.uint32)
    maps = []
    for c in range(NCORES):
        maps.append(dict(
            x2=x2_sl[c], shro=shro_sl[c], oeall=oeall, eidx_pi=eidx_pi,
            selvlb=(np.arange(128) < c * NT).astype(np.float32).reshape(128, 1),
            eoq=eidx_c[c], gateq=gate_c[c]))
    return maps


TPAD = 2176
NTP = TPAD // 128
CCW = 67
CCI = CONV_DIM // 128
NB5 = [(0, 512), (512, 1024), (1024, 1536), (1536, 2048), (2048, 2176)]


def build_l3():
    nc = mk_nc()
    xb = nc.dram_tensor("xb", [T, D], F32, kind="ExternalInput").ap()
    xblb = nc.dram_tensor("xblb", [128, D], F32, kind="ExternalInput").ap()
    shro = nc.dram_tensor("shro", [T, D], BF16, kind="ExternalInput").ap()
    shlb = nc.dram_tensor("shlb", [128, D], BF16, kind="ExternalInput").ap()
    oeall = nc.dram_tensor("oeall", [NCORES * T, D], BF16, kind="ExternalInput").ap()
    eidx_pi = nc.dram_tensor("eidx_pi", [128, 128], U32, kind="ExternalInput").ap()
    selvlb = nc.dram_tensor("selvlb", [128, 1], F32, kind="ExternalInput").ap()
    eoq17 = nc.dram_tensor("eoq17", [128, NTP], U32, kind="ExternalInput").ap()
    gq17 = nc.dram_tensor("gq17", [128, NTP], F32, kind="ExternalInput").ap()
    winr = nc.dram_tensor("winr", [CCW, 128, KD, 128], F32, kind="ExternalInput").ap()
    convwr = nc.dram_tensor("convwr", [128, CCI, 4], F32, kind="ExternalInput").ap()
    convbr = nc.dram_tensor("convbr", [128, CCI], F32, kind="ExternalInput").ap()
    dtbr = nc.dram_tensor("dtbr", [MH, 1], F32, kind="ExternalInput").ap()
    alogr = nc.dram_tensor("alogr", [MH, 1], F32, kind="ExternalInput").ap()
    z_o = nc.dram_tensor("z_o", [INTER, T], BF16, kind="ExternalOutput").ap()
    xfm_o = nc.dram_tensor("xfm_o", [INTER, T], BF16, kind="ExternalOutput").ap()
    bfm_o = nc.dram_tensor("bfm_o", [MN, T], BF16, kind="ExternalOutput").ap()
    cfm_o = nc.dram_tensor("cfm_o", [MN, T], BF16, kind="ExternalOutput").ap()
    dtf_o = nc.dram_tensor("dtf_o", [MH, T], F32, kind="ExternalOutput").ap()
    acsf_o = nc.dram_tensor("acsf_o", [MH, T], F32, kind="ExternalOutput").ap()
    nacsf_o = nc.dram_tensor("nacsf_o", [MH, T], F32, kind="ExternalOutput").ap()
    ce_o = nc.dram_tensor("ce_o", [MH, NCH], F32, kind="ExternalOutput").ap()
    stn_o = nc.dram_tensor("stn_o", [NCH, MN, INTER], BF16, kind="ExternalOutput").ap()
    xmid_o = nc.dram_tensor("xmid_o", [T, D], F32, kind="ExternalOutput").ap()
    xn2 = nc.dram_tensor("xn2", [TPAD, D], BF16, kind="Internal").ap()
    ddec = nc.dram_tensor("ddec", [MH, T], BF16, kind="Internal").ap()
    xlb_d = nc.dram_tensor("xlb_d", [128, D], F32, kind="Internal").ap()

    with contextlib.ExitStack() as st:
        ctx = Ctx(nc, st)
        sb = lambda n, s_, d: st.enter_context(nc.sbuf_tensor(n, s_, d)).ap()
        ctx.block = st.enter_context(nc.Block())

        with contextlib.ExitStack() as stC:
            xbt_ = xb.rearrange("(n p) d -> p n d", p=128)
            sht_ = shro.rearrange("(n p) d -> p n d", p=128)
            xmt_ = xmid_o.rearrange("(n p) d -> p n d", p=128)
            emit_combine(nc, ctx, stC, eidx_pi, selvlb, eoq17, gq17, oeall,
                         [xblb] + [xbt_[:, j, :] for j in range(NT)],
                         [shlb] + [sht_[:, j, :] for j in range(NT)],
                         [xlb_d] + [xmt_[:, j, :] for j in range(NT)], "cb_")

        cw = sb("cw", [128, CCI, 4], F32)
        cb = sb("cb", [128, CCI], F32)
        dtb_t = sb("dtb_t", [MH, 1], F32)
        nega = sb("nega", [MH, 1], F32)
        dtft = sb("dtft", [MH, T], F32)
        acst = sb("acst", [MH, T], F32)
        tmp64 = sb("tmp64", [MH, T], F32)
        cet = sb("cet", [MH, NCH], F32)
        onc64 = sb("onc64", [MH, 1], F32)
        ones_b = sb("ones_b", [128, 128], BF16)
        xn2t = xn2.rearrange("(n p) d -> p n d", p=128)

        def p0(sy):
            sy.dma(cw, convwr)
            sy.dma(cb, convbr)
            sy.dma(dtb_t, dtbr)
            sy.dma(nega, alogr)
        def p0g(g):
            g.memset(onc64, 1.0)
            g.memset(ones_b, 1.0)
        ctx.phase({"sync": p0, "gpsimd": p0g})
        ctx.phase({"scalar": lambda s_: s_.activation(nega, nega, AF.Exp)})
        ctx.phase({"vector": lambda v: v.tensor_scalar(nega, nega, -1.0, None, ALU.mult)})

        with nc.sbuf_tensor("lnx", [128, NTP, D], F32) as _a, \
             nc.sbuf_tensor("lsq", [128, D], BF16) as _b, \
             nc.sbuf_tensor("lms", [128, NTP], F32) as _c, \
             nc.sbuf_tensor("lvr", [128, NTP], F32) as _d, \
             nc.sbuf_tensor("lm", [128, NTP], F32) as _e, \
             nc.sbuf_tensor("llv", [128, NTP], F32) as _f, \
             nc.sbuf_tensor("lr", [128, NTP], F32) as _g, \
             nc.sbuf_tensor("lnr", [128, 4, D], BF16) as _h, \
             nc.sbuf_tensor("lneps", [128, 1], F32) as _i:
            lnx, lsq, lms, lvr, lm, llv, lr, lnr, lneps = (
                _a.ap(), _b.ap(), _c.ap(), _d.ap(), _e.ap(), _f.ap(), _g.ap(), _h.ap(), _i.ap())

            def q0_g(g):
                g.memset(lnx[:, NTP - 1, :], 0.0)
                g.memset(lneps, float(EPS))
            def q0_sy(sy):
                sy.dma(lnx[0:3, 0, :], xlb_d[125:128, :])
                sy.dma(lnx[3:128, 0, :], xmid_o[0:125])
                sy.dma(lnx[:, 1:NTP - 1, :],
                       xmid_o[125:2045].rearrange("(n p) d -> p n d", p=128))
            ctx.phase({"gpsimd": q0_g, "sync": q0_sy})
            ctx.phase({"sync": lambda sy: sy.dma(lnx[0:3, NTP - 1, :], xmid_o[2045:2048])})

            def q1_v(v):
                for j in range(NTP):
                    v.reduce_sum(lms[:, j:j + 1], lnx[:, j, :], axis=AX)
            def q1_s(s_):
                for j in range(NTP):
                    s_.activation(lsq, lnx[:, j, :], AF.Square, accum_out=lvr[:, j:j + 1])
            ctx.phase({"vector": q1_v, "scalar": q1_s})

            def q2_v(v):
                v.tensor_scalar(lm, lms, 1.0 / D, None, ALU.mult)
                v.tensor_scalar(lvr, lvr, 1.0 / D, None, ALU.mult)
                v.tensor_tensor(lms, lm, lm, ALU.mult)
                v.tensor_tensor(lvr, lvr, lms, ALU.subtract)
            ctx.phase({"vector": q2_v})
            def q3_s(s_):
                s_.activation(llv, lvr, AF.Ln, bias=lneps)
                s_.activation(lr, llv, AF.Exp, scale=-0.5)
            ctx.phase({"scalar": q3_s})

            def q4_v(v):
                for j in range(NTP):
                    if j >= 4:
                        ctx.dwait(v, f"l3nr{j % 4}", j // 4)
                    v.tensor_scalar(lnr[:, j % 4, :], lnx[:, j, :], lm[:, j:j + 1],
                                    lr[:, j:j + 1], ALU.subtract, ALU.mult)
            def q4_s(s_):
                v0 = ctx.c0["vector"]
                for j in range(NTP):
                    s_.wait(ctx.ch["vector"], v0 + j + 1)
                    s_.dma(xn2t[:, j, :], lnr[:, j % 4, :], key=f"l3nr{j % 4}")
            ctx.phase({"vector": q4_v, "scalar": q4_s})

        with nc.sbuf_tensor("xn2f", [128, KD, TPAD], BF16) as _a, \
             nc.sbuf_tensor("wrin", [128, 2, KD, 128], BF16) as _b, \
             nc.sbuf_tensor("evb", [128, 2, TPAD], BF16) as _c, \
             nc.sbuf_tensor("xbcp", [128, TPAD], BF16) as _d, \
             nc.sbuf_tensor("cdg", [128, 4, 128], BF16) as _e, \
             nc.psum_tensor("psW", [128, TPAD], F32) as _p:
            xn2f, wrin, evb, xbcp, cdg, psW = (
                _a.ap(), _b.ap(), _c.ap(), _d.ap(), _e.ap(), _p.ap())
            psC = psW[:, 0:2048]

            def r0_sy(sy):
                for k in range(KD):
                    sy.dma(xn2f[:, k, :], xn2[:, k * 128:(k + 1) * 128], transpose=True)
            ctx.phase({"sync": r0_sy})

            def win_mms(t, sl):
                for kc in range(KD):
                    for (a, b) in NB5:
                        t.matmul(psW[:, a:b], wrin[:, sl, kc, :], xn2f[:, kc, a:b],
                                 start=(kc == 0), stop=(kc == KD - 1))
                t.mark()

            ctx.phase({"gpsimd": lambda g: g.dma(wrin[:, 0, :, :], winr[66])})
            ctx.phase({"tensor": lambda t: win_mms(t, 0)})
            dtraw = psW[0:64, 3:2051]
            ctx.phase({"vector": lambda v: v.tensor_scalar_add(tmp64, dtraw, dtb_t)})
            with nc.sbuf_tensor("spa", [MH, T], F32) as _sa, \
                 nc.sbuf_tensor("spl", [MH, T], F32) as _sl2, \
                 nc.sbuf_tensor("dect", [MH, T], F32) as _dc, \
                 nc.sbuf_tensor("ddecb", [MH, T], BF16) as _db, \
                 nc.sbuf_tensor("onec", [MH, 1], F32) as _oc:
                spa, spl, dect, ddecb, onec = _sa.ap(), _sl2.ap(), _dc.ap(), _db.ap(), _oc.ap()
                ctx.phase({"gpsimd": lambda g: g.memset(onec, 1.0)})
                def s1_s(s_):
                    s_.activation(spa, tmp64, AF.Abs)
                    s_.activation(spa, spa, AF.Exp, scale=-1.0)
                    s_.activation(spl, spa, AF.Ln, bias=onec)
                ctx.phase({"scalar": s1_s})
                def s2_v(v):
                    v.tensor_scalar_max(dtft, tmp64, 0.0)
                    v.tensor_tensor(dtft, dtft, spl, ALU.add)
                    v.tensor_scalar_min(dtft, dtft, 100.0)
                    v.tensor_scalar(tmp64, dtft, nega, None, ALU.mult)
                    for c in range(NCH):
                        cs = slice(c * CH, (c + 1) * CH)
                        v.tensor_tensor_scan(acst[:, cs],
                                             onc64.broadcast_to((MH, 1, CH))[:, 0, :],
                                             tmp64[:, cs], 0.0, ALU.mult, ALU.add)
                    v.tensor_copy(cet, acst.rearrange("h (c l) -> h c l", l=CH)[:, :, CH - 1])
                ctx.phase({"vector": s2_v})
                def s3_s(s_):
                    for c in range(NCH):
                        cs = slice(c * CH, (c + 1) * CH)
                        s_.activation(dect[:, cs], acst[:, cs], AF.Exp, scale=-1.0,
                                      bias=cet[:, c:c + 1])
                ctx.phase({"scalar": s3_s})
                def s4_v(v):
                    v.tensor_tensor(dect, dtft, dect, ALU.mult)
                    v.tensor_copy(ddecb, dect)
                    v.tensor_scalar(tmp64, acst, -1.0, None, ALU.mult)
                ctx.phase({"vector": s4_v})
                def s5_sy(sy):
                    sy.dma(dtf_o, dtft)
                    sy.dma(acsf_o, acst)
                    sy.dma(nacsf_o, tmp64)
                    sy.dma(ce_o, cet)
                    sy.dma(ddec, ddecb)
                ctx.phase({"sync": s5_sy})

            with nc.sbuf_tensor("cvt", [128, 2, 2048], F32) as _cv:
                cvt = _cv.ap()

                def ci_g(g):
                    t0 = ctx.c0["tensor"]
                    for ci in range(CCI):
                        sl = ci % 2
                        if ci >= 2:
                            g.wait_ge(ctx.ch["tensor"], t0 + ci - 1)
                        g.dma(wrin[:, sl, :, :], winr[32 + ci], key=f"wri{sl}")
                def ci_t(t):
                    v0 = ctx.c0["vector"]
                    for ci in range(CCI):
                        sl = ci % 2
                        key = f"wri{sl}"
                        ctx.dwait(t, key, ctx.d0.get(key, 0) + ci // 2 + 1)
                        if ci >= 1:
                            t.wait(ctx.ch["vector"], v0 + 4 * ci)
                        win_mms(t, sl)
                def ci_v(v):
                    t0 = ctx.c0["tensor"]
                    s0 = ctx.c0["scalar"]
                    for ci in range(CCI):
                        sl = ci % 2
                        v.wait(ctx.ch["tensor"], t0 + ci + 1)
                        if ci >= 2:
                            v.wait(ctx.ch["scalar"], s0 + ci - 1)
                        v.tensor_scalar(cvt[:, sl, :], psW[:, 0:2048],
                                        cw[:, ci, 0:1], None, ALU.mult)
                        for k in range(1, 4):
                            v.scalar_tensor_tensor(cvt[:, sl, :], psW[:, k:k + 2048],
                                                   cw[:, ci, k:k + 1],
                                                   cvt[:, sl, :], ALU.mult, ALU.add)
                def ci_s(s_):
                    v0 = ctx.c0["vector"]
                    for ci in range(CCI):
                        sl = ci % 2
                        s_.wait(ctx.ch["vector"], v0 + 4 * ci + 4)
                        if ci >= 2:
                            key = f"evs{sl}"
                            ctx.dwait(s_, key, ctx.d0.get(key, 0) + ci // 2)
                        s_.activation(evb[:, sl, 0:2048], cvt[:, sl, :], AF.Silu,
                                      bias=cb[:, ci:ci + 1])
                def ci_sy(sy):
                    s0 = ctx.c0["scalar"]
                    for ci in range(CCI):
                        sl = ci % 2
                        sy.wait_ge(ctx.ch["scalar"], s0 + ci + 1)
                        if ci < 32:
                            sy.dma(xfm_o[ci * 128:(ci + 1) * 128, :],
                                   evb[:, sl, 0:2048], key=f"evs{sl}")
                        elif ci == 32:
                            sy.dma(bfm_o, evb[:, sl, 0:2048], key=f"evs{sl}")
                        else:
                            sy.dma(cfm_o, evb[:, sl, 0:2048], key=f"evs{sl}")
                ctx.phase({"gpsimd": ci_g, "tensor": ci_t, "vector": ci_v,
                           "scalar": ci_s, "sync": ci_sy})

            def cz_g(g):
                t0 = ctx.c0["tensor"]
                for cz in range(32):
                    sl = cz % 2
                    if cz >= 2:
                        g.wait_ge(ctx.ch["tensor"], t0 + cz - 1)
                    g.dma(wrin[:, sl, :, :], winr[cz], key=f"wri{sl}")
            def cz_t(t):
                v0 = ctx.c0["vector"]
                for cz in range(32):
                    sl = cz % 2
                    key = f"wri{sl}"
                    ctx.dwait(t, key, ctx.d0.get(key, 0) + cz // 2 + 1)
                    if cz >= 1:
                        t.wait(ctx.ch["vector"], v0 + cz)
                    win_mms(t, sl)
            def cz_v(v):
                t0 = ctx.c0["tensor"]
                for cz in range(32):
                    sl = cz % 2
                    v.wait(ctx.ch["tensor"], t0 + cz + 1)
                    if cz >= 2:
                        key = f"zvs{sl}"
                        ctx.dwait(v, key, ctx.d0.get(key, 0) + cz // 2)
                    v.tensor_copy(evb[:, sl, 0:2048], psW[:, 3:2051])
            def cz_sy(sy):
                v0 = ctx.c0["vector"]
                for cz in range(32):
                    sl = cz % 2
                    sy.wait_ge(ctx.ch["vector"], v0 + cz + 1)
                    sy.dma(z_o[cz * 128:(cz + 1) * 128, :], evb[:, sl, 0:2048],
                           key=f"zvs{sl}")
            ctx.phase({"gpsimd": cz_g, "tensor": cz_t, "vector": cz_v,
                       "sync": cz_sy})

        with nc.sbuf_tensor("xtok", [128, 2, INTER], BF16) as _a, \
             nc.sbuf_tensor("xdd", [128, 2, INTER], BF16) as _b, \
             nc.sbuf_tensor("ddtok", [128, 2, MH], BF16) as _c, \
             nc.sbuf_tensor("btok", [128, 2, MN], BF16) as _d, \
             nc.sbuf_tensor("stnb", [128, INTER], BF16) as _e, \
             nc.psum_tensor("psSt", [128, INTER], F32) as _p:
            xtok, xdd, ddtok, btok, stnb, psSt = (
                _a.ap(), _b.ap(), _c.ap(), _d.ap(), _e.ap(), _p.ap())
            for c in range(NCH):
                def w0(sy, c=c):
                    for sh in range(2):
                        j = 2 * c + sh
                        sy.dma(xtok[:, sh, :], xfm_o[:, j * 128:(j + 1) * 128], transpose=True)
                        sy.dma(ddtok[:, sh, :], ddec[:, j * 128:(j + 1) * 128], transpose=True)
                        sy.dma(btok[:, sh, :], bfm_o[:, j * 128:(j + 1) * 128], transpose=True)
                ctx.phase({"sync": w0})
                def w1(v, c=c):
                    for sh in range(2):
                        v.tensor_tensor(xdd[:, sh, :].rearrange("p (h q) -> p h q", q=MP),
                                        xtok[:, sh, :].rearrange("p (h q) -> p h q", q=MP),
                                        ddtok[:, sh, :].broadcast_to((128, MH, MP)),
                                        ALU.mult)
                ctx.phase({"vector": w1})
                def w2(t, c=c):
                    for nb in range(INTER // 512):
                        for sh in range(2):
                            t.matmul(psSt[:, nb * 512:(nb + 1) * 512], btok[:, sh, :],
                                     xdd[:, sh, nb * 512:(nb + 1) * 512],
                                     start=(sh == 0), stop=(sh == 1))
                    t.mark()
                ctx.phase({"tensor": w2})
                ctx.phase({"vector": lambda v: v.tensor_copy(stnb, psSt)})
                ctx.phase({"sync": lambda sy, c=c: sy.dma(stn_o[c], stnb)})
    return nc


NG8 = 8
_L4_UPTO = 9


def build_l4():
    nc = mk_nc()
    xm = nc.dram_tensor("xm", [T, D], F32, kind="ExternalInput").ap()
    zfm = nc.dram_tensor("zfm", [INTER, T], BF16, kind="ExternalInput").ap()
    xfm = nc.dram_tensor("xfm", [INTER, T], BF16, kind="ExternalInput").ap()
    xtk = nc.dram_tensor("xtk", [T, INTER], BF16, kind="ExternalInput").ap()
    bfm = nc.dram_tensor("bfm", [MN, T], BF16, kind="ExternalInput").ap()
    cfm = nc.dram_tensor("cfm", [MN, T], BF16, kind="ExternalInput").ap()
    dtt = nc.dram_tensor("dtt", [T, MH], F32, kind="ExternalInput").ap()
    acsf = nc.dram_tensor("acsf", [MH, T], F32, kind="ExternalInput").ap()
    nactk = nc.dram_tensor("nactk", [T, MH], F32, kind="ExternalInput").ap()
    ce16t = nc.dram_tensor("ce16t", [16, MH], F32, kind="ExternalInput").ap()
    msk16 = nc.dram_tensor("msk16", [16, 1], F32, kind="ExternalInput").ap()
    st16 = nc.dram_tensor("st16", [MH, 16, MP, MN], BF16, kind="ExternalInput").ap()
    dexp = nc.dram_tensor("dexp", [128, 32], F32, kind="ExternalInput").ap()
    nrmw = nc.dram_tensor("nrmw", [128, 32], F32, kind="ExternalInput").ap()
    wout = nc.dram_tensor("wout", [INTER, D], F32, kind="ExternalInput").ap()
    wg = nc.dram_tensor("wg", [D, E], F32, kind="ExternalInput").ap()
    x2_o = nc.dram_tensor("x2_o", [T, D], F32, kind="ExternalOutput").ap()
    xn3_o = nc.dram_tensor("xn3_o", [T, D], BF16, kind="ExternalOutput").ap()
    eidx = nc.dram_tensor("eidx", [128, NT], U32, kind="ExternalOutput").ap()
    gate = nc.dram_tensor("gate", [128, NT], F32, kind="ExternalOutput").ap()
    xcomp = nc.dram_tensor("xcomp", [T, D], BF16, kind="ExternalOutput").ap()
    lc = nc.dram_tensor("lc", [1, E], F32, kind="ExternalOutput").ap()
    wbd_d = nc.dram_tensor("wbd_d", [NG8, 128, MH], BF16, kind="Internal").ap()
    wob = nc.dram_tensor("wob", [INTER, D], BF16, kind="Internal").ap()
    yraw = nc.dram_tensor("yraw", [32, NCH, 128, CH], BF16, kind="Internal").ap()
    yf_d = nc.dram_tensor("yf_d", [32, 128, T], BF16, kind="Internal").ap()

    with contextlib.ExitStack() as st:
        ctx = Ctx(nc, st)
        sb = lambda n, s_, d: st.enter_context(nc.sbuf_tensor(n, s_, d)).ap()
        ctx.block = st.enter_context(nc.Block())

        castsem = ctx.sem()
        def _cast_sec(g):
            g.dma_start(out=wob, in_=wout).then_inc(castsem, 16)
        ctx.block.gpsimd(_cast_sec)

        nrmq = sb("nrmq", [128, 32], F32)
        ones_b = sb("ones_b", [128, 128], BF16)
        one1 = sb("one1", [1, 128], F32)
        onc128 = sb("onc128", [128, 1], BF16)
        rb = sb("rb", [128, D], F32)
        epsc = sb("epsc", [128, 1], F32)
        idf64 = sb("idf64", [MH, MH], F32)

        def p0(sy):
            sy.dma(nrmq, nrmw)
        def p0g(g):
            g.memset(ones_b, 1.0)
            g.memset(one1, 1.0)
            g.memset(onc128, 1.0)
            g.memset(epsc, float(EPS))
            g.memset(idf64, 0.0)
            g.affine_select(idf64, idf64, [[-1, MH]], ALU.not_equal, 1.0,
                            base=0, channel_multiplier=1)
        ctx.phase({"sync": p0, "gpsimd": p0g})

        with nc.sbuf_tensor("si", [128, MH, NCH, MP], BF16) as _si, \
             nc.sbuf_tensor("dexq", [128, 32], F32) as _dx, \
             nc.sbuf_tensor("ddg", [128, 32, 128], BF16) as _dd, \
             nc.sbuf_tensor("dtq", [128, NT, MH], F32) as _dq, \
             nc.sbuf_tensor("naq", [128, NT, MH], F32) as _nq, \
             nc.sbuf_tensor("acs_s", [MH, T], F32) as _ac:
            si, dexq, ddg = _si.ap(), _dx.ap(), _dd.ap()
            dtq, naq, acs_s = _dq.ap(), _nq.ap(), _ac.ap()

            def ab0_sy(sy):
                sy.dma(dexq, dexp)
                sy.dma(dtq, dtt.rearrange("(n p) h -> p n h", p=128))
                sy.dma(naq, nactk.rearrange("(n p) h -> p n h", p=128))
                sy.dma(acs_s, acsf)
            ctx.phase({"sync": ab0_sy})
            def p1_v(v):
                for h2 in range(32):
                    v.tensor_scalar(ddg[:, h2, :], ones_b, dexq[:, h2:h2 + 1], None, ALU.mult)
            ctx.phase({"vector": p1_v})
            def p1_g(g):
                for h2 in range(32):
                    g.affine_select(ddg[:, h2, :], ddg[:, h2, :], [[-1, 128]],
                                    ALU.is_equal, 0.0, base=0, channel_multiplier=1)
            ctx.phase({"gpsimd": p1_g})

            with contextlib.ExitStack() as stA:
                sbA = lambda n, s_, d: stA.enter_context(nc.sbuf_tensor(n, s_, d)).ap()
                psA_ = lambda n, s_, d: stA.enter_context(nc.psum_tensor(n, s_, d)).ap()
                ce16s = sbA("ce16s", [16, MH], F32)
                m16 = sbA("m16", [16, 1], F32)
                umask = sbA("umask", [16, NCH], F32)
                rhsu = sbA("rhsu", [16, MH, NCH], F32)
                pcsr = sbA("pcsr", [1, NCH * MH], F32)
                lin16 = sbA("lin16", [16, 16], F32)
                s16 = sbA("s16", [16, MH], F32)
                argt = sbA("argt", [16, MH, NCH], F32)
                mzs = sbA("mzs", [16, MH, NCH], F32)
                wtb = sbA("wtb", [16, MH, NCH], BF16)
                on16 = sbA("on16", [16, 1], F32)
                zrow = sbA("zrow", [128, NG8 * MH], BF16)
                wbd = sbA("wbd", [128, NG8, MH], BF16)
                stg = sbA("stg", [128, 2, 8192], BF16)
                psSm = psA_("psSm", [16, 1600], F32)
                psq = psA_("psq", [128, 16, MH], F32)
                def a0_sy(sy):
                    sy.dma(ce16s, ce16t)
                    sy.dma(m16, msk16)
                def a0_g(g):
                    g.memset(on16, 1.0)
                    g.memset(zrow, 0.0)
                    g.memset(umask, 1.0)
                    g.affine_select(umask, umask, [[1, NCH]], ALU.is_ge, 0.0,
                                    base=7, channel_multiplier=-1)
                    g.memset(mzs, 1.0)
                    g.affine_select(mzs, mzs, [[0, MH], [1, NCH]], ALU.is_ge, 0.0,
                                    base=7, channel_multiplier=-1)
                    g.memset(lin16, 1.0)
                    g.affine_select(lin16, lin16, [[1, 16]], ALU.is_ge, 0.0,
                                    base=0, channel_multiplier=-1)
                ctx.phase({"sync": a0_sy, "gpsimd": a0_g})
                if _L4_UPTO <= 0.2:
                    return nc
                def az_sy(sy):
                    for gg in range(NG8):
                        sy.dma(wbd_d[gg], zrow[:, 0:MH])
                ctx.phase({"sync": az_sy})

                def a1_v(v):
                    v.tensor_tensor(rhsu,
                                    umask.rearrange("j (o z) -> j o z", o=1)
                                         .broadcast_to((16, MH, NCH)),
                                    ce16s.rearrange("j h -> j h").broadcast_to((16, MH, NCH)),
                                    ALU.mult)
                ctx.phase({"vector": a1_v})
                if _L4_UPTO <= 0.3:
                    return nc
                def a2_t(t):
                    t.matmul(psSm[0:1, 0:512], on16,
                             rhsu.rearrange("j h z -> j (h z)"), start=True, stop=True)
                    t.matmul(psSm[:, 512:512 + MH], lin16, ce16s, start=True, stop=True)
                    t.mark()
                ctx.phase({"tensor": a2_t})
                def a3_v(v):
                    v.tensor_copy(pcsr, psSm[0:1, 0:512])
                    v.tensor_copy(s16, psSm[:, 512:512 + MH])
                ctx.phase({"vector": a3_v})
                def a4_t(t):
                    t.matmul(psSm[:, 1024:1536], one1[0:1, 0:16], pcsr, start=True, stop=True)
                    t.mark()
                ctx.phase({"tensor": a4_t})
                if _L4_UPTO <= 0.4:
                    return nc
                def a5_v(v):
                    v.tensor_tensor(argt,
                                    psSm[:, 1024:1536].rearrange("c (h z) -> c h z", h=MH),
                                    s16.broadcast_to((16, MH, NCH)), ALU.subtract)
                    v.tensor_scalar_min(argt, argt, 0.0)
                ctx.phase({"vector": a5_v})
                ctx.phase({"scalar": lambda s_: s_.activation(argt, argt, AF.Exp)})
                def a7_v(v):
                    v.tensor_tensor(argt, argt, mzs, ALU.mult)
                    v.tensor_scalar(wtb, argt, m16, None, ALU.mult)
                ctx.phase({"vector": a7_v})
                if _L4_UPTO <= 0.5:
                    return nc
                def a8_g(g):
                    for gg in range(NG8):
                        for hh in range(NG8):
                            g.dma(wbd_d[gg, hh * 16:(hh + 1) * 16, hh * 8:(hh + 1) * 8],
                                  wtb[:, gg * 8 + hh, :])
                ctx.phase({"gpsimd": a8_g})
                ctx.phase({"sync": lambda sy: sy.dma(wbd, wbd_d.rearrange("g p z -> p g z"))})
                if _L4_UPTO <= 0.6:
                    return nc

                def rec_sy(sy):
                    v0 = ctx.c0["vector"]
                    for gg in range(NG8):
                        if gg >= 2:
                            sy.wait_ge(ctx.ch["vector"], v0 + (gg - 1) * 8)
                        sy.dma(stg[:, gg % 2, :], st16[gg * 8:(gg + 1) * 8]
                               .rearrange("h c p n -> (h c) (p n)"), key=f"stg{gg % 2}")
                def rec_t(t):
                    v0 = ctx.c0["vector"]
                    for gg in range(NG8):
                        ctx.dwait(t, f"stg{gg % 2}", gg // 2 + 1)
                        for cb in range(8):
                            kb = gg * 8 + cb
                            if kb >= 2:
                                t.wait(ctx.ch["vector"], v0 + kb - 1)
                            for ci in range(8):
                                ch = cb * 8 + ci
                                t.matmul(psq[:, (kb % 2) * 8 + ci, :],
                                         stg[:, gg % 2, ch * 128:(ch + 1) * 128],
                                         wbd[:, gg, :], start=True, stop=True)
                            t.mark()
                def rec_v(v):
                    t0 = ctx.c0["tensor"]
                    for gg in range(NG8):
                        for cb in range(8):
                            kb = gg * 8 + cb
                            v.wait(ctx.ch["tensor"], t0 + kb + 1)
                            for ci in range(8):
                                ch = cb * 8 + ci
                                dst = si[:, gg * 8:(gg + 1) * 8, :, ch] \
                                    .rearrange("n h z -> n (h z)")
                                if ci < 7:
                                    v.u.tensor_copy(dst, psq[:, (kb % 2) * 8 + ci, :])
                                else:
                                    v.tensor_copy(dst, psq[:, (kb % 2) * 8 + ci, :])
                ctx.phase({"sync": rec_sy, "tensor": rec_t, "vector": rec_v})

            if _L4_UPTO <= 1:
                return nc
            with contextlib.ExitStack() as stB:
                sbB = lambda n, s_, d: stB.enter_context(nc.sbuf_tensor(n, s_, d)).ap()
                psB_ = lambda n, s_, d: stB.enter_context(nc.psum_tensor(n, s_, d)).ap()
                xfh = sbB("xfh", [128, 32, 1024], BF16)
                gg2 = sbB("gg2", [128, 2, CH], BF16)
                bch = sbB("bch", [128, 2, 128], BF16)
                cch = sbB("cch", [128, CH], BF16)
                xdt = sbB("xdt", [128, 2, INTER], BF16)
                stt = sbB("stt", [128, 2, 2, 2, CH], BF16)
                sdt = sbB("sdt", [128, 2, 2, CH], BF16)
                esb = sbB("esb", [128, 2, 2, 2, CH], BF16)
                estin = sbB("estin", [128, 2, 2, 2, CH], F32)
                csd = sbB("csd", [128, 2, 2, CH], BF16)
                yrb = sbB("yrb", [128, 2, CH], BF16)
                psY = psB_("psY", [128, 2, CH], F32)
                psrb = psB_("psrb", [128, 2, 2, CH], F32)
                psG = psB_("psG", [128, 2, CH], F32)
                for half in range(2):
                    ctx.phase({"sync": lambda sy, half=half: sy.dma(
                        xfh, xfm[:, half * 1024:(half + 1) * 1024]
                        .rearrange("(b p) t -> p b t", p=128))})
                    for cc_ in range(4):
                        c = half * 4 + cc_
                        def cp_sy(sy, c=c):
                            for sh in range(2):
                                j = 2 * c + sh
                                sy.dma(xdt[:, sh, :], xtk[j * 128:(j + 1) * 128, :])
                                sy.dma(bch[:, sh, :],
                                       bfm[:, c * CH + sh * 128:c * CH + (sh + 1) * 128])
                            sy.dma(cch, cfm[:, c * CH:(c + 1) * CH])
                        ctx.phase({"sync": cp_sy})
                        def cp_t(t, c=c):
                            for sh in range(2):
                                t.matmul(psG[:, sh, :], bch[:, sh, :], cch,
                                         start=True, stop=True)
                            t.mark()
                        def cp_v(v, c=c):
                            for sh in range(2):
                                v.tensor_tensor(
                                    xdt[:, sh, :].rearrange("p (h q) -> p h q", q=MP),
                                    xdt[:, sh, :].rearrange("p (h q) -> p h q", q=MP),
                                    dtq[:, 2 * c + sh, :].rearrange("p (h o) -> p h o", o=1)
                                       .broadcast_to((128, MH, MP)), ALU.mult)
                        ctx.phase({"tensor": cp_t, "vector": cp_v})
                        def cp_v2(v):
                            for sh in range(2):
                                v.tensor_copy(gg2[:, sh, :], psG[:, sh, :])
                        ctx.phase({"vector": cp_v2})
                        def cp_g(g):
                            for sh in range(2):
                                g.affine_select(gg2[:, sh, :], gg2[:, sh, :], [[1, CH]],
                                                ALU.is_ge, 0.0, base=-sh * 128,
                                                channel_multiplier=-1)
                        ctx.phase({"gpsimd": cp_g})

                        def yl_t(t, c=c, cc_=cc_):
                            v0 = ctx.c0["vector"]
                            s0 = ctx.c0["scalar"]
                            for h2 in range(32):
                                s2 = h2 % 2
                                if h2 >= 2:
                                    t.wait(ctx.ch["vector"], v0 + 4 * (h2 - 2) + 1)
                                    t.wait(ctx.ch["scalar"], s0 + 2 * (h2 - 2) + 2)
                                for hp in range(2):
                                    sel = idf64[:, 2 * h2 + hp:2 * h2 + hp + 1] \
                                        .broadcast_to((MH, 1, 128))[:, 0, :]
                                    t.matmul(psrb[:, s2, hp, :], sel,
                                             acs_s[:, c * CH:(c + 1) * CH],
                                             start=True, stop=True)
                                t.mark()
                                t.wait(ctx.ch["vector"], v0 + 4 * h2 + 3)
                                if h2 >= 2:
                                    t.wait(ctx.ch["vector"], v0 + 4 * (h2 - 2) + 4)
                                for hp in range(2):
                                    h = 2 * h2 + hp
                                    po = psY[hp * 64:(hp + 1) * 64, s2, :]
                                    tp = (0, hp * 64)
                                    t.matmul(po, ddg[:, h2, hp * 64:(hp + 1) * 64],
                                             xfh[:, h2, cc_ * 256:(cc_ + 1) * 256],
                                             start=True, stop=False, tile_position=tp)
                                    for sh in range(2):
                                        t.matmul(po, xdt[:, sh, h * MP:(h + 1) * MP],
                                                 stt[:, s2, hp, sh, :], start=False,
                                                 stop=False, tile_position=tp)
                                    t.matmul(po, si[:, h, c, :], csd[:, s2, hp, :],
                                             start=False, stop=True, tile_position=tp)
                                t.mark()
                        def yl_s(s_, c=c):
                            t0 = ctx.c0["tensor"]
                            v0 = ctx.c0["vector"]
                            for h2 in range(32):
                                s2 = h2 % 2
                                s_.wait(ctx.ch["vector"], v0 + 4 * h2 + 1)
                                for hp in range(2):
                                    for sh in range(2):
                                        op = s_.u.activation if (hp, sh) != (1, 1) \
                                            else s_.activation
                                        op(esb[:, s2, hp, sh, :],
                                           estin[:, s2, hp, sh, :], AF.Exp)
                                s_.wait(ctx.ch["tensor"], t0 + 2 * h2 + 1)
                                if h2 >= 2:
                                    s_.wait(ctx.ch["vector"], v0 + 4 * (h2 - 2) + 3)
                                s_.u.activation(sdt[:, s2, 0, :], psrb[:, s2, 0, :], AF.Exp)
                                s_.activation(sdt[:, s2, 1, :], psrb[:, s2, 1, :], AF.Exp)
                        def yl_v(v, c=c):
                            t0 = ctx.c0["tensor"]
                            s0 = ctx.c0["scalar"]
                            for h2 in range(32):
                                s2 = h2 % 2
                                v.wait(ctx.ch["tensor"], t0 + 2 * h2 + 1)
                                if h2 >= 2:
                                    v.wait(ctx.ch["scalar"], s0 + 2 * (h2 - 2) + 1)
                                for hp in range(2):
                                    for sh in range(2):
                                        op = v.u.tensor_scalar if (hp, sh) != (1, 1) \
                                            else v.tensor_scalar
                                        op(estin[:, s2, hp, sh, :], psrb[:, s2, hp, :],
                                           naq[:, 2 * c + sh,
                                               2 * h2 + hp:2 * h2 + hp + 1],
                                           0.0, ALU.add, ALU.min)
                                v.wait(ctx.ch["scalar"], s0 + 2 * h2 + 1)
                                for hp in range(2):
                                    for sh in range(2):
                                        op = v.u.tensor_tensor if (hp, sh) != (1, 1) \
                                            else v.tensor_tensor
                                        op(stt[:, s2, hp, sh, :], esb[:, s2, hp, sh, :],
                                           gg2[:, sh, :], ALU.mult)
                                v.wait(ctx.ch["scalar"], s0 + 2 * h2 + 2)
                                v.u.tensor_tensor(csd[:, s2, 0, :], cch,
                                                  sdt[:, s2, 0, :], ALU.mult)
                                v.tensor_tensor(csd[:, s2, 1, :], cch,
                                                sdt[:, s2, 1, :], ALU.mult)
                                v.wait(ctx.ch["tensor"], t0 + 2 * h2 + 2)
                                if h2 >= 2:
                                    key = f"yrb{s2}"
                                    ctx.dwait(v, key, ctx.d0.get(key, 0) + h2 // 2)
                                v.tensor_copy(yrb[:, s2, :], psY[:, s2, :])
                        def yl_g(g, c=c):
                            v0 = ctx.c0["vector"]
                            for h2 in range(32):
                                g.wait_ge(ctx.ch["vector"], v0 + 4 * h2 + 4)
                                g.dma(yraw[h2, c], yrb[:, h2 % 2, :], key=f"yrb{h2 % 2}")
                        ctx.phase({"tensor": yl_t, "scalar": yl_s, "vector": yl_v,
                                   "gpsimd": yl_g})

        if _L4_UPTO <= 2:
            return nc
        with contextlib.ExitStack() as stB2:
            sbC = lambda n, s_, d: stB2.enter_context(nc.sbuf_tensor(n, s_, d)).ap()
            zbl = sbC("zbl", [128, 2, T], BF16)
            szb = sbC("szb", [128, T], BF16)
            ybl = sbC("ybl", [128, 2, T], BF16)
            yfb = sbC("yfb", [128, 2, T], BF16)
            sqb = sbC("sqb", [128, 2, T], BF16)
            psV = stB2.enter_context(nc.psum_tensor("psV", [1, D], F32)).ap()
            def b2_sy(sy):
                s0 = ctx.c0["scalar"]
                v0 = ctx.c0["vector"]
                for h2 in range(32):
                    if h2 >= 2:
                        sy.wait_ge(ctx.ch["scalar"], s0 + 2 * (h2 - 1))
                        sy.wait_ge(ctx.ch["vector"], v0 + (h2 - 1))
                    sy.dma(zbl[:, h2 % 2, :], zfm[h2 * 128:(h2 + 1) * 128, :],
                           key=f"zl{h2 % 2}")
                    sy.dma(ybl[:, h2 % 2, :].rearrange("p (c l) -> p c l", l=CH),
                           yraw[h2].rearrange("c p l -> p c l"), key=f"zl{h2 % 2}")
            def b2_s(s_):
                v0 = ctx.c0["vector"]
                t0 = ctx.c0["tensor"]
                for h2 in range(32):
                    ctx.dwait(s_, f"zl{h2 % 2}", 2 * (h2 // 2 + 1))
                    s_.activation(szb, zbl[:, h2 % 2, :], AF.Silu)
                    s_.wait(ctx.ch["vector"], v0 + h2 + 1)
                    if h2 >= 2:
                        s_.wait(ctx.ch["tensor"], t0 + h2 - 1)
                    s_.activation(sqb[:, h2 % 2, :], yfb[:, h2 % 2, :], AF.Square)
            def b2_v(v):
                s0 = ctx.c0["scalar"]
                for h2 in range(32):
                    v.wait(ctx.ch["scalar"], s0 + 2 * h2 + 1)
                    if h2 >= 2:
                        ctx.dwait(v, f"yfs{h2 % 2}", h2 // 2)
                    v.tensor_tensor(yfb[:, h2 % 2, :], ybl[:, h2 % 2, :], szb, ALU.mult)
            def b2_t(t):
                s0 = ctx.c0["scalar"]
                for h2 in range(32):
                    t.wait(ctx.ch["scalar"], s0 + 2 * h2 + 2)
                    for nb in range(4):
                        cs = slice(nb * 512, (nb + 1) * 512)
                        t.matmul(psV[0:1, cs], onc128, sqb[:, h2 % 2, cs],
                                 start=(h2 == 0), stop=(h2 == 31))
                    t.mark()
            def b2_g(g):
                v0 = ctx.c0["vector"]
                for h2 in range(32):
                    g.wait_ge(ctx.ch["vector"], v0 + h2 + 1)
                    g.dma(yf_d[h2], yfb[:, h2 % 2, :], key=f"yfs{h2 % 2}")
            ctx.phase({"sync": b2_sy, "scalar": b2_s, "vector": b2_v,
                       "tensor": b2_t, "gpsimd": b2_g})

            with contextlib.ExitStack() as stRS:
                vrow = stRS.enter_context(nc.sbuf_tensor("vrow", [1, D], F32)).ap()
                psRB = stRS.enter_context(nc.psum_tensor("psRB", [128, D], F32)).ap()
                ctx.phase({"vector": lambda v: v.tensor_scalar(
                    vrow, psV, 1.0 / INTER, None, ALU.mult)})
                def rs_s(s_):
                    s_.activation(vrow, vrow, AF.Ln, bias=epsc[0:1, :])
                    s_.activation(vrow, vrow, AF.Exp, scale=-0.5)
                ctx.phase({"scalar": rs_s})
                def rs_t(t):
                    for nb in range(4):
                        cs = slice(nb * 512, (nb + 1) * 512)
                        t.matmul(psRB[:, cs], one1, vrow[:, cs], start=True, stop=True)
                    t.mark()
                ctx.phase({"tensor": rs_t})
                ctx.phase({"vector": lambda v: v.tensor_copy(rb, psRB)})

        if _L4_UPTO <= 3:
            return nc
        with contextlib.ExitStack() as stC:
            sbD = lambda n, s_, d: stC.enter_context(nc.sbuf_tensor(n, s_, d)).ap()
            wos = sbD("wos", [128, 32, 1024], BF16)
            ynh = sbD("ynh", [128, 32, 1024], BF16)
            yfl = sbD("yfl", [128, 2, 1024], BF16)
            xmt = sbD("xmt", [128, 2, 1024], F32)
            x2r = sbD("x2r", [128, 2, 1024], F32)
            psO = stC.enter_context(nc.psum_tensor("psO", [128, 1024], F32)).ap()
            x2t_d = x2_o.rearrange("(n p) d -> p n d", p=128)
            xmt_d = xm.rearrange("(n p) d -> p n d", p=128)

            for half in range(2):
                def c1_sy(sy, half=half):
                    v0 = ctx.c0["vector"]
                    for h2 in range(32):
                        if h2 >= 2:
                            sy.wait_ge(ctx.ch["vector"], v0 + 2 * (h2 - 1))
                        sy.dma(yfl[:, h2 % 2, :],
                               yf_d[h2, :, half * 1024:(half + 1) * 1024],
                               key=f"yfl{h2 % 2}")
                def c1_v(v, half=half):
                    for h2 in range(32):
                        key = f"yfl{h2 % 2}"
                        ctx.dwait(v, key, ctx.d0.get(key, 0) + h2 // 2 + 1)
                        v.tensor_tensor(yfl[:, h2 % 2, :], yfl[:, h2 % 2, :],
                                        rb[:, half * 1024:(half + 1) * 1024], ALU.mult)
                        v.tensor_scalar(ynh[:, h2, :], yfl[:, h2 % 2, :],
                                        nrmq[:, h2:h2 + 1], None, ALU.mult)
                ctx.phase({"sync": c1_sy, "vector": c1_v})

                for dhalf in range(2):
                    def wl_g(g, dhalf=dhalf):
                        g.wait_ge(castsem, 16)
                        for h2 in range(32):
                            g.dma(wos[:, h2, :],
                                  wob[h2 * 128:(h2 + 1) * 128,
                                      dhalf * 1024:(dhalf + 1) * 1024])
                    ctx.phase({"gpsimd": wl_g})

                    def c2_sy(sy, half=half, dhalf=dhalf):
                        v0 = ctx.c0["vector"]
                        for tc_ in range(NT // 2):
                            tc = half * (NT // 2) + tc_
                            if tc_ >= 2:
                                sy.wait_ge(ctx.ch["vector"], v0 + tc_ - 1)
                            sy.dma(xmt[:, tc_ % 2, :],
                                   xmt_d[:, tc, dhalf * 1024:(dhalf + 1) * 1024],
                                   key=f"xmt{tc_ % 2}")
                    def c2_t(t, half=half):
                        v0 = ctx.c0["vector"]
                        for tc_ in range(NT // 2):
                            if tc_ >= 1:
                                t.wait(ctx.ch["vector"], v0 + tc_)
                            for h2 in range(32):
                                for nb in range(2):
                                    cs = slice(nb * 512, (nb + 1) * 512)
                                    t.matmul(psO[:, cs],
                                             ynh[:, h2, tc_ * 128:(tc_ + 1) * 128],
                                             wos[:, h2, cs],
                                             start=(h2 == 0), stop=(h2 == 31))
                            t.mark()
                    def c2_v(v, half=half):
                        t0 = ctx.c0["tensor"]
                        for tc_ in range(NT // 2):
                            v.wait(ctx.ch["tensor"], t0 + tc_ + 1)
                            key = f"xmt{tc_ % 2}"
                            ctx.dwait(v, key, ctx.d0.get(key, 0) + tc_ // 2 + 1)
                            if tc_ >= 2:
                                key2 = f"x2s{tc_ % 2}"
                                ctx.dwait(v, key2, ctx.d0.get(key2, 0) + tc_ // 2)
                            v.tensor_tensor(x2r[:, tc_ % 2, :], psO, xmt[:, tc_ % 2, :],
                                            ALU.add)
                    def c2_s(s_, half=half, dhalf=dhalf):
                        v0 = ctx.c0["vector"]
                        for tc_ in range(NT // 2):
                            tc = half * (NT // 2) + tc_
                            s_.wait(ctx.ch["vector"], v0 + tc_ + 1)
                            s_.dma(x2t_d[:, tc, dhalf * 1024:(dhalf + 1) * 1024],
                                   x2r[:, tc_ % 2, :], key=f"x2s{tc_ % 2}")
                    ctx.phase({"sync": c2_sy, "tensor": c2_t, "vector": c2_v,
                               "scalar": c2_s})

        if _L4_UPTO <= 4:
            return nc
        emit_ln_router(nc, ctx, st, x2_o, xn3_o, eidx, gate, wg, "q_")
        with contextlib.ExitStack() as stK:
            emit_compact(nc, ctx, stK, eidx, xn3_o, xcomp, lc, "k2_")
    return nc


def prep_ffn_arrays(inputs):
    W1 = np.ascontiguousarray(inputs["L0_ffn_W1"], dtype=np.float32)
    W3 = np.ascontiguousarray(inputs["L0_ffn_W3"], dtype=np.float32)
    W2 = np.ascontiguousarray(inputs["L0_ffn_W2"], dtype=np.float32)
    sw1 = np.ascontiguousarray(inputs["L0_ffn_sw1"], dtype=np.float32)
    sw3 = np.ascontiguousarray(inputs["L0_ffn_sw3"], dtype=np.float32)
    sw2 = np.ascontiguousarray(inputs["L0_ffn_sw2"], dtype=np.float32)
    w1r = np.ascontiguousarray(W1.reshape(E, KD, 128, HC, 128).transpose(0, 3, 2, 1, 4))
    w3r = np.ascontiguousarray(W3.reshape(E, KD, 128, HC, 128).transpose(0, 3, 2, 1, 4))
    w2r = np.ascontiguousarray(W2.reshape(E, HC, 128, D))
    sw1r = np.ascontiguousarray(sw1.reshape(KD, 128, HC, 128).transpose(2, 1, 0, 3))
    sw3r = np.ascontiguousarray(sw3.reshape(KD, 128, HC, 128).transpose(2, 1, 0, 3))
    sw2r = np.ascontiguousarray(sw2.reshape(HC, 128, D))
    return dict(w1r=w1r, w3r=w3r, w2r=w2r, sw1r=sw1r, sw3r=sw3r, sw2r=sw2r)


def ffn_in_maps(x_sl, xn_sl, eidx_cores, gate_cores, wf):
    eidx_pi = np.ascontiguousarray(
        np.concatenate([eidx_cores[c].T for c in range(NCORES)], axis=0))
    maps = []
    for c in range(NCORES):
        selv = (np.arange(128) < c * NT).astype(np.float32).reshape(128, 1)
        maps.append(dict(
            xin=x_sl[c], xn=xn_sl[c], eidx_pi=eidx_pi, selv=selv,
            eoq=np.ascontiguousarray(eidx_cores[c]),
            gateq=np.ascontiguousarray(gate_cores[c]), **wf))
    return maps


_CACHE = {}


def _prog(name, builder):
    if name not in _CACHE:
        _CACHE[name] = builder()
    return _CACHE[name]


def _run(nc, maps):
    return run_bass_kernel_spmd(nc, maps, core_ids=list(range(NCORES)))


def kernel(**inputs):
    f32 = np.float32
    ca = np.ascontiguousarray
    x = ca(inputs["L0_attn_hidden_states"], dtype=f32)
    xf = x.reshape(NCORES * T, D)
    x_sl = [ca(xf[c * T:(c + 1) * T]) for c in range(NCORES)]
    wg = ca(inputs["L0_ffn_Wg"], dtype=f32)
    wf = prep_ffn_arrays(inputs)

    W_in = ca(inputs["L0_attn_W_in"], dtype=f32)
    winp = np.zeros((D, CCW * 128), f32)
    winp[:, :OUT_IN] = W_in
    winr = ca(winp.reshape(KD, 128, CCW, 128).transpose(2, 1, 0, 3))
    conv_w = ca(inputs["L0_attn_conv_w"], dtype=f32)
    conv_b = ca(inputs["L0_attn_conv_b"], dtype=f32)
    convwr = ca(conv_w.T.reshape(CCI, 128, 4).transpose(1, 0, 2))
    convbr = ca(conv_b.reshape(CCI, 128).T)
    dtbr = ca(inputs["L0_attn_dt_bias"], dtype=f32).reshape(MH, 1)
    alogr = ca(inputs["L0_attn_A_log"], dtype=f32).reshape(MH, 1)
    dexp = ca(np.repeat(ca(inputs["L0_attn_D"], dtype=f32), MP).reshape(32, 128).T)
    nrmw = ca(ca(inputs["L0_attn_norm_weight"], dtype=f32).reshape(32, 128).T)
    wout = ca(inputs["L0_attn_W_out"], dtype=f32)

    r1 = _run(_prog("l1", build_l1),
              [{"x0": x_sl[c], "wg": wg} for c in range(NCORES)])
    xn1 = [r1.results[c]["xn1"] for c in range(NCORES)]
    ei1 = [r1.results[c]["eidx"] for c in range(NCORES)]
    ga1 = [r1.results[c]["gate"] for c in range(NCORES)]
    xc1 = [r1.results[c]["xcomp"] for c in range(NCORES)]
    lc1 = [r1.results[c]["lc"] for c in range(NCORES)]

    r2 = _run(_prog("fexp", build_ffn_exp), ffn_exp_in_maps(xn1, xc1, lc1, wf))
    oe1 = [r2.results[c]["oe_o"] for c in range(NCORES)]
    sh1 = [r2.results[c]["shro"] for c in range(NCORES)]

    oeall1 = np.concatenate(oe1, axis=0)
    eip1 = ca(np.concatenate([e.T for e in ei1], axis=0)).astype(np.uint32)
    z128f = np.zeros((128, D), f32)
    z128b = np.zeros((128, D), BF)
    ncol_u = np.full((128, 1), 255, np.uint32)
    zcol_f = np.zeros((128, 1), f32)
    maps3 = []
    for c in range(NCORES):
        odd = c % 2 == 1
        xblb = ca(xf[c * T - 128:c * T]) if odd else z128f
        shlb = ca(sh1[c - 1][T - 128:]) if odd else z128b
        lbq = ca(ei1[c - 1][:, NT - 1:NT]) if odd else ncol_u
        lbg = ca(ga1[c - 1][:, NT - 1:NT]) if odd else zcol_f
        selv = (np.arange(128) < (c * NT - 1 if odd else c * NT)
                ).astype(f32).reshape(128, 1)
        maps3.append(dict(
            xb=x_sl[c], xblb=xblb, shro=sh1[c], shlb=shlb, oeall=oeall1,
            eidx_pi=eip1, selvlb=selv,
            eoq17=ca(np.concatenate([lbq, ei1[c]], axis=1)),
            gq17=ca(np.concatenate([lbg, ga1[c]], axis=1)),
            winr=winr, convwr=convwr, convbr=convbr, dtbr=dtbr, alogr=alogr))
    r3 = _run(_prog("l3", build_l3), maps3)
    xmid = [r3.results[c]["xmid_o"] for c in range(NCORES)]

    maps4 = []
    zst = np.zeros((MH, NCH, MP, MN), BF)
    zce = np.zeros((NCH, MH), f32)
    mk_even = np.concatenate([np.zeros((8, 1)), np.ones((8, 1))]).astype(f32)
    mk_odd = np.ones((16, 1), f32)
    for c in range(NCORES):
        o = r3.results[c]
        own_st = ca(o["stn_o"].reshape(NCH, MN, MH, MP).transpose(2, 0, 3, 1))
        own_ce = ca(o["ce_o"].T)
        if c % 2 == 1:
            po = r3.results[c - 1]
            par_st = ca(po["stn_o"].reshape(NCH, MN, MH, MP).transpose(2, 0, 3, 1))
            par_ce = ca(po["ce_o"].T)
            msk = mk_odd
        else:
            par_st, par_ce, msk = zst.transpose(0, 1, 2, 3), zce, mk_even
        st16 = ca(np.concatenate([par_st, own_st], axis=1))
        ce16 = ca(np.concatenate([par_ce, own_ce], axis=0))
        maps4.append(dict(
            xm=xmid[c], zfm=o["z_o"], xfm=o["xfm_o"],
            xtk=ca(o["xfm_o"].T), bfm=o["bfm_o"], cfm=o["cfm_o"],
            dtt=ca(o["dtf_o"].T), acsf=o["acsf_o"], nactk=ca(o["nacsf_o"].T),
            ce16t=ce16, msk16=msk, st16=st16, dexp=dexp, nrmw=nrmw,
            wout=wout, wg=wg))
    r4 = _run(_prog("l4", build_l4), maps4)
    x2 = [r4.results[c]["x2_o"] for c in range(NCORES)]
    xn3 = [r4.results[c]["xn3_o"] for c in range(NCORES)]
    ei2 = [r4.results[c]["eidx"] for c in range(NCORES)]
    ga2 = [r4.results[c]["gate"] for c in range(NCORES)]
    xc2 = [r4.results[c]["xcomp"] for c in range(NCORES)]
    lc2 = [r4.results[c]["lc"] for c in range(NCORES)]

    r5 = _run(_prog("fexp", build_ffn_exp), ffn_exp_in_maps(xn3, xc2, lc2, wf))
    oe2 = [r5.results[c]["oe_o"] for c in range(NCORES)]
    sh2 = [r5.results[c]["shro"] for c in range(NCORES)]

    r6 = _run(_prog("fin", build_final), final_in_maps(x2, oe2, sh2, ei2, ga2))
    out = np.concatenate([r6.results[c]["out"] for c in range(NCORES)], axis=0)
    return out.reshape(4, 4096, 2048)


# version 05 (final)
# speedup vs baseline: 1.1874x; 1.0237x over previous
import contextlib

import numpy as np
import ml_dtypes
import concourse.bass as bass
import concourse.mybir as mybir
from concourse.bass_utils import run_bass_kernel_spmd

F32 = mybir.dt.float32
BF16 = mybir.dt.bfloat16
I32 = mybir.dt.int32
U32 = mybir.dt.uint32
AF = mybir.ActivationFunctionType
ALU = mybir.AluOpType
AX = mybir.AxisListType.X
BF = ml_dtypes.bfloat16

NCORES = 8
T = 2048
D = 2048
E = 8
HF = 2816
CAP = 2048
NT = T // 128
KD = D // 128
EPS = 1e-5

MH = 64
MP = 64
MN = 128
CH = 256
NCH = T // CH
INTER = MH * MP
CONV_DIM = INTER + 2 * MN
OUT_IN = 2 * (INTER + MN) + MH

ENGS = ("sync", "vector", "scalar", "tensor", "gpsimd")
_PASSTHRU = {"wait_ge", "sem_inc", "nop", "Fori", "dma_start"}


class EW:

    def __init__(self, ctx, en, eng):
        self._ctx, self._en, self._eng = ctx, en, eng

    def wait(self, sem, val):
        self._eng.wait_ge(sem, val)

    @property
    def u(self):
        return self._eng

    def dma(self, out, in_, key=None, **kw):
        ctx, en = self._ctx, self._en
        inst = self._eng.dma_start(out=out, in_=in_, **kw)
        sem, rec = ctx.get_dsem(key, en)
        inst.then_inc(sem, 16)
        rec["cnt"] += 1
        return rec["cnt"]

    def idma(self, out, out_off, in_, in_off, key=None, **kw):
        ctx, en = self._ctx, self._en
        inst = self._eng.indirect_dma_start(out, out_off, in_, in_off, **kw)
        sem, rec = ctx.get_dsem(key, en)
        inst.then_inc(sem, 16)
        rec["cnt"] += 1
        return rec["cnt"]

    def __getattr__(self, name):
        if name.startswith("_"):
            raise AttributeError(name)
        fn = getattr(self._eng, name)
        if name in _PASSTHRU:
            return fn
        ctx, en = self._ctx, self._en

        def call(*a, **kw):
            self._eng.wait_ge(ctx.ch[en], ctx.ccnt[en])
            inst = fn(*a, **kw)
            inst.then_inc(ctx.ch[en], 1)
            ctx.ccnt[en] += 1
            return inst

        return call


class TE:

    def __init__(self, ctx, eng):
        self._ctx, self._eng = ctx, eng
        self._last = None

    def wait(self, sem, val):
        self._eng.wait_ge(sem, val)

    def matmul(self, *a, **kw):
        self._last = self._eng.matmul(*a, **kw)
        return self._last

    def transpose(self, *a, **kw):
        self._last = self._eng.transpose(*a, **kw)
        return self._last

    def mark(self):
        ctx = self._ctx
        assert self._last is not None
        self._last.then_inc(ctx.ch["tensor"], 1)
        ctx.ccnt["tensor"] += 1
        self._last = None
        return ctx.ccnt["tensor"]


class Ctx:
    def __init__(self, nc, stack):
        self.nc = nc
        self.bar = stack.enter_context(nc.semaphore("bar"))
        self.nbar = 0
        self.ch = {e: stack.enter_context(nc.semaphore(f"ch_{e}")) for e in ENGS}
        self.ccnt = {e: 0 for e in ENGS}
        self.block = None
        self._pool = [stack.enter_context(nc.semaphore(f"sp{i}")) for i in range(80)]
        self._pi = 0
        self.dsems = {}

    def sem(self):
        s = self._pool[self._pi]
        self._pi += 1
        return s

    def get_dsem(self, key, en=None):
        key = key if key is not None else f"_default_{en}"
        if key not in self.dsems:
            self.dsems[key] = (self.sem(), {"cnt": 0, "eng": en})
        sem, rec = self.dsems[key]
        if rec["eng"] is None:
            rec["eng"] = en
        assert en is None or rec["eng"] == en, \
            f"dma sem {key} used from {en} and {rec['eng']}"
        return sem, rec

    def dwait(self, w, key, cnt):
        sem, rec = self.get_dsem(key)
        w.wait(sem, 16 * cnt)

    def dtotal(self, key):
        return self.dsems[key][1]["cnt"] if key in self.dsems else 0

    def wrap(self, en, eng):
        return TE(self, eng) if en == "tensor" else EW(self, en, eng)

    def phase(self, fns):
        block = self.block
        self.c0 = dict(self.ccnt)
        self.d0 = {k: v[1]["cnt"] for k, v in self.dsems.items()}
        entry = self.nbar * len(ENGS)
        for en in ENGS:
            reg = getattr(block, en)

            def body(eng, en=en):
                w = self.wrap(en, eng)
                if entry:
                    eng.wait_ge(self.bar, entry)
                if en in fns:
                    fns[en](w)
                if self.ccnt[en]:
                    eng.wait_ge(self.ch[en], self.ccnt[en])
                for key, (sem, rec) in self.dsems.items():
                    if rec["eng"] == en and rec["cnt"]:
                        eng.wait_ge(sem, 16 * rec["cnt"])
                eng.sem_inc(self.bar, 1)

            reg(body)
        self.nbar += 1


def mk_nc():
    return bass.Bass("TRN2", target_bir_lowering=False, debug=False)


def emit_ln_router(nc, ctx, st, xsrc, xn_out, eidx_out, gate_out, wg_in, pre):
    sb = lambda n, s_, d: st.enter_context(nc.sbuf_tensor(pre + n, s_, d)).ap()
    xt = sb("xt", [128, NT // 2, D], F32)
    msum = sb("msum", [128, NT], F32)
    sq = sb("sq", [128, D], BF16)
    var = sb("var", [128, NT], F32)
    m_t = sb("m_t", [128, NT], F32)
    lnv = sb("lnv", [128, NT], F32)
    r_t = sb("r_t", [128, NT], F32)
    xnr = sb("xnr", [128, 4, D], BF16)
    xnf = sb("xnf", [128, KD, T], BF16)
    wgt = sb("wgt", [128, KD, E], F32)
    wgb = sb("wgb", [128, KD, E], BF16)
    lgs = sb("lgs", [128, NT, E], F32)
    mx8 = sb("mx8", [128, 8], F32)
    nmx = sb("nmx", [128, NT], F32)
    ix8 = sb("ix8", [128, 8], U32)
    eix = sb("eix", [128, NT], U32)
    gex = sb("gex", [128, NT, E], F32)
    gsum = sb("gsum", [128, NT], F32)
    gt = sb("gt", [128, NT], F32)
    epsc2 = sb("epsc2", [128, 1], F32)
    lg = st.enter_context(nc.psum_tensor(pre + "lg", [128, NT * E], F32)).ap()

    x0t = xsrc.rearrange("(n p) d -> p n d", p=128)
    xn1t = xn_out.rearrange("(n p) d -> p n d", p=128)
    HT = NT // 2
    ctx.phase({"sync": lambda s: s.dma(wgt, wg_in.rearrange("(k p) e -> p k e", p=128)),
               "gpsimd": lambda g: g.memset(epsc2, float(EPS))})
    for half in range(2):
        jo = half * HT
        ctx.phase({"sync": lambda s, jo=jo: s.dma(xt, x0t[:, jo:jo + HT, :])})

        def p1_vec(v, jo=jo):
            for j in range(HT):
                v.reduce_sum(msum[:, jo + j:jo + j + 1], xt[:, j, :], axis=AX)
        def p1_sca(s, jo=jo):
            for j in range(HT):
                s.activation(sq, xt[:, j, :], AF.Square, accum_out=var[:, jo + j:jo + j + 1])
        ctx.phase({"vector": p1_vec, "scalar": p1_sca})

        def p2_vec(v, jo=jo):
            sl = slice(jo, jo + HT)
            v.tensor_scalar(m_t[:, sl], msum[:, sl], 1.0 / D, None, ALU.mult)
            v.tensor_scalar(var[:, sl], var[:, sl], 1.0 / D, None, ALU.mult)
            v.tensor_tensor(msum[:, sl], m_t[:, sl], m_t[:, sl], ALU.mult)
            v.tensor_tensor(var[:, sl], var[:, sl], msum[:, sl], ALU.subtract)
        ctx.phase({"vector": p2_vec})
        def p3_sca(s, jo=jo):
            sl = slice(jo, jo + HT)
            s.activation(lnv[:, sl], var[:, sl], AF.Ln, bias=epsc2)
            s.activation(r_t[:, sl], lnv[:, sl], AF.Exp, scale=-0.5)
        ctx.phase({"scalar": p3_sca})

        def p4_vec(v, jo=jo):
            for j in range(HT):
                if j >= 4:
                    ctx.dwait(v, pre + f"xnr{j % 4}",
                              ctx.d0.get(pre + f"xnr{j % 4}", 0) + j // 4)
                v.tensor_scalar(xnr[:, j % 4, :], xt[:, j, :], m_t[:, jo + j:jo + j + 1],
                                r_t[:, jo + j:jo + j + 1], ALU.subtract, ALU.mult)
        def p4_sca(s, jo=jo):
            v0 = ctx.c0["vector"]
            for j in range(HT):
                s.wait(ctx.ch["vector"], v0 + j + 1)
                s.dma(xn1t[:, jo + j, :], xnr[:, j % 4, :], key=pre + f"xnr{j % 4}")
        ctx.phase({"vector": p4_vec, "scalar": p4_sca})

    ctx.phase({"vector": lambda v: v.tensor_copy(wgb, wgt)})

    def p5_sync(s):
        for k in range(KD):
            s.dma(xnf[:, k, :], xn_out[:, k * 128:(k + 1) * 128], transpose=True)
    ctx.phase({"sync": p5_sync})

    def p6_ten(t):
        for j in range(NT):
            for k in range(KD):
                t.matmul(lg[:, j * E:(j + 1) * E],
                         xnf[:, k, j * 128:(j + 1) * 128],
                         wgb[:, k, :], start=(k == 0), stop=(k == KD - 1))
        t.mark()
    ctx.phase({"tensor": p6_ten})

    def p7_vec(v):
        v.tensor_copy(lgs, lg.rearrange("p (n e) -> p n e", e=E))
        for j in range(NT):
            v.max(mx8, lgs[:, j, :])
            v.max_index(ix8, mx8, lgs[:, j, :])
            v.tensor_copy(eix[:, j:j + 1], ix8[:, 0:1])
            v.tensor_scalar(nmx[:, j:j + 1], mx8[:, 0:1], -1.0, None, ALU.mult)
    ctx.phase({"vector": p7_vec})

    def p8_sca(s):
        for j in range(NT):
            s.activation(gex[:, j, :], lgs[:, j, :], AF.Exp, bias=nmx[:, j:j + 1])
    ctx.phase({"scalar": p8_sca})
    def p9_vec(v):
        for j in range(NT):
            v.reduce_sum(gsum[:, j:j + 1], gex[:, j, :], axis=AX)
        v.reciprocal(gt, gsum)
    ctx.phase({"vector": p9_vec})

    def p10_sync(s):
        s.dma(eidx_out, eix)
        s.dma(gate_out, gt)
    ctx.phase({"sync": p10_sync})


def emit_compact(nc, ctx, st, eidx_d, xn_d, xcomp, lc_out, pre):
    sb = lambda n, s_, d: st.enter_context(nc.sbuf_tensor(pre + n, s_, d)).ap()
    eoqu = sb("eoqu", [128, NT], U32)
    eoqf = sb("eoqf", [128, NT], F32)
    ohq = sb("ohq", [128, E, NT], F32)
    cnts = sb("cnts", [1, E, NT], F32)
    csc = sb("csc", [1, E, NT], F32)
    cex = sb("cex", [1, E, NT], F32)
    tote = sb("tote", [1, E], F32)
    lcoff = sb("lcoff", [1, E], F32)
    onesT = sb("onesT", [1, NT], F32)
    onesE = sb("onesE", [1, E], F32)
    lincl = sb("lincl", [128, 128], F32)
    ones1 = sb("ones1", [1, 128], F32)
    onc = sb("onc", [128, 1], F32)
    inclq = sb("inclq", [128, E, NT], F32)
    lflat = sb("lflat", [128, NT], F32)
    tsc = sb("tsc", [128, NT], F32)
    lfi = sb("lfi", [128, NT], I32)
    xnrl = sb("xnrl", [128, 2, D], BF16)
    psK = st.enter_context(nc.psum_tensor(pre + "psK", [128, 256], F32)).ap()
    xn_t = xn_d.rearrange("(n p) d -> p n d", p=128)

    def k0_sy(sy):
        sy.dma(eoqu, eidx_d)
    def k0_g(g):
        g.memset(lincl, 1.0)
        g.affine_select(lincl, lincl, [[1, 128]], ALU.is_ge, 0.0,
                        base=0, channel_multiplier=-1)
        g.memset(ones1, 1.0)
        g.memset(onc, 1.0)
        g.memset(onesT, 1.0)
        g.memset(onesE, 1.0)
    ctx.phase({"sync": k0_sy, "gpsimd": k0_g})

    def k1_v(v):
        v.tensor_copy(eoqf, eoqu)
        for e in range(E):
            v.tensor_scalar(ohq[:, e, :], eoqf, float(e), None, ALU.is_equal)
    ctx.phase({"vector": k1_v})

    def k2_t(t):
        for e in range(E):
            t.matmul(psK[0:1, e * NT:(e + 1) * NT], onc, ohq[:, e, :],
                     start=True, stop=True)
        t.mark()
    ctx.phase({"tensor": k2_t})
    def k2b_v(v):
        v.tensor_copy(cnts, psK[0:1, 0:E * NT].rearrange("p (e n) -> p e n", e=E))
        for e in range(E):
            v.tensor_tensor_scan(csc[:, e, :], onesT, cnts[:, e, :], 0.0, ALU.mult, ALU.add)
            v.tensor_copy(tote[:, e:e + 1], csc[:, e, NT - 1:NT])
        v.tensor_tensor(cex, csc, cnts, ALU.subtract)
        v.tensor_tensor_scan(lcoff, onesE, tote, 0.0, ALU.mult, ALU.add)
        v.tensor_tensor(lcoff, lcoff, tote, ALU.subtract)
        for e in range(E):
            v.tensor_scalar_add(cex[:, e, :], cex[:, e, :], lcoff[:, e:e + 1])
    ctx.phase({"vector": k2b_v})

    def k3_t(t):
        for e in range(E):
            t.matmul(psK[:, e * NT:(e + 1) * NT], lincl, ohq[:, e, :], start=True, stop=False)
            t.matmul(psK[:, e * NT:(e + 1) * NT], ones1, cex[:, e, :], start=False, stop=True)
        t.mark()
    ctx.phase({"tensor": k3_t})
    def k3b_v(v):
        v.tensor_copy(inclq, psK[:, 0:E * NT].rearrange("p (e n) -> p e n", e=E))
        v.memset(lflat, 0.0)
        for e in range(E):
            v.tensor_scalar(tsc, inclq[:, e, :], -1.0, None, ALU.add)
            v.tensor_tensor(tsc, tsc, ohq[:, e, :], ALU.mult)
            v.tensor_tensor(lflat, lflat, tsc, ALU.add)
        v.tensor_copy(lfi, lflat)
    ctx.phase({"vector": k3b_v})
    ctx.phase({"sync": lambda s: s.dma(lc_out, tote)})

    def k4_sy(s):
        for j in range(NT):
            if j >= 2:
                kk = pre + f"ks{j % 2}"
                ctx.dwait(s, kk, ctx.d0.get(kk, 0) + j // 2)
            s.dma(xnrl[:, j % 2, :], xn_t[:, j, :], key=pre + f"xr{j % 2}")
    def k4_g(g):
        for j in range(NT):
            key = pre + f"xr{j % 2}"
            ctx.dwait(g, key, ctx.d0.get(key, 0) + j // 2 + 1)
            g.idma(xcomp, bass.IndirectOffsetOnAxis(lfi[:, j:j + 1], 0),
                   xnrl[:, j % 2, :], None,
                   bounds_check=T - 1, oob_is_err=False, key=pre + f"ks{j % 2}")
    ctx.phase({"sync": k4_sy, "gpsimd": k4_g})


def build_l1():
    nc = mk_nc()
    x0 = nc.dram_tensor("x0", [T, D], F32, kind="ExternalInput").ap()
    wg = nc.dram_tensor("wg", [D, E], F32, kind="ExternalInput").ap()
    xn1 = nc.dram_tensor("xn1", [T, D], BF16, kind="ExternalOutput").ap()
    eidx = nc.dram_tensor("eidx", [128, NT], U32, kind="ExternalOutput").ap()
    gate = nc.dram_tensor("gate", [128, NT], F32, kind="ExternalOutput").ap()
    xcomp = nc.dram_tensor("xcomp", [T, D], BF16, kind="ExternalOutput").ap()
    lc = nc.dram_tensor("lc", [1, E], F32, kind="ExternalOutput").ap()
    with contextlib.ExitStack() as st:
        ctx = Ctx(nc, st)
        ctx.block = st.enter_context(nc.Block())
        emit_ln_router(nc, ctx, st, x0, xn1, eidx, gate, wg, "r_")
        emit_compact(nc, ctx, st, eidx, xn1, xcomp, lc, "k_")
    return nc


HC = HF // 128


def build_ffn():
    nc = mk_nc()
    xin = nc.dram_tensor("xin", [T, D], F32, kind="ExternalInput").ap()
    xn = nc.dram_tensor("xn", [T, D], BF16, kind="ExternalInput").ap()
    eidx_pi = nc.dram_tensor("eidx_pi", [128, 128], U32, kind="ExternalInput").ap()
    selv = nc.dram_tensor("selv", [128, 1], F32, kind="ExternalInput").ap()
    eoq = nc.dram_tensor("eoq", [128, NT], U32, kind="ExternalInput").ap()
    gateq = nc.dram_tensor("gateq", [128, NT], F32, kind="ExternalInput").ap()
    w1r = nc.dram_tensor("w1r", [E, HC, 128, KD, 128], F32, kind="ExternalInput").ap()
    w3r = nc.dram_tensor("w3r", [E, HC, 128, KD, 128], F32, kind="ExternalInput").ap()
    w2r = nc.dram_tensor("w2r", [E, HC, 128, D], F32, kind="ExternalInput").ap()
    sw1r = nc.dram_tensor("sw1r", [HC, 128, KD, 128], F32, kind="ExternalInput").ap()
    sw3r = nc.dram_tensor("sw3r", [HC, 128, KD, 128], F32, kind="ExternalInput").ap()
    sw2r = nc.dram_tensor("sw2r", [HC, 128, D], F32, kind="ExternalInput").ap()
    xmid = nc.dram_tensor("xmid", [T, D], F32, kind="ExternalOutput").ap()

    with contextlib.ExitStack() as st:
        ctx = Ctx(nc, st)
        sb = lambda n, s_, d: st.enter_context(nc.sbuf_tensor(n, s_, d)).ap()
        ps = lambda n, s_, d: st.enter_context(nc.psum_tensor(n, s_, d)).ap()
        wbig = sb("wbig", [128, HC, D], BF16)
        hbuf = sb("hbuf", [128, HC, T], BF16)
        scr = sb("scr", [128, 2 * D], F32)
        accs = scr.rearrange("p (a b) -> p a b", a=2)
        _scrb = scr.bitcast(BF16)
        wr1 = _scrb[:, 0:2 * KD * 128].rearrange("p (s k h) -> p s k h", s=2, k=KD)
        wr3 = _scrb[:, 2 * KD * 128:4 * KD * 128].rearrange("p (s k h) -> p s k h", s=2, k=KD)
        sA = sb("sA", [128, T], BF16)
        efu = sb("efu", [128, 128], U32)
        ef = sb("ef", [128, 128], F32)
        ohg = sb("ohg", [128, 128], F32)
        rcg = sb("rcg", [128, E], F32)
        ooff = sb("ooff", [1, E], F32)
        eoqu = sb("eoqu", [128, NT], U32)
        eoqf = sb("eoqf", [128, NT], F32)
        ohq = sb("ohq", [128, E, NT], F32)
        cnts = sb("cnts", [1, E, NT], F32)
        csc = sb("csc", [1, E, NT], F32)
        cex = sb("cex", [1, E, NT], F32)
        onesT = sb("onesT", [1, NT], F32)
        gq5 = sb("gq5", [128, NT], F32)
        lincl = sb("lincl", [128, 128], F32)
        ones1 = sb("ones1", [1, 128], F32)
        onc = sb("onc", [128, 1], F32)
        inclq = sb("inclq", [128, E, NT], F32)
        masks = sb("masks", [128, E, NT], F32)
        tsc = sb("tsc", [128, NT], F32)
        selv_sb = sb("selv_sb", [128, 1], F32)
        psA = ps("psA", [128, D], F32)
        psB = ps("psB", [128, D], F32)
        xnf = wbig.rearrange("p c d -> p (c d)")[:, 0:KD * T].rearrange("p (k t) -> p k t", k=KD)
        w2res = wbig

        ctx.block = st.enter_context(nc.Block())
        xmid_t = xmid.rearrange("(n p) d -> p n d", p=128)
        xin_t = xin.rearrange("(n p) d -> p n d", p=128)

        def p0_sync(s):
            s.dma(efu, eidx_pi)
            s.dma(eoqu, eoq)
            s.dma(gq5, gateq)
            s.dma(selv_sb, selv)
        def p0_g(g):
            g.memset(lincl, 1.0)
            g.affine_select(lincl, lincl, [[1, 128]], ALU.is_ge, 0.0, base=0, channel_multiplier=-1)
            g.memset(ones1, 1.0)
            g.memset(onc, 1.0)
            g.memset(onesT, 1.0)
        ctx.phase({"sync": p0_sync, "gpsimd": p0_g})

        def p1_vec(v):
            v.tensor_copy(ef, efu)
            v.tensor_copy(eoqf, eoqu)
            v.tensor_scalar(gq5, gq5, 0.5, None, ALU.mult)
            for e in range(E):
                v.tensor_scalar(ohg, ef, float(e), None, ALU.is_equal)
                v.reduce_sum(rcg[:, e:e + 1], ohg, axis=AX)
            for e in range(E):
                v.tensor_scalar(ohq[:, e, :], eoqf, float(e), None, ALU.is_equal)
        ctx.phase({"vector": p1_vec})

        def p2_ten(t):
            t.matmul(psB[0:1, 0:E], selv_sb, rcg, start=True, stop=True)
            for e in range(E):
                t.matmul(psB[0:1, E + e * NT:E + (e + 1) * NT], onc, ohq[:, e, :],
                         start=True, stop=True)
            t.mark()
        ctx.phase({"tensor": p2_ten})
        def p2b_vec(v):
            v.tensor_copy(ooff, psB[0:1, 0:E])
            v.tensor_copy(cnts, psB[0:1, E:E + E * NT].rearrange("p (e n) -> p e n", e=E))
            for e in range(E):
                v.tensor_tensor_scan(csc[:, e, :], onesT, cnts[:, e, :], 0.0, ALU.mult, ALU.add)
            v.tensor_tensor(cex, csc, cnts, ALU.subtract)
            for e in range(E):
                v.tensor_scalar_add(cex[:, e, :], cex[:, e, :], ooff[:, e:e + 1])
        ctx.phase({"vector": p2b_vec})

        def p3_ten(t):
            for e in range(E):
                t.matmul(psB[:, e * NT:(e + 1) * NT], lincl, ohq[:, e, :], start=True, stop=False)
                t.matmul(psB[:, e * NT:(e + 1) * NT], ones1, cex[:, e, :], start=False, stop=True)
            t.mark()
        ctx.phase({"tensor": p3_ten})
        def p3b_vec(v):
            v.tensor_copy(inclq, psB[:, 0:E * NT].rearrange("p (e n) -> p e n", e=E))
            for e in range(E):
                v.tensor_scalar(tsc, inclq[:, e, :], float(CAP), None, ALU.is_le)
                v.tensor_tensor(tsc, tsc, ohq[:, e, :], ALU.mult)
                v.tensor_tensor(masks[:, e, :], tsc, gq5, ALU.mult)
        ctx.phase({"vector": p3b_vec})

        for pi in range(E + 1):
            shared = (pi == 0)
            e = pi - 1

            def pa_sync(s):
                for k in range(KD):
                    s.dma(xnf[:, k, :], xn[:, k * 128:(k + 1) * 128], transpose=True)
            ctx.phase({"sync": pa_sync})

            def pb_g(g, shared=shared, e=e):
                t0 = ctx.c0["tensor"]
                for hc in range(HC):
                    if hc >= 2:
                        g.wait_ge(ctx.ch["tensor"], t0 + hc - 1)
                    sl = hc % 2
                    if shared:
                        g.dma(wr1[:, sl, :, :], sw1r[hc], key=f"wr{sl}")
                        g.dma(wr3[:, sl, :, :], sw3r[hc], key=f"wr{sl}")
                    else:
                        g.dma(wr1[:, sl, :, :], w1r[e, hc], key=f"wr{sl}")
                        g.dma(wr3[:, sl, :, :], w3r[e, hc], key=f"wr{sl}")
            def pb_t(t):
                v0 = ctx.c0["vector"]
                s0 = ctx.c0["scalar"]
                for hc in range(HC):
                    key = f"wr{hc % 2}"
                    ctx.dwait(t, key, ctx.d0.get(key, 0) + 2 * (hc // 2 + 1))
                    if hc >= 1:
                        t.wait(ctx.ch["scalar"], s0 + hc)
                        t.wait(ctx.ch["vector"], v0 + hc)
                    sl = hc % 2
                    for kc in range(KD):
                        for nb in range(4):
                            cs = slice(nb * 512, (nb + 1) * 512)
                            t.matmul(psA[:, cs], wr1[:, sl, kc, :], xnf[:, kc, cs],
                                     start=(kc == 0), stop=(kc == KD - 1))
                    for kc in range(KD):
                        for nb in range(4):
                            cs = slice(nb * 512, (nb + 1) * 512)
                            t.matmul(psB[:, cs], wr3[:, sl, kc, :], xnf[:, kc, cs],
                                     start=(kc == 0), stop=(kc == KD - 1))
                    t.mark()
            def pb_s(s):
                t0 = ctx.c0["tensor"]
                v0 = ctx.c0["vector"]
                for hc in range(HC):
                    s.wait(ctx.ch["tensor"], t0 + hc + 1)
                    if hc >= 1:
                        s.wait(ctx.ch["vector"], v0 + hc)
                    s.activation(sA, psA[:], AF.Silu)
            def pb_v(v):
                t0 = ctx.c0["tensor"]
                s0 = ctx.c0["scalar"]
                for hc in range(HC):
                    v.wait(ctx.ch["tensor"], t0 + hc + 1)
                    v.wait(ctx.ch["scalar"], s0 + hc + 1)
                    v.tensor_tensor(hbuf[:, hc, :], sA, psB[:], ALU.mult)
            ctx.phase({"gpsimd": pb_g, "tensor": pb_t, "scalar": pb_s, "vector": pb_v})

            def pc_g(g, shared=shared, e=e):
                for hc in range(HC):
                    g.dma(w2res[:, hc, :], (sw2r[hc] if shared else w2r[e, hc]))
            ctx.phase({"gpsimd": pc_g})

            def pd_sync(s, pi=pi):
                src = xin_t if pi == 0 else xmid_t
                for sc in range(NT):
                    if sc >= 2:
                        key = f"ast{sc % 2}"
                        ctx.dwait(s, key, ctx.d0.get(key, 0) + sc // 2)
                    s.dma(accs[:, sc % 2, :], src[:, sc, :], key=f"acl{sc % 2}")
            def pd_t(t):
                v0 = ctx.c0["vector"]
                for sc in range(NT):
                    if sc >= 1:
                        t.wait(ctx.ch["vector"], v0 + sc)
                    for hc in range(HC):
                        for nb in range(4):
                            cs = slice(nb * 512, (nb + 1) * 512)
                            t.matmul(psA[:, cs], hbuf[:, hc, sc * 128:(sc + 1) * 128],
                                     w2res[:, hc, cs], start=(hc == 0), stop=(hc == HC - 1))
                    t.mark()
            def pd_v(v, pi=pi, e=e):
                t0 = ctx.c0["tensor"]
                for sc in range(NT):
                    v.wait(ctx.ch["tensor"], t0 + sc + 1)
                    key = f"acl{sc % 2}"
                    ctx.dwait(v, key, ctx.d0.get(key, 0) + sc // 2 + 1)
                    if pi == 0:
                        v.scalar_tensor_tensor(accs[:, sc % 2, :], psA[:], 0.5,
                                               accs[:, sc % 2, :], ALU.mult, ALU.add)
                    else:
                        v.scalar_tensor_tensor(accs[:, sc % 2, :], psA[:], masks[:, e, sc:sc + 1],
                                               accs[:, sc % 2, :], ALU.mult, ALU.add)
            def pd_s(s):
                v0 = ctx.c0["vector"]
                for sc in range(NT):
                    s.wait(ctx.ch["vector"], v0 + sc + 1)
                    s.dma(xmid_t[:, sc, :], accs[:, sc % 2, :], key=f"ast{sc % 2}")
            ctx.phase({"sync": pd_sync, "tensor": pd_t, "vector": pd_v, "scalar": pd_s})
    return nc


def build_ffn_exp():
    nc = mk_nc()
    xcompall = nc.dram_tensor("xcompall", [NCORES * T, D], BF16, kind="ExternalInput").ap()
    xnown = nc.dram_tensor("xnown", [T, D], BF16, kind="ExternalInput").ap()
    lcT = nc.dram_tensor("lcT", [E, NCORES], F32, kind="ExternalInput").ap()
    esel = nc.dram_tensor("esel", [128, 1], F32, kind="ExternalInput").ap()
    w1e = nc.dram_tensor("w1e", [HC, 128, KD, 128], F32, kind="ExternalInput").ap()
    w3e = nc.dram_tensor("w3e", [HC, 128, KD, 128], F32, kind="ExternalInput").ap()
    w2e = nc.dram_tensor("w2e", [HC, 128, D], F32, kind="ExternalInput").ap()
    sw1r = nc.dram_tensor("sw1r", [HC, 128, KD, 128], F32, kind="ExternalInput").ap()
    sw3r = nc.dram_tensor("sw3r", [HC, 128, KD, 128], F32, kind="ExternalInput").ap()
    sw2r = nc.dram_tensor("sw2r", [HC, 128, D], F32, kind="ExternalInput").ap()
    oe_o = nc.dram_tensor("oe_o", [CAP, D], BF16, kind="ExternalOutput").ap()
    shro = nc.dram_tensor("shro", [T, D], BF16, kind="ExternalOutput").ap()
    xd = nc.dram_tensor("xd", [CAP, D], BF16, kind="Internal").ap()
    w2b = nc.dram_tensor("w2b", [HC, 128, D], BF16, kind="Internal").ap()
    sw2b = nc.dram_tensor("sw2b", [HC, 128, D], BF16, kind="Internal").ap()

    with contextlib.ExitStack() as st:
        ctx = Ctx(nc, st)
        sb = lambda n, s_, d: st.enter_context(nc.sbuf_tensor(n, s_, d)).ap()
        ps = lambda n, s_, d: st.enter_context(nc.psum_tensor(n, s_, d)).ap()
        ctx.block = st.enter_context(nc.Block())

        castsem = ctx.sem()
        def _cast_sec(g):
            g.dma_start(out=sw2b, in_=sw2r).then_inc(castsem, 16)
            g.dma_start(out=w2b, in_=w2e).then_inc(castsem, 16)
        ctx.block.gpsimd(_cast_sec)

        with contextlib.ExitStack() as st2:
            sb2 = lambda n, s_, d: st2.enter_context(nc.sbuf_tensor(n, s_, d)).ap()
            lcts = sb2("lcts", [E, NCORES], F32)
            esl = sb2("esl", [128, 1], F32)
            iop = sb2("iop", [E, 1], F32)
            ohc = sb2("ohc", [E, 1], F32)
            ltc = sb2("ltc", [E, 1], F32)
            lcrow = sb2("lcrow", [1, NCORES], F32)
            lofr = sb2("lofr", [1, NCORES], F32)
            gexr = sb2("gexr", [1, NCORES], F32)
            oner = sb2("oner", [1, NCORES], F32)
            ones1 = sb2("ones1", [1, 128], F32)
            LCB = sb2("LCB", [128, NCORES], F32)
            LOB = sb2("LOB", [128, NCORES], F32)
            GB = sb2("GB", [128, NCORES], F32)
            Sio = sb2("Sio", [128, NT], F32)
            uu = sb2("uu", [128, NT], F32)
            tt1 = sb2("tt1", [128, NT], F32)
            vt = sb2("vt", [128, NT], F32)
            idxf = sb2("idxf", [128, NT], F32)
            idxi = sb2("idxi", [128, NT], I32)
            xg = sb2("xg", [128, 2, D], BF16)
            psD = st2.enter_context(nc.psum_tensor("psD", [128, 64], F32)).ap()
            xd_t = xd.rearrange("(n p) d -> p n d", p=128)

            def d0_sy(sy):
                sy.dma(lcts, lcT)
                sy.dma(esl, esel)
            def d0_g(g):
                g.iota(iop, [[0, 1]], channel_multiplier=1,
                       allow_small_or_imprecise_dtypes=True)
                g.iota(Sio, [[128, NT]], channel_multiplier=1,
                       allow_small_or_imprecise_dtypes=True)
                g.memset(ones1, 1.0)
                g.memset(oner, 1.0)
            ctx.phase({"sync": d0_sy, "gpsimd": d0_g})

            def d1_v(v):
                v.tensor_scalar(ohc, iop, esl[0:E, :], None, ALU.is_equal)
                v.tensor_scalar(ltc, iop, esl[0:E, :], None, ALU.is_lt)
            ctx.phase({"vector": d1_v})
            def d2_t(t):
                t.matmul(psD[0:1, 0:NCORES], ohc, lcts, start=True, stop=True)
                t.matmul(psD[0:1, NCORES:2 * NCORES], ltc, lcts, start=True, stop=True)
                t.mark()
            ctx.phase({"tensor": d2_t})
            def d3_v(v):
                v.tensor_copy(lcrow, psD[0:1, 0:NCORES])
                v.tensor_copy(lofr, psD[0:1, NCORES:2 * NCORES])
                v.tensor_tensor_scan(gexr, oner, lcrow, 0.0, ALU.mult, ALU.add)
                v.tensor_tensor(gexr, gexr, lcrow, ALU.subtract)
            ctx.phase({"vector": d3_v})
            def d4_t(t):
                t.matmul(psD[:, 2 * NCORES:3 * NCORES], ones1, lcrow, start=True, stop=True)
                t.matmul(psD[:, 3 * NCORES:4 * NCORES], ones1, lofr, start=True, stop=True)
                t.matmul(psD[:, 4 * NCORES:5 * NCORES], ones1, gexr, start=True, stop=True)
                t.mark()
            ctx.phase({"tensor": d4_t})
            def d5_v(v):
                v.tensor_copy(LCB, psD[:, 2 * NCORES:3 * NCORES])
                v.tensor_copy(LOB, psD[:, 3 * NCORES:4 * NCORES])
                v.tensor_copy(GB, psD[:, 4 * NCORES:5 * NCORES])
                v.memset(idxf, 0.0)
                for k in range(NCORES):
                    v.tensor_scalar(uu, Sio, GB[:, k:k + 1], None, ALU.subtract)
                    v.tensor_scalar(tt1, uu, 0.0, None, ALU.is_ge)
                    v.tensor_scalar(vt, uu, LCB[:, k:k + 1], None, ALU.is_lt)
                    v.tensor_tensor(tt1, tt1, vt, ALU.mult)
                    v.tensor_scalar(vt, uu, LOB[:, k:k + 1], None, ALU.add)
                    v.tensor_scalar(vt, vt, float(k * T), None, ALU.add)
                    v.tensor_tensor(vt, vt, tt1, ALU.mult)
                    v.tensor_tensor(idxf, idxf, vt, ALU.add)
                v.tensor_copy(idxi, idxf)
            ctx.phase({"vector": d5_v})

            def d6_g(g):
                for n in range(NT):
                    if n >= 2:
                        key = f"xds{n % 2}"
                        ctx.dwait(g, key, ctx.d0.get(key, 0) + n // 2)
                    g.idma(xg[:, n % 2, :], None, xcompall,
                           bass.IndirectOffsetOnAxis(idxi[:, n:n + 1], 0),
                           bounds_check=NCORES * T - 1, oob_is_err=False,
                           key=f"xgl{n % 2}")
            def d6_sy(s):
                for n in range(NT):
                    key = f"xgl{n % 2}"
                    ctx.dwait(s, key, ctx.d0.get(key, 0) + n // 2 + 1)
                    s.dma(xd_t[:, n, :], xg[:, n % 2, :], key=f"xds{n % 2}")
            ctx.phase({"gpsimd": d6_g, "sync": d6_sy})

        wbig = sb("wbig", [128, HC, D], BF16)
        hbuf = sb("hbuf", [128, HC, T], BF16)
        scr = sb("scr", [128, 2 * D], F32)
        _scrb = scr.bitcast(BF16)
        wr1 = _scrb[:, 0:2 * KD * 128].rearrange("p (s k h) -> p s k h", s=2, k=KD)
        wr3 = _scrb[:, 2 * KD * 128:4 * KD * 128].rearrange("p (s k h) -> p s k h", s=2, k=KD)
        oer = _scrb[:, 0:2 * D].rearrange("p (s d) -> p s d", s=2)
        sA = sb("sA", [128, T], BF16)
        psA = ps("psA", [128, D], F32)
        psB = ps("psB", [128, D], F32)
        xnf = wbig.rearrange("p c d -> p (c d)")[:, 0:KD * T].rearrange("p (k t) -> p k t", k=KD)
        w2res = wbig

        for pi in range(2):
            src = xnown if pi == 0 else xd
            wa, wb = (sw1r, sw3r) if pi == 0 else (w1e, w3e)
            wc = sw2b if pi == 0 else w2b
            dst = shro if pi == 0 else oe_o
            dst_t = dst.rearrange("(n p) d -> p n d", p=128)

            def pa_sync(s, src=src):
                for k in range(KD):
                    s.dma(xnf[:, k, :], src[:, k * 128:(k + 1) * 128], transpose=True)
            ctx.phase({"sync": pa_sync})

            def pb_g(g, wa=wa, wb=wb):
                t0 = ctx.c0["tensor"]
                for hc in range(HC):
                    if hc >= 2:
                        g.wait_ge(ctx.ch["tensor"], t0 + hc - 1)
                    sl = hc % 2
                    g.dma(wr1[:, sl, :, :], wa[hc], key=f"wr{sl}")
                    g.dma(wr3[:, sl, :, :], wb[hc], key=f"wr{sl}")
            def pb_t(t):
                v0 = ctx.c0["vector"]
                s0 = ctx.c0["scalar"]
                for hc in range(HC):
                    key = f"wr{hc % 2}"
                    ctx.dwait(t, key, ctx.d0.get(key, 0) + 2 * (hc // 2 + 1))
                    if hc >= 1:
                        t.wait(ctx.ch["scalar"], s0 + hc)
                        t.wait(ctx.ch["vector"], v0 + hc)
                    sl = hc % 2
                    for kc in range(KD):
                        for nb in range(4):
                            cs = slice(nb * 512, (nb + 1) * 512)
                            t.matmul(psA[:, cs], wr1[:, sl, kc, :], xnf[:, kc, cs],
                                     start=(kc == 0), stop=(kc == KD - 1))
                    for kc in range(KD):
                        for nb in range(4):
                            cs = slice(nb * 512, (nb + 1) * 512)
                            t.matmul(psB[:, cs], wr3[:, sl, kc, :], xnf[:, kc, cs],
                                     start=(kc == 0), stop=(kc == KD - 1))
                    t.mark()
            def pb_s(s):
                t0 = ctx.c0["tensor"]
                v0 = ctx.c0["vector"]
                for hc in range(HC):
                    s.wait(ctx.ch["tensor"], t0 + hc + 1)
                    if hc >= 1:
                        s.wait(ctx.ch["vector"], v0 + hc)
                    s.activation(sA, psA[:], AF.Silu)
            def pb_v(v):
                t0 = ctx.c0["tensor"]
                s0 = ctx.c0["scalar"]
                for hc in range(HC):
                    v.wait(ctx.ch["tensor"], t0 + hc + 1)
                    v.wait(ctx.ch["scalar"], s0 + hc + 1)
                    v.tensor_tensor(hbuf[:, hc, :], sA, psB[:], ALU.mult)
            ctx.phase({"gpsimd": pb_g, "tensor": pb_t, "scalar": pb_s, "vector": pb_v})

            def pc_g(g, wc=wc, pi=pi):
                if pi == 0:
                    g.wait_ge(castsem, 32)
                for hc in range(HC):
                    g.dma(w2res[:, hc, :], wc[hc])
            ctx.phase({"gpsimd": pc_g})

            def pd_t(t):
                v0 = ctx.c0["vector"]
                for sc in range(NT):
                    if sc >= 1:
                        t.wait(ctx.ch["vector"], v0 + sc)
                    for hc in range(HC):
                        for nb in range(4):
                            cs = slice(nb * 512, (nb + 1) * 512)
                            t.matmul(psA[:, cs], hbuf[:, hc, sc * 128:(sc + 1) * 128],
                                     w2res[:, hc, cs], start=(hc == 0), stop=(hc == HC - 1))
                    t.mark()
            def pd_v(v):
                t0 = ctx.c0["tensor"]
                for sc in range(NT):
                    v.wait(ctx.ch["tensor"], t0 + sc + 1)
                    if sc >= 2:
                        key = f"oes{sc % 2}"
                        ctx.dwait(v, key, ctx.d0.get(key, 0) + sc // 2)
                    v.tensor_copy(oer[:, sc % 2, :], psA[:])
            def pd_s(s, dst_t=dst_t):
                v0 = ctx.c0["vector"]
                for sc in range(NT):
                    s.wait(ctx.ch["vector"], v0 + sc + 1)
                    s.dma(dst_t[:, sc, :], oer[:, sc % 2, :], key=f"oes{sc % 2}")
            ctx.phase({"tensor": pd_t, "vector": pd_v, "scalar": pd_s})
    return nc



def emit_combine(nc, ctx, st, eidx_pi, selvlb, eoq17, gq17, oeall,
                 xb_blocks, sh_blocks, out_blocks, pre):
    sb = lambda n, s_, d: st.enter_context(nc.sbuf_tensor(pre + n, s_, d)).ap()
    NB = len(xb_blocks)
    efu = sb("efu", [128, 128], U32)
    ef = sb("ef", [128, 128], F32)
    ohg = sb("ohg", [128, 128], F32)
    rcg = sb("rcg", [128, E], F32)
    ooff = sb("ooff", [1, E], F32)
    eoqu = sb("eoqu", [128, NB], U32)
    eoqf = sb("eoqf", [128, NB], F32)
    gq = sb("gq", [128, NB], F32)
    selv = sb("selv", [128, 1], F32)
    ohq = sb("ohq", [128, E, NB], F32)
    cnts = sb("cnts", [1, E, NB], F32)
    csc = sb("csc", [1, E, NB], F32)
    cex = sb("cex", [1, E, NB], F32)
    onesT = sb("onesT", [1, NB], F32)
    lincl = sb("lincl", [128, 128], F32)
    ones1 = sb("ones1", [1, 128], F32)
    onc = sb("onc", [128, 1], F32)
    inclq = sb("inclq", [128, E, NB], F32)
    flatq = sb("flatq", [128, NB], F32)
    keepq = sb("keepq", [128, NB], F32)
    gk = sb("gk", [128, NB], F32)
    idxq = sb("idxq", [128, NB], I32)
    tsc = sb("tsc", [128, NB], F32)
    oeg = sb("oeg", [128, 2, D], BF16)
    xbt = sb("xbt", [128, 2, D], F32)
    sht = sb("sht", [128, 2, D], BF16)
    otr = sb("otr", [128, 2, D], F32)
    psC = st.enter_context(nc.psum_tensor(pre + "psC", [128, 256], F32)).ap()

    def c0_sy(sy):
        sy.dma(efu, eidx_pi)
        sy.dma(eoqu, eoq17)
        sy.dma(gq, gq17)
        sy.dma(selv, selvlb)
    def c0_g(g):
        g.memset(lincl, 1.0)
        g.affine_select(lincl, lincl, [[1, 128]], ALU.is_ge, 0.0,
                        base=0, channel_multiplier=-1)
        g.memset(ones1, 1.0)
        g.memset(onc, 1.0)
        g.memset(onesT, 1.0)
    ctx.phase({"sync": c0_sy, "gpsimd": c0_g})

    def c1_v(v):
        v.tensor_copy(ef, efu)
        v.tensor_copy(eoqf, eoqu)
        v.tensor_scalar(gq, gq, 0.5, None, ALU.mult)
        for e in range(E):
            v.tensor_scalar(ohg, ef, float(e), None, ALU.is_equal)
            v.reduce_sum(rcg[:, e:e + 1], ohg, axis=AX)
        for e in range(E):
            v.tensor_scalar(ohq[:, e, :], eoqf, float(e), None, ALU.is_equal)
    ctx.phase({"vector": c1_v})

    def c2_t(t):
        t.matmul(psC[0:1, 0:E], selv, rcg, start=True, stop=True)
        for e in range(E):
            t.matmul(psC[0:1, E + e * NB:E + (e + 1) * NB], onc, ohq[:, e, :],
                     start=True, stop=True)
        t.mark()
    ctx.phase({"tensor": c2_t})
    def c2b_v(v):
        v.tensor_copy(ooff, psC[0:1, 0:E])
        v.tensor_copy(cnts, psC[0:1, E:E + E * NB].rearrange("p (e n) -> p e n", e=E))
        for e in range(E):
            v.tensor_tensor_scan(csc[:, e, :], onesT, cnts[:, e, :], 0.0, ALU.mult, ALU.add)
        v.tensor_tensor(cex, csc, cnts, ALU.subtract)
        for e in range(E):
            v.tensor_scalar_add(cex[:, e, :], cex[:, e, :], ooff[:, e:e + 1])
    ctx.phase({"vector": c2b_v})

    def c3_t(t):
        for e in range(E):
            t.matmul(psC[:, e * NB:(e + 1) * NB], lincl, ohq[:, e, :], start=True, stop=False)
            t.matmul(psC[:, e * NB:(e + 1) * NB], ones1, cex[:, e, :], start=False, stop=True)
        t.mark()
    ctx.phase({"tensor": c3_t})
    def c3b_v(v):
        v.tensor_copy(inclq, psC[:, 0:E * NB].rearrange("p (e n) -> p e n", e=E))
        v.memset(flatq, 0.0)
        v.memset(keepq, 0.0)
        for e in range(E):
            v.tensor_scalar(tsc, inclq[:, e, :], float(CAP), None, ALU.is_le)
            v.tensor_tensor(tsc, tsc, ohq[:, e, :], ALU.mult)
            v.tensor_tensor(keepq, keepq, tsc, ALU.add)
            v.tensor_scalar(tsc, inclq[:, e, :], float(e * CAP) - 1.0, None, ALU.add)
            v.tensor_tensor(tsc, tsc, ohq[:, e, :], ALU.mult)
            v.tensor_tensor(flatq, flatq, tsc, ALU.add)
        v.tensor_scalar_min(flatq, flatq, float(E * CAP - 1))
        v.tensor_tensor(gk, gq, keepq, ALU.mult)
        v.tensor_copy(idxq, flatq)
    ctx.phase({"vector": c3b_v})

    def cl_g(g):
        v0 = ctx.c0["vector"]
        for j in range(NB):
            sl = j % 2
            if j >= 2:
                g.wait_ge(ctx.ch["vector"], v0 + 2 * (j - 1))
            g.idma(oeg[:, sl, :], None, oeall,
                   bass.IndirectOffsetOnAxis(idxq[:, j:j + 1], 0),
                   key=pre + f"oeg{sl}")
    def cl_sy(sy):
        v0 = ctx.c0["vector"]
        for j in range(NB):
            sl = j % 2
            if j >= 2:
                sy.wait(ctx.ch["vector"], v0 + 2 * (j - 1))
            sy.dma(xbt[:, sl, :], xb_blocks[j], key=pre + f"xb{sl}")
            sy.dma(sht[:, sl, :], sh_blocks[j], key=pre + f"sh{sl}")
    def cl_v(v):
        for j in range(NB):
            sl = j % 2
            n = j // 2 + 1
            ctx.dwait(v, pre + f"oeg{sl}", ctx.d0.get(pre + f"oeg{sl}", 0) + n)
            ctx.dwait(v, pre + f"xb{sl}", ctx.d0.get(pre + f"xb{sl}", 0) + n)
            ctx.dwait(v, pre + f"sh{sl}", ctx.d0.get(pre + f"sh{sl}", 0) + n)
            if j >= 2:
                ctx.dwait(v, pre + f"ot{sl}", ctx.d0.get(pre + f"ot{sl}", 0) + j // 2)
            v.scalar_tensor_tensor(otr[:, sl, :], sht[:, sl, :], 0.5,
                                   xbt[:, sl, :], ALU.mult, ALU.add)
            v.scalar_tensor_tensor(otr[:, sl, :], oeg[:, sl, :], gk[:, j:j + 1],
                                   otr[:, sl, :], ALU.mult, ALU.add)
    def cl_s(s_):
        v0 = ctx.c0["vector"]
        for j in range(NB):
            sl = j % 2
            s_.wait(ctx.ch["vector"], v0 + 2 * (j + 1))
            s_.dma(out_blocks[j], otr[:, sl, :], key=pre + f"ot{sl}")
    ctx.phase({"gpsimd": cl_g, "sync": cl_sy, "vector": cl_v, "scalar": cl_s})


def build_final():
    nc = mk_nc()
    x2 = nc.dram_tensor("x2", [T, D], F32, kind="ExternalInput").ap()
    shro = nc.dram_tensor("shro", [T, D], BF16, kind="ExternalInput").ap()
    oeall = nc.dram_tensor("oeall", [E * CAP, D], BF16, kind="ExternalInput").ap()
    eidx_pi = nc.dram_tensor("eidx_pi", [128, 128], U32, kind="ExternalInput").ap()
    selvlb = nc.dram_tensor("selvlb", [128, 1], F32, kind="ExternalInput").ap()
    eoq = nc.dram_tensor("eoq", [128, NT], U32, kind="ExternalInput").ap()
    gateq = nc.dram_tensor("gateq", [128, NT], F32, kind="ExternalInput").ap()
    out = nc.dram_tensor("out", [T, D], F32, kind="ExternalOutput").ap()
    with contextlib.ExitStack() as st:
        ctx = Ctx(nc, st)
        ctx.block = st.enter_context(nc.Block())
        x2t = x2.rearrange("(n p) d -> p n d", p=128)
        sht = shro.rearrange("(n p) d -> p n d", p=128)
        outt = out.rearrange("(n p) d -> p n d", p=128)
        emit_combine(nc, ctx, st, eidx_pi, selvlb, eoq, gateq, oeall,
                     [x2t[:, j, :] for j in range(NT)],
                     [sht[:, j, :] for j in range(NT)],
                     [outt[:, j, :] for j in range(NT)], "cb_")
    return nc


def ffn_exp_in_maps(xn_sl, xcomp_sl, lc_sl, wf):
    xcompall = np.concatenate(xcomp_sl, axis=0)
    lcT = np.ascontiguousarray(np.concatenate(lc_sl, axis=0).T)
    maps = []
    for c in range(NCORES):
        maps.append(dict(
            xcompall=xcompall, xnown=xn_sl[c], lcT=lcT,
            esel=np.full((128, 1), float(c), np.float32),
            w1e=wf["w1r"][c], w3e=wf["w3r"][c], w2e=wf["w2r"][c],
            sw1r=wf["sw1r"], sw3r=wf["sw3r"], sw2r=wf["sw2r"]))
    return maps


def final_in_maps(x2_sl, oe_sl, shro_sl, eidx_c, gate_c):
    oeall = np.concatenate(oe_sl, axis=0)
    eidx_pi = np.ascontiguousarray(
        np.concatenate([e.T for e in eidx_c], axis=0)).astype(np.uint32)
    maps = []
    for c in range(NCORES):
        maps.append(dict(
            x2=x2_sl[c], shro=shro_sl[c], oeall=oeall, eidx_pi=eidx_pi,
            selvlb=(np.arange(128) < c * NT).astype(np.float32).reshape(128, 1),
            eoq=eidx_c[c], gateq=gate_c[c]))
    return maps


TPAD = 2176
NTP = TPAD // 128
CCW = 67
CCI = CONV_DIM // 128
NB5 = [(0, 512), (512, 1024), (1024, 1536), (1536, 2048), (2048, 2176)]


def build_l3():
    nc = mk_nc()
    xb = nc.dram_tensor("xb", [T, D], F32, kind="ExternalInput").ap()
    xblb = nc.dram_tensor("xblb", [128, D], F32, kind="ExternalInput").ap()
    shro = nc.dram_tensor("shro", [T, D], BF16, kind="ExternalInput").ap()
    shlb = nc.dram_tensor("shlb", [128, D], BF16, kind="ExternalInput").ap()
    oeall = nc.dram_tensor("oeall", [NCORES * T, D], BF16, kind="ExternalInput").ap()
    eidx_pi = nc.dram_tensor("eidx_pi", [128, 128], U32, kind="ExternalInput").ap()
    selvlb = nc.dram_tensor("selvlb", [128, 1], F32, kind="ExternalInput").ap()
    eoq17 = nc.dram_tensor("eoq17", [128, NTP], U32, kind="ExternalInput").ap()
    gq17 = nc.dram_tensor("gq17", [128, NTP], F32, kind="ExternalInput").ap()
    winr = nc.dram_tensor("winr", [CCW, 128, KD, 128], F32, kind="ExternalInput").ap()
    convwr = nc.dram_tensor("convwr", [128, CCI, 4], F32, kind="ExternalInput").ap()
    convbr = nc.dram_tensor("convbr", [128, CCI], F32, kind="ExternalInput").ap()
    dtbr = nc.dram_tensor("dtbr", [MH, 1], F32, kind="ExternalInput").ap()
    alogr = nc.dram_tensor("alogr", [MH, 1], F32, kind="ExternalInput").ap()
    z_o = nc.dram_tensor("z_o", [INTER, T], BF16, kind="ExternalOutput").ap()
    xfm_o = nc.dram_tensor("xfm_o", [INTER, T], BF16, kind="ExternalOutput").ap()
    bfm_o = nc.dram_tensor("bfm_o", [MN, T], BF16, kind="ExternalOutput").ap()
    cfm_o = nc.dram_tensor("cfm_o", [MN, T], BF16, kind="ExternalOutput").ap()
    dtf_o = nc.dram_tensor("dtf_o", [MH, T], F32, kind="ExternalOutput").ap()
    acsf_o = nc.dram_tensor("acsf_o", [MH, T], F32, kind="ExternalOutput").ap()
    nacsf_o = nc.dram_tensor("nacsf_o", [MH, T], F32, kind="ExternalOutput").ap()
    ce_o = nc.dram_tensor("ce_o", [MH, NCH], F32, kind="ExternalOutput").ap()
    stn_o = nc.dram_tensor("stn_o", [NCH, MN, INTER], BF16, kind="ExternalOutput").ap()
    xmid_o = nc.dram_tensor("xmid_o", [T, D], F32, kind="ExternalOutput").ap()
    xn2 = nc.dram_tensor("xn2", [TPAD, D], BF16, kind="Internal").ap()
    ddec = nc.dram_tensor("ddec", [MH, T], BF16, kind="Internal").ap()
    xlb_d = nc.dram_tensor("xlb_d", [128, D], F32, kind="Internal").ap()

    with contextlib.ExitStack() as st:
        ctx = Ctx(nc, st)
        sb = lambda n, s_, d: st.enter_context(nc.sbuf_tensor(n, s_, d)).ap()
        ctx.block = st.enter_context(nc.Block())

        with contextlib.ExitStack() as stC:
            xbt_ = xb.rearrange("(n p) d -> p n d", p=128)
            sht_ = shro.rearrange("(n p) d -> p n d", p=128)
            xmt_ = xmid_o.rearrange("(n p) d -> p n d", p=128)
            emit_combine(nc, ctx, stC, eidx_pi, selvlb, eoq17, gq17, oeall,
                         [xblb] + [xbt_[:, j, :] for j in range(NT)],
                         [shlb] + [sht_[:, j, :] for j in range(NT)],
                         [xlb_d] + [xmt_[:, j, :] for j in range(NT)], "cb_")

        cw = sb("cw", [128, CCI, 4], F32)
        cb = sb("cb", [128, CCI], F32)
        dtb_t = sb("dtb_t", [MH, 1], F32)
        nega = sb("nega", [MH, 1], F32)
        dtft = sb("dtft", [MH, T], F32)
        acst = sb("acst", [MH, T], F32)
        tmp64 = sb("tmp64", [MH, T], F32)
        cet = sb("cet", [MH, NCH], F32)
        onc64 = sb("onc64", [MH, 1], F32)
        ones_b = sb("ones_b", [128, 128], BF16)
        xn2t = xn2.rearrange("(n p) d -> p n d", p=128)

        def p0(sy):
            sy.dma(cw, convwr)
            sy.dma(cb, convbr)
            sy.dma(dtb_t, dtbr)
            sy.dma(nega, alogr)
        def p0g(g):
            g.memset(onc64, 1.0)
            g.memset(ones_b, 1.0)
        ctx.phase({"sync": p0, "gpsimd": p0g})
        ctx.phase({"scalar": lambda s_: s_.activation(nega, nega, AF.Exp)})
        ctx.phase({"vector": lambda v: v.tensor_scalar(nega, nega, -1.0, None, ALU.mult)})

        with nc.sbuf_tensor("lnx", [128, NTP, D], F32) as _a, \
             nc.sbuf_tensor("lsq", [128, D], BF16) as _b, \
             nc.sbuf_tensor("lms", [128, NTP], F32) as _c, \
             nc.sbuf_tensor("lvr", [128, NTP], F32) as _d, \
             nc.sbuf_tensor("lm", [128, NTP], F32) as _e, \
             nc.sbuf_tensor("llv", [128, NTP], F32) as _f, \
             nc.sbuf_tensor("lr", [128, NTP], F32) as _g, \
             nc.sbuf_tensor("lnr", [128, 4, D], BF16) as _h, \
             nc.sbuf_tensor("lneps", [128, 1], F32) as _i:
            lnx, lsq, lms, lvr, lm, llv, lr, lnr, lneps = (
                _a.ap(), _b.ap(), _c.ap(), _d.ap(), _e.ap(), _f.ap(), _g.ap(), _h.ap(), _i.ap())

            def q0_g(g):
                g.memset(lnx[:, NTP - 1, :], 0.0)
                g.memset(lneps, float(EPS))
            def q0_sy(sy):
                sy.dma(lnx[0:3, 0, :], xlb_d[125:128, :])
                sy.dma(lnx[3:128, 0, :], xmid_o[0:125])
                sy.dma(lnx[:, 1:NTP - 1, :],
                       xmid_o[125:2045].rearrange("(n p) d -> p n d", p=128))
            ctx.phase({"gpsimd": q0_g, "sync": q0_sy})
            ctx.phase({"sync": lambda sy: sy.dma(lnx[0:3, NTP - 1, :], xmid_o[2045:2048])})

            def q1_v(v):
                for j in range(NTP):
                    v.reduce_sum(lms[:, j:j + 1], lnx[:, j, :], axis=AX)
            def q1_s(s_):
                for j in range(NTP):
                    s_.activation(lsq, lnx[:, j, :], AF.Square, accum_out=lvr[:, j:j + 1])
            ctx.phase({"vector": q1_v, "scalar": q1_s})

            def q2_v(v):
                v.tensor_scalar(lm, lms, 1.0 / D, None, ALU.mult)
                v.tensor_scalar(lvr, lvr, 1.0 / D, None, ALU.mult)
                v.tensor_tensor(lms, lm, lm, ALU.mult)
                v.tensor_tensor(lvr, lvr, lms, ALU.subtract)
            ctx.phase({"vector": q2_v})
            def q3_s(s_):
                s_.activation(llv, lvr, AF.Ln, bias=lneps)
                s_.activation(lr, llv, AF.Exp, scale=-0.5)
            ctx.phase({"scalar": q3_s})

            def q4_v(v):
                for j in range(NTP):
                    if j >= 4:
                        ctx.dwait(v, f"l3nr{j % 4}", j // 4)
                    v.tensor_scalar(lnr[:, j % 4, :], lnx[:, j, :], lm[:, j:j + 1],
                                    lr[:, j:j + 1], ALU.subtract, ALU.mult)
            def q4_s(s_):
                v0 = ctx.c0["vector"]
                for j in range(NTP):
                    s_.wait(ctx.ch["vector"], v0 + j + 1)
                    s_.dma(xn2t[:, j, :], lnr[:, j % 4, :], key=f"l3nr{j % 4}")
            ctx.phase({"vector": q4_v, "scalar": q4_s})

        with nc.sbuf_tensor("xn2f", [128, KD, TPAD], BF16) as _a, \
             nc.sbuf_tensor("wrin", [128, 2, KD, 128], BF16) as _b, \
             nc.sbuf_tensor("evb", [128, 2, TPAD], BF16) as _c, \
             nc.sbuf_tensor("xbcp", [128, TPAD], BF16) as _d, \
             nc.sbuf_tensor("cdg", [128, 4, 128], BF16) as _e, \
             nc.psum_tensor("psW", [128, TPAD], F32) as _p:
            xn2f, wrin, evb, xbcp, cdg, psW = (
                _a.ap(), _b.ap(), _c.ap(), _d.ap(), _e.ap(), _p.ap())
            psC = psW[:, 0:2048]

            def r0_sy(sy):
                for k in range(KD):
                    sy.dma(xn2f[:, k, :], xn2[:, k * 128:(k + 1) * 128], transpose=True)
            ctx.phase({"sync": r0_sy})

            def win_mms(t, sl):
                for kc in range(KD):
                    for (a, b) in NB5:
                        t.matmul(psW[:, a:b], wrin[:, sl, kc, :], xn2f[:, kc, a:b],
                                 start=(kc == 0), stop=(kc == KD - 1))
                t.mark()

            ctx.phase({"gpsimd": lambda g: g.dma(wrin[:, 0, :, :], winr[66])})
            ctx.phase({"tensor": lambda t: win_mms(t, 0)})
            dtraw = psW[0:64, 3:2051]
            ctx.phase({"vector": lambda v: v.tensor_scalar_add(tmp64, dtraw, dtb_t)})
            with nc.sbuf_tensor("spa", [MH, T], F32) as _sa, \
                 nc.sbuf_tensor("spl", [MH, T], F32) as _sl2, \
                 nc.sbuf_tensor("dect", [MH, T], F32) as _dc, \
                 nc.sbuf_tensor("ddecb", [MH, T], BF16) as _db, \
                 nc.sbuf_tensor("onec", [MH, 1], F32) as _oc:
                spa, spl, dect, ddecb, onec = _sa.ap(), _sl2.ap(), _dc.ap(), _db.ap(), _oc.ap()
                ctx.phase({"gpsimd": lambda g: g.memset(onec, 1.0)})
                def s1_s(s_):
                    s_.activation(spa, tmp64, AF.Abs)
                    s_.activation(spa, spa, AF.Exp, scale=-1.0)
                    s_.activation(spl, spa, AF.Ln, bias=onec)
                ctx.phase({"scalar": s1_s})
                def s2_v(v):
                    v.tensor_scalar_max(dtft, tmp64, 0.0)
                    v.tensor_tensor(dtft, dtft, spl, ALU.add)
                    v.tensor_scalar_min(dtft, dtft, 100.0)
                    v.tensor_scalar(tmp64, dtft, nega, None, ALU.mult)
                    for c in range(NCH):
                        cs = slice(c * CH, (c + 1) * CH)
                        v.tensor_tensor_scan(acst[:, cs],
                                             onc64.broadcast_to((MH, 1, CH))[:, 0, :],
                                             tmp64[:, cs], 0.0, ALU.mult, ALU.add)
                    v.tensor_copy(cet, acst.rearrange("h (c l) -> h c l", l=CH)[:, :, CH - 1])
                ctx.phase({"vector": s2_v})
                def s3_s(s_):
                    for c in range(NCH):
                        cs = slice(c * CH, (c + 1) * CH)
                        s_.activation(dect[:, cs], acst[:, cs], AF.Exp, scale=-1.0,
                                      bias=cet[:, c:c + 1])
                ctx.phase({"scalar": s3_s})
                def s4_v(v):
                    v.tensor_tensor(dect, dtft, dect, ALU.mult)
                    v.tensor_copy(ddecb, dect)
                    v.tensor_scalar(tmp64, acst, -1.0, None, ALU.mult)
                ctx.phase({"vector": s4_v})
                def s5_sy(sy):
                    sy.dma(dtf_o, dtft)
                    sy.dma(acsf_o, acst)
                    sy.dma(nacsf_o, tmp64)
                    sy.dma(ce_o, cet)
                    sy.dma(ddec, ddecb)
                ctx.phase({"sync": s5_sy})

            with nc.sbuf_tensor("cvt", [128, 2, 2048], F32) as _cv:
                cvt = _cv.ap()

                def ci_g(g):
                    t0 = ctx.c0["tensor"]
                    for ci in range(CCI):
                        sl = ci % 2
                        if ci >= 2:
                            g.wait_ge(ctx.ch["tensor"], t0 + ci - 1)
                        g.dma(wrin[:, sl, :, :], winr[32 + ci], key=f"wri{sl}")
                def ci_t(t):
                    v0 = ctx.c0["vector"]
                    for ci in range(CCI):
                        sl = ci % 2
                        key = f"wri{sl}"
                        ctx.dwait(t, key, ctx.d0.get(key, 0) + ci // 2 + 1)
                        if ci >= 1:
                            t.wait(ctx.ch["vector"], v0 + 4 * ci)
                        win_mms(t, sl)
                def ci_v(v):
                    t0 = ctx.c0["tensor"]
                    s0 = ctx.c0["scalar"]
                    for ci in range(CCI):
                        sl = ci % 2
                        v.wait(ctx.ch["tensor"], t0 + ci + 1)
                        if ci >= 2:
                            v.wait(ctx.ch["scalar"], s0 + ci - 1)
                        v.tensor_scalar(cvt[:, sl, :], psW[:, 0:2048],
                                        cw[:, ci, 0:1], None, ALU.mult)
                        for k in range(1, 4):
                            v.scalar_tensor_tensor(cvt[:, sl, :], psW[:, k:k + 2048],
                                                   cw[:, ci, k:k + 1],
                                                   cvt[:, sl, :], ALU.mult, ALU.add)
                def ci_s(s_):
                    v0 = ctx.c0["vector"]
                    for ci in range(CCI):
                        sl = ci % 2
                        s_.wait(ctx.ch["vector"], v0 + 4 * ci + 4)
                        if ci >= 2:
                            key = f"evs{sl}"
                            ctx.dwait(s_, key, ctx.d0.get(key, 0) + ci // 2)
                        s_.activation(evb[:, sl, 0:2048], cvt[:, sl, :], AF.Silu,
                                      bias=cb[:, ci:ci + 1])
                def ci_sy(sy):
                    s0 = ctx.c0["scalar"]
                    for ci in range(CCI):
                        sl = ci % 2
                        sy.wait_ge(ctx.ch["scalar"], s0 + ci + 1)
                        if ci < 32:
                            sy.dma(xfm_o[ci * 128:(ci + 1) * 128, :],
                                   evb[:, sl, 0:2048], key=f"evs{sl}")
                        elif ci == 32:
                            sy.dma(bfm_o, evb[:, sl, 0:2048], key=f"evs{sl}")
                        else:
                            sy.dma(cfm_o, evb[:, sl, 0:2048], key=f"evs{sl}")
                ctx.phase({"gpsimd": ci_g, "tensor": ci_t, "vector": ci_v,
                           "scalar": ci_s, "sync": ci_sy})

            def cz_g(g):
                t0 = ctx.c0["tensor"]
                for cz in range(32):
                    sl = cz % 2
                    if cz >= 2:
                        g.wait_ge(ctx.ch["tensor"], t0 + cz - 1)
                    g.dma(wrin[:, sl, :, :], winr[cz], key=f"wri{sl}")
            def cz_t(t):
                v0 = ctx.c0["vector"]
                for cz in range(32):
                    sl = cz % 2
                    key = f"wri{sl}"
                    ctx.dwait(t, key, ctx.d0.get(key, 0) + cz // 2 + 1)
                    if cz >= 1:
                        t.wait(ctx.ch["vector"], v0 + cz)
                    win_mms(t, sl)
            def cz_v(v):
                t0 = ctx.c0["tensor"]
                for cz in range(32):
                    sl = cz % 2
                    v.wait(ctx.ch["tensor"], t0 + cz + 1)
                    if cz >= 2:
                        key = f"zvs{sl}"
                        ctx.dwait(v, key, ctx.d0.get(key, 0) + cz // 2)
                    v.tensor_copy(evb[:, sl, 0:2048], psW[:, 3:2051])
            def cz_sy(sy):
                v0 = ctx.c0["vector"]
                for cz in range(32):
                    sl = cz % 2
                    sy.wait_ge(ctx.ch["vector"], v0 + cz + 1)
                    sy.dma(z_o[cz * 128:(cz + 1) * 128, :], evb[:, sl, 0:2048],
                           key=f"zvs{sl}")
            ctx.phase({"gpsimd": cz_g, "tensor": cz_t, "vector": cz_v,
                       "sync": cz_sy})

        with nc.sbuf_tensor("xtok", [128, 2, INTER], BF16) as _a, \
             nc.sbuf_tensor("xdd", [128, 2, INTER], BF16) as _b, \
             nc.sbuf_tensor("ddtok", [128, 2, MH], BF16) as _c, \
             nc.sbuf_tensor("btok", [128, 2, MN], BF16) as _d, \
             nc.sbuf_tensor("stnb", [128, INTER], BF16) as _e, \
             nc.psum_tensor("psSt", [128, INTER], F32) as _p:
            xtok, xdd, ddtok, btok, stnb, psSt = (
                _a.ap(), _b.ap(), _c.ap(), _d.ap(), _e.ap(), _p.ap())
            for c in range(NCH):
                def w0(sy, c=c):
                    for sh in range(2):
                        j = 2 * c + sh
                        sy.dma(xtok[:, sh, :], xfm_o[:, j * 128:(j + 1) * 128], transpose=True)
                        sy.dma(ddtok[:, sh, :], ddec[:, j * 128:(j + 1) * 128], transpose=True)
                        sy.dma(btok[:, sh, :], bfm_o[:, j * 128:(j + 1) * 128], transpose=True)
                ctx.phase({"sync": w0})
                def w1(v, c=c):
                    for sh in range(2):
                        v.tensor_tensor(xdd[:, sh, :].rearrange("p (h q) -> p h q", q=MP),
                                        xtok[:, sh, :].rearrange("p (h q) -> p h q", q=MP),
                                        ddtok[:, sh, :].broadcast_to((128, MH, MP)),
                                        ALU.mult)
                ctx.phase({"vector": w1})
                def w2(t, c=c):
                    for nb in range(INTER // 512):
                        for sh in range(2):
                            t.matmul(psSt[:, nb * 512:(nb + 1) * 512], btok[:, sh, :],
                                     xdd[:, sh, nb * 512:(nb + 1) * 512],
                                     start=(sh == 0), stop=(sh == 1))
                    t.mark()
                ctx.phase({"tensor": w2})
                ctx.phase({"vector": lambda v: v.tensor_copy(stnb, psSt)})
                ctx.phase({"sync": lambda sy, c=c: sy.dma(stn_o[c], stnb)})
    return nc


NG8 = 8
_L4_UPTO = 9


def build_l4():
    nc = mk_nc()
    xm = nc.dram_tensor("xm", [T, D], F32, kind="ExternalInput").ap()
    zfm = nc.dram_tensor("zfm", [INTER, T], BF16, kind="ExternalInput").ap()
    xfm = nc.dram_tensor("xfm", [INTER, T], BF16, kind="ExternalInput").ap()
    xtk = nc.dram_tensor("xtk", [T, INTER], BF16, kind="ExternalInput").ap()
    bfm = nc.dram_tensor("bfm", [MN, T], BF16, kind="ExternalInput").ap()
    cfm = nc.dram_tensor("cfm", [MN, T], BF16, kind="ExternalInput").ap()
    dtt = nc.dram_tensor("dtt", [T, MH], F32, kind="ExternalInput").ap()
    acsf = nc.dram_tensor("acsf", [MH, T], F32, kind="ExternalInput").ap()
    nactk = nc.dram_tensor("nactk", [T, MH], F32, kind="ExternalInput").ap()
    ce16t = nc.dram_tensor("ce16t", [16, MH], F32, kind="ExternalInput").ap()
    msk16 = nc.dram_tensor("msk16", [16, 1], F32, kind="ExternalInput").ap()
    st16 = nc.dram_tensor("st16", [MH, 16, MP, MN], BF16, kind="ExternalInput").ap()
    dexp = nc.dram_tensor("dexp", [128, 32], F32, kind="ExternalInput").ap()
    nrmw = nc.dram_tensor("nrmw", [128, 32], F32, kind="ExternalInput").ap()
    wout = nc.dram_tensor("wout", [INTER, D], F32, kind="ExternalInput").ap()
    wg = nc.dram_tensor("wg", [D, E], F32, kind="ExternalInput").ap()
    x2_o = nc.dram_tensor("x2_o", [T, D], F32, kind="ExternalOutput").ap()
    xn3_o = nc.dram_tensor("xn3_o", [T, D], BF16, kind="ExternalOutput").ap()
    eidx = nc.dram_tensor("eidx", [128, NT], U32, kind="ExternalOutput").ap()
    gate = nc.dram_tensor("gate", [128, NT], F32, kind="ExternalOutput").ap()
    xcomp = nc.dram_tensor("xcomp", [T, D], BF16, kind="ExternalOutput").ap()
    lc = nc.dram_tensor("lc", [1, E], F32, kind="ExternalOutput").ap()
    wbd_d = nc.dram_tensor("wbd_d", [NG8, 128, MH], BF16, kind="Internal").ap()
    wob = nc.dram_tensor("wob", [INTER, D], BF16, kind="Internal").ap()
    yraw = nc.dram_tensor("yraw", [32, NCH, 128, CH], BF16, kind="Internal").ap()
    yf_d = nc.dram_tensor("yf_d", [32, 128, T], BF16, kind="Internal").ap()

    with contextlib.ExitStack() as st:
        ctx = Ctx(nc, st)
        sb = lambda n, s_, d: st.enter_context(nc.sbuf_tensor(n, s_, d)).ap()
        ctx.block = st.enter_context(nc.Block())

        castsem = ctx.sem()
        def _cast_sec(g):
            g.dma_start(out=wob, in_=wout).then_inc(castsem, 16)
        ctx.block.gpsimd(_cast_sec)

        nrmq = sb("nrmq", [128, 32], F32)
        ones_b = sb("ones_b", [128, 128], BF16)
        one1 = sb("one1", [1, 128], F32)
        onc128 = sb("onc128", [128, 1], BF16)
        rb = sb("rb", [128, D], F32)
        epsc = sb("epsc", [128, 1], F32)
        idf64 = sb("idf64", [MH, MH], F32)

        def p0(sy):
            sy.dma(nrmq, nrmw)
        def p0g(g):
            g.memset(ones_b, 1.0)
            g.memset(one1, 1.0)
            g.memset(onc128, 1.0)
            g.memset(epsc, float(EPS))
            g.memset(idf64, 0.0)
            g.affine_select(idf64, idf64, [[-1, MH]], ALU.not_equal, 1.0,
                            base=0, channel_multiplier=1)
        ctx.phase({"sync": p0, "gpsimd": p0g})

        with nc.sbuf_tensor("si", [128, MH, NCH, MP], BF16) as _si, \
             nc.sbuf_tensor("dexq", [128, 32], F32) as _dx, \
             nc.sbuf_tensor("ddg", [128, 32, 128], BF16) as _dd, \
             nc.sbuf_tensor("dtq", [128, NT, MH], F32) as _dq, \
             nc.sbuf_tensor("naq", [128, NT, MH], F32) as _nq, \
             nc.sbuf_tensor("acs_s", [MH, T], F32) as _ac:
            si, dexq, ddg = _si.ap(), _dx.ap(), _dd.ap()
            dtq, naq, acs_s = _dq.ap(), _nq.ap(), _ac.ap()

            def ab0_sy(sy):
                sy.dma(dexq, dexp)
                sy.dma(dtq, dtt.rearrange("(n p) h -> p n h", p=128))
                sy.dma(naq, nactk.rearrange("(n p) h -> p n h", p=128))
                sy.dma(acs_s, acsf)
            ctx.phase({"sync": ab0_sy})
            def p1_v(v):
                for h2 in range(32):
                    v.tensor_scalar(ddg[:, h2, :], ones_b, dexq[:, h2:h2 + 1], None, ALU.mult)
            ctx.phase({"vector": p1_v})
            def p1_g(g):
                for h2 in range(32):
                    g.affine_select(ddg[:, h2, :], ddg[:, h2, :], [[-1, 128]],
                                    ALU.is_equal, 0.0, base=0, channel_multiplier=1)
            ctx.phase({"gpsimd": p1_g})

            with contextlib.ExitStack() as stA:
                sbA = lambda n, s_, d: stA.enter_context(nc.sbuf_tensor(n, s_, d)).ap()
                psA_ = lambda n, s_, d: stA.enter_context(nc.psum_tensor(n, s_, d)).ap()
                ce16s = sbA("ce16s", [16, MH], F32)
                m16 = sbA("m16", [16, 1], F32)
                umask = sbA("umask", [16, NCH], F32)
                rhsu = sbA("rhsu", [16, MH, NCH], F32)
                pcsr = sbA("pcsr", [1, NCH * MH], F32)
                lin16 = sbA("lin16", [16, 16], F32)
                s16 = sbA("s16", [16, MH], F32)
                argt = sbA("argt", [16, MH, NCH], F32)
                mzs = sbA("mzs", [16, MH, NCH], F32)
                wtb = sbA("wtb", [16, MH, NCH], BF16)
                on16 = sbA("on16", [16, 1], F32)
                zrow = sbA("zrow", [128, NG8 * MH], BF16)
                wbd = sbA("wbd", [128, NG8, MH], BF16)
                stg = sbA("stg", [128, 2, 8192], BF16)
                psSm = psA_("psSm", [16, 1600], F32)
                psq = psA_("psq", [128, 16, MH], F32)
                def a0_sy(sy):
                    sy.dma(ce16s, ce16t)
                    sy.dma(m16, msk16)
                def a0_g(g):
                    g.memset(on16, 1.0)
                    g.memset(zrow, 0.0)
                    g.memset(umask, 1.0)
                    g.affine_select(umask, umask, [[1, NCH]], ALU.is_ge, 0.0,
                                    base=7, channel_multiplier=-1)
                    g.memset(mzs, 1.0)
                    g.affine_select(mzs, mzs, [[0, MH], [1, NCH]], ALU.is_ge, 0.0,
                                    base=7, channel_multiplier=-1)
                    g.memset(lin16, 1.0)
                    g.affine_select(lin16, lin16, [[1, 16]], ALU.is_ge, 0.0,
                                    base=0, channel_multiplier=-1)
                ctx.phase({"sync": a0_sy, "gpsimd": a0_g})
                if _L4_UPTO <= 0.2:
                    return nc
                def az_sy(sy):
                    for gg in range(NG8):
                        sy.dma(wbd_d[gg], zrow[:, 0:MH])
                ctx.phase({"sync": az_sy})

                def a1_v(v):
                    v.tensor_tensor(rhsu,
                                    umask.rearrange("j (o z) -> j o z", o=1)
                                         .broadcast_to((16, MH, NCH)),
                                    ce16s.rearrange("j h -> j h").broadcast_to((16, MH, NCH)),
                                    ALU.mult)
                ctx.phase({"vector": a1_v})
                if _L4_UPTO <= 0.3:
                    return nc
                def a2_t(t):
                    t.matmul(psSm[0:1, 0:512], on16,
                             rhsu.rearrange("j h z -> j (h z)"), start=True, stop=True)
                    t.matmul(psSm[:, 512:512 + MH], lin16, ce16s, start=True, stop=True)
                    t.mark()
                ctx.phase({"tensor": a2_t})
                def a3_v(v):
                    v.tensor_copy(pcsr, psSm[0:1, 0:512])
                    v.tensor_copy(s16, psSm[:, 512:512 + MH])
                ctx.phase({"vector": a3_v})
                def a4_t(t):
                    t.matmul(psSm[:, 1024:1536], one1[0:1, 0:16], pcsr, start=True, stop=True)
                    t.mark()
                ctx.phase({"tensor": a4_t})
                if _L4_UPTO <= 0.4:
                    return nc
                def a5_v(v):
                    v.tensor_tensor(argt,
                                    psSm[:, 1024:1536].rearrange("c (h z) -> c h z", h=MH),
                                    s16.broadcast_to((16, MH, NCH)), ALU.subtract)
                    v.tensor_scalar_min(argt, argt, 0.0)
                ctx.phase({"vector": a5_v})
                ctx.phase({"scalar": lambda s_: s_.activation(argt, argt, AF.Exp)})
                def a7_v(v):
                    v.tensor_tensor(argt, argt, mzs, ALU.mult)
                    v.tensor_scalar(wtb, argt, m16, None, ALU.mult)
                ctx.phase({"vector": a7_v})
                if _L4_UPTO <= 0.5:
                    return nc
                def a8_g(g):
                    for gg in range(NG8):
                        for hh in range(NG8):
                            g.dma(wbd_d[gg, hh * 16:(hh + 1) * 16, hh * 8:(hh + 1) * 8],
                                  wtb[:, gg * 8 + hh, :])
                ctx.phase({"gpsimd": a8_g})
                ctx.phase({"sync": lambda sy: sy.dma(wbd, wbd_d.rearrange("g p z -> p g z"))})
                if _L4_UPTO <= 0.6:
                    return nc

                def rec_sy(sy):
                    v0 = ctx.c0["vector"]
                    for gg in range(NG8):
                        if gg >= 2:
                            sy.wait_ge(ctx.ch["vector"], v0 + (gg - 1) * 8)
                        sy.dma(stg[:, gg % 2, :], st16[gg * 8:(gg + 1) * 8]
                               .rearrange("h c p n -> (h c) (p n)"), key=f"stg{gg % 2}")
                def rec_t(t):
                    v0 = ctx.c0["vector"]
                    for gg in range(NG8):
                        ctx.dwait(t, f"stg{gg % 2}", gg // 2 + 1)
                        for cb in range(8):
                            kb = gg * 8 + cb
                            if kb >= 2:
                                t.wait(ctx.ch["vector"], v0 + kb - 1)
                            for ci in range(8):
                                ch = cb * 8 + ci
                                t.matmul(psq[:, (kb % 2) * 8 + ci, :],
                                         stg[:, gg % 2, ch * 128:(ch + 1) * 128],
                                         wbd[:, gg, :], start=True, stop=True)
                            t.mark()
                def rec_v(v):
                    t0 = ctx.c0["tensor"]
                    for gg in range(NG8):
                        for cb in range(8):
                            kb = gg * 8 + cb
                            v.wait(ctx.ch["tensor"], t0 + kb + 1)
                            for ci in range(8):
                                ch = cb * 8 + ci
                                dst = si[:, gg * 8:(gg + 1) * 8, :, ch] \
                                    .rearrange("n h z -> n (h z)")
                                if ci < 7:
                                    v.u.tensor_copy(dst, psq[:, (kb % 2) * 8 + ci, :])
                                else:
                                    v.tensor_copy(dst, psq[:, (kb % 2) * 8 + ci, :])
                ctx.phase({"sync": rec_sy, "tensor": rec_t, "vector": rec_v})

            if _L4_UPTO <= 1:
                return nc
            with contextlib.ExitStack() as stB:
                sbB = lambda n, s_, d: stB.enter_context(nc.sbuf_tensor(n, s_, d)).ap()
                psB_ = lambda n, s_, d: stB.enter_context(nc.psum_tensor(n, s_, d)).ap()
                xfh = sbB("xfh", [128, 32, 1024], BF16)
                gg2 = sbB("gg2", [128, 2, CH], BF16)
                bch = sbB("bch", [128, 2, 128], BF16)
                cch = sbB("cch", [128, CH], BF16)
                xdt = sbB("xdt", [128, 2, INTER], BF16)
                stt = sbB("stt", [128, 2, 2, 2, CH], BF16)
                sdt = sbB("sdt", [128, 2, 2, CH], BF16)
                esb = sbB("esb", [128, 2, 2, 2, CH], BF16)
                estin = sbB("estin", [128, 2, 2, 2, CH], F32)
                csd = sbB("csd", [128, 2, 2, CH], BF16)
                yrb = sbB("yrb", [128, 2, CH], BF16)
                psY = psB_("psY", [128, 2, CH], F32)
                psrb = psB_("psrb", [128, 2, 2, CH], F32)
                psG = psB_("psG", [128, 2, CH], F32)
                for half in range(2):
                    ctx.phase({"sync": lambda sy, half=half: sy.dma(
                        xfh, xfm[:, half * 1024:(half + 1) * 1024]
                        .rearrange("(b p) t -> p b t", p=128))})
                    for cc_ in range(4):
                        c = half * 4 + cc_
                        def cp_sy(sy, c=c):
                            for sh in range(2):
                                j = 2 * c + sh
                                sy.dma(xdt[:, sh, :], xtk[j * 128:(j + 1) * 128, :])
                                sy.dma(bch[:, sh, :],
                                       bfm[:, c * CH + sh * 128:c * CH + (sh + 1) * 128])
                            sy.dma(cch, cfm[:, c * CH:(c + 1) * CH])
                        ctx.phase({"sync": cp_sy})
                        def cp_t(t, c=c):
                            for sh in range(2):
                                t.matmul(psG[:, sh, :], bch[:, sh, :], cch,
                                         start=True, stop=True)
                            t.mark()
                        def cp_v(v, c=c):
                            for sh in range(2):
                                v.tensor_tensor(
                                    xdt[:, sh, :].rearrange("p (h q) -> p h q", q=MP),
                                    xdt[:, sh, :].rearrange("p (h q) -> p h q", q=MP),
                                    dtq[:, 2 * c + sh, :].rearrange("p (h o) -> p h o", o=1)
                                       .broadcast_to((128, MH, MP)), ALU.mult)
                        ctx.phase({"tensor": cp_t, "vector": cp_v})
                        def cp_v2(v):
                            for sh in range(2):
                                v.tensor_copy(gg2[:, sh, :], psG[:, sh, :])
                        ctx.phase({"vector": cp_v2})
                        def cp_g(g):
                            for sh in range(2):
                                g.affine_select(gg2[:, sh, :], gg2[:, sh, :], [[1, CH]],
                                                ALU.is_ge, 0.0, base=-sh * 128,
                                                channel_multiplier=-1)
                        ctx.phase({"gpsimd": cp_g})

                        def yl_t(t, c=c, cc_=cc_):
                            v0 = ctx.c0["vector"]
                            s0 = ctx.c0["scalar"]
                            for it in range(16):
                                if it >= 1:
                                    t.wait(ctx.ch["vector"], v0 + 4 * (it - 1) + 1)
                                    t.wait(ctx.ch["scalar"], s0 + 2 * (it - 1) + 2)
                                for j in range(2):
                                    h2 = 2 * it + j
                                    for hp in range(2):
                                        sel = idf64[:, 2 * h2 + hp:2 * h2 + hp + 1] \
                                            .broadcast_to((MH, 1, 128))[:, 0, :]
                                        t.matmul(psrb[:, j, hp, :], sel,
                                                 acs_s[:, c * CH:(c + 1) * CH],
                                                 start=True, stop=True)
                                t.mark()
                                t.wait(ctx.ch["vector"], v0 + 4 * it + 3)
                                if it >= 1:
                                    t.wait(ctx.ch["vector"], v0 + 4 * (it - 1) + 4)
                                for j in range(2):
                                    h2 = 2 * it + j
                                    for hp in range(2):
                                        h = 2 * h2 + hp
                                        po = psY[hp * 64:(hp + 1) * 64, j, :]
                                        tp = (0, hp * 64)
                                        t.matmul(po, ddg[:, h2, hp * 64:(hp + 1) * 64],
                                                 xfh[:, h2, cc_ * 256:(cc_ + 1) * 256],
                                                 start=True, stop=False, tile_position=tp)
                                        for sh in range(2):
                                            t.matmul(po, xdt[:, sh, h * MP:(h + 1) * MP],
                                                     stt[:, j, hp, sh, :], start=False,
                                                     stop=False, tile_position=tp)
                                        t.matmul(po, si[:, h, c, :], csd[:, j, hp, :],
                                                 start=False, stop=True, tile_position=tp)
                                t.mark()
                        def yl_s(s_, c=c):
                            t0 = ctx.c0["tensor"]
                            v0 = ctx.c0["vector"]
                            for it in range(16):
                                s_.wait(ctx.ch["vector"], v0 + 4 * it + 1)
                                for j in range(2):
                                    for hp in range(2):
                                        for sh in range(2):
                                            op = s_.u.activation \
                                                if (j, hp, sh) != (1, 1, 1) \
                                                else s_.activation
                                            op(esb[:, j, hp, sh, :],
                                               estin[:, j, hp, sh, :], AF.Exp)
                                s_.wait(ctx.ch["tensor"], t0 + 2 * it + 1)
                                if it >= 1:
                                    s_.wait(ctx.ch["vector"], v0 + 4 * (it - 1) + 3)
                                for j in range(2):
                                    for hp in range(2):
                                        op = s_.u.activation if (j, hp) != (1, 1) \
                                            else s_.activation
                                        op(sdt[:, j, hp, :], psrb[:, j, hp, :], AF.Exp)
                        def yl_v(v, c=c):
                            t0 = ctx.c0["tensor"]
                            s0 = ctx.c0["scalar"]
                            for it in range(16):
                                v.wait(ctx.ch["tensor"], t0 + 2 * it + 1)
                                if it >= 1:
                                    v.wait(ctx.ch["scalar"], s0 + 2 * (it - 1) + 1)
                                for j in range(2):
                                    h2 = 2 * it + j
                                    for hp in range(2):
                                        for sh in range(2):
                                            op = v.u.tensor_scalar \
                                                if (j, hp, sh) != (1, 1, 1) \
                                                else v.tensor_scalar
                                            op(estin[:, j, hp, sh, :],
                                               psrb[:, j, hp, :],
                                               naq[:, 2 * c + sh,
                                                   2 * h2 + hp:2 * h2 + hp + 1],
                                               0.0, ALU.add, ALU.min)
                                v.wait(ctx.ch["scalar"], s0 + 2 * it + 1)
                                for j in range(2):
                                    for hp in range(2):
                                        for sh in range(2):
                                            op = v.u.tensor_tensor \
                                                if (j, hp, sh) != (1, 1, 1) \
                                                else v.tensor_tensor
                                            op(stt[:, j, hp, sh, :],
                                               esb[:, j, hp, sh, :],
                                               gg2[:, sh, :], ALU.mult)
                                v.wait(ctx.ch["scalar"], s0 + 2 * it + 2)
                                for j in range(2):
                                    for hp in range(2):
                                        op = v.u.tensor_tensor if (j, hp) != (1, 1) \
                                            else v.tensor_tensor
                                        op(csd[:, j, hp, :], cch,
                                           sdt[:, j, hp, :], ALU.mult)
                                v.wait(ctx.ch["tensor"], t0 + 2 * it + 2)
                                if it >= 1:
                                    ctx.dwait(v, "yrb0", ctx.d0.get("yrb0", 0) + it)
                                    ctx.dwait(v, "yrb1", ctx.d0.get("yrb1", 0) + it)
                                v.u.tensor_copy(yrb[:, 0, :], psY[:, 0, :])
                                v.tensor_copy(yrb[:, 1, :], psY[:, 1, :])
                        def yl_g(g, c=c):
                            v0 = ctx.c0["vector"]
                            for it in range(16):
                                g.wait_ge(ctx.ch["vector"], v0 + 4 * it + 4)
                                g.dma(yraw[2 * it, c], yrb[:, 0, :], key="yrb0")
                                g.dma(yraw[2 * it + 1, c], yrb[:, 1, :], key="yrb1")
                        ctx.phase({"tensor": yl_t, "scalar": yl_s, "vector": yl_v,
                                   "gpsimd": yl_g})

        if _L4_UPTO <= 2:
            return nc
        with contextlib.ExitStack() as stB2:
            sbC = lambda n, s_, d: stB2.enter_context(nc.sbuf_tensor(n, s_, d)).ap()
            zbl = sbC("zbl", [128, 2, T], BF16)
            szb = sbC("szb", [128, T], BF16)
            ybl = sbC("ybl", [128, 2, T], BF16)
            yfb = sbC("yfb", [128, 2, T], BF16)
            sqb = sbC("sqb", [128, 2, T], BF16)
            psV = stB2.enter_context(nc.psum_tensor("psV", [1, D], F32)).ap()
            def b2_sy(sy):
                s0 = ctx.c0["scalar"]
                v0 = ctx.c0["vector"]
                for h2 in range(32):
                    if h2 >= 2:
                        sy.wait_ge(ctx.ch["scalar"], s0 + 2 * (h2 - 1))
                        sy.wait_ge(ctx.ch["vector"], v0 + (h2 - 1))
                    sy.dma(zbl[:, h2 % 2, :], zfm[h2 * 128:(h2 + 1) * 128, :],
                           key=f"zl{h2 % 2}")
                    sy.dma(ybl[:, h2 % 2, :].rearrange("p (c l) -> p c l", l=CH),
                           yraw[h2].rearrange("c p l -> p c l"), key=f"zl{h2 % 2}")
            def b2_s(s_):
                v0 = ctx.c0["vector"]
                t0 = ctx.c0["tensor"]
                for h2 in range(32):
                    ctx.dwait(s_, f"zl{h2 % 2}", 2 * (h2 // 2 + 1))
                    s_.activation(szb, zbl[:, h2 % 2, :], AF.Silu)
                    s_.wait(ctx.ch["vector"], v0 + h2 + 1)
                    if h2 >= 2:
                        s_.wait(ctx.ch["tensor"], t0 + h2 - 1)
                    s_.activation(sqb[:, h2 % 2, :], yfb[:, h2 % 2, :], AF.Square)
            def b2_v(v):
                s0 = ctx.c0["scalar"]
                for h2 in range(32):
                    v.wait(ctx.ch["scalar"], s0 + 2 * h2 + 1)
                    if h2 >= 2:
                        ctx.dwait(v, f"yfs{h2 % 2}", h2 // 2)
                    v.tensor_tensor(yfb[:, h2 % 2, :], ybl[:, h2 % 2, :], szb, ALU.mult)
            def b2_t(t):
                s0 = ctx.c0["scalar"]
                for h2 in range(32):
                    t.wait(ctx.ch["scalar"], s0 + 2 * h2 + 2)
                    for nb in range(4):
                        cs = slice(nb * 512, (nb + 1) * 512)
                        t.matmul(psV[0:1, cs], onc128, sqb[:, h2 % 2, cs],
                                 start=(h2 == 0), stop=(h2 == 31))
                    t.mark()
            def b2_g(g):
                v0 = ctx.c0["vector"]
                for h2 in range(32):
                    g.wait_ge(ctx.ch["vector"], v0 + h2 + 1)
                    g.dma(yf_d[h2], yfb[:, h2 % 2, :], key=f"yfs{h2 % 2}")
            ctx.phase({"sync": b2_sy, "scalar": b2_s, "vector": b2_v,
                       "tensor": b2_t, "gpsimd": b2_g})

            with contextlib.ExitStack() as stRS:
                vrow = stRS.enter_context(nc.sbuf_tensor("vrow", [1, D], F32)).ap()
                psRB = stRS.enter_context(nc.psum_tensor("psRB", [128, D], F32)).ap()
                ctx.phase({"vector": lambda v: v.tensor_scalar(
                    vrow, psV, 1.0 / INTER, None, ALU.mult)})
                def rs_s(s_):
                    s_.activation(vrow, vrow, AF.Ln, bias=epsc[0:1, :])
                    s_.activation(vrow, vrow, AF.Exp, scale=-0.5)
                ctx.phase({"scalar": rs_s})
                def rs_t(t):
                    for nb in range(4):
                        cs = slice(nb * 512, (nb + 1) * 512)
                        t.matmul(psRB[:, cs], one1, vrow[:, cs], start=True, stop=True)
                    t.mark()
                ctx.phase({"tensor": rs_t})
                ctx.phase({"vector": lambda v: v.tensor_copy(rb, psRB)})

        if _L4_UPTO <= 3:
            return nc
        with contextlib.ExitStack() as stC:
            sbD = lambda n, s_, d: stC.enter_context(nc.sbuf_tensor(n, s_, d)).ap()
            wos = sbD("wos", [128, 32, 1024], BF16)
            ynh = sbD("ynh", [128, 32, 1024], BF16)
            yfl = sbD("yfl", [128, 2, 1024], BF16)
            xmt = sbD("xmt", [128, 2, 1024], F32)
            x2r = sbD("x2r", [128, 2, 1024], F32)
            psO = stC.enter_context(nc.psum_tensor("psO", [128, 1024], F32)).ap()
            x2t_d = x2_o.rearrange("(n p) d -> p n d", p=128)
            xmt_d = xm.rearrange("(n p) d -> p n d", p=128)

            for half in range(2):
                def c1_sy(sy, half=half):
                    v0 = ctx.c0["vector"]
                    for h2 in range(32):
                        if h2 >= 2:
                            sy.wait_ge(ctx.ch["vector"], v0 + 2 * (h2 - 1))
                        sy.dma(yfl[:, h2 % 2, :],
                               yf_d[h2, :, half * 1024:(half + 1) * 1024],
                               key=f"yfl{h2 % 2}")
                def c1_v(v, half=half):
                    for h2 in range(32):
                        key = f"yfl{h2 % 2}"
                        ctx.dwait(v, key, ctx.d0.get(key, 0) + h2 // 2 + 1)
                        v.tensor_tensor(yfl[:, h2 % 2, :], yfl[:, h2 % 2, :],
                                        rb[:, half * 1024:(half + 1) * 1024], ALU.mult)
                        v.tensor_scalar(ynh[:, h2, :], yfl[:, h2 % 2, :],
                                        nrmq[:, h2:h2 + 1], None, ALU.mult)
                ctx.phase({"sync": c1_sy, "vector": c1_v})

                for dhalf in range(2):
                    def wl_g(g, dhalf=dhalf):
                        g.wait_ge(castsem, 16)
                        for h2 in range(32):
                            g.dma(wos[:, h2, :],
                                  wob[h2 * 128:(h2 + 1) * 128,
                                      dhalf * 1024:(dhalf + 1) * 1024])
                    ctx.phase({"gpsimd": wl_g})

                    def c2_sy(sy, half=half, dhalf=dhalf):
                        v0 = ctx.c0["vector"]
                        for tc_ in range(NT // 2):
                            tc = half * (NT // 2) + tc_
                            if tc_ >= 2:
                                sy.wait_ge(ctx.ch["vector"], v0 + tc_ - 1)
                            sy.dma(xmt[:, tc_ % 2, :],
                                   xmt_d[:, tc, dhalf * 1024:(dhalf + 1) * 1024],
                                   key=f"xmt{tc_ % 2}")
                    def c2_t(t, half=half):
                        v0 = ctx.c0["vector"]
                        for tc_ in range(NT // 2):
                            if tc_ >= 1:
                                t.wait(ctx.ch["vector"], v0 + tc_)
                            for h2 in range(32):
                                for nb in range(2):
                                    cs = slice(nb * 512, (nb + 1) * 512)
                                    t.matmul(psO[:, cs],
                                             ynh[:, h2, tc_ * 128:(tc_ + 1) * 128],
                                             wos[:, h2, cs],
                                             start=(h2 == 0), stop=(h2 == 31))
                            t.mark()
                    def c2_v(v, half=half):
                        t0 = ctx.c0["tensor"]
                        for tc_ in range(NT // 2):
                            v.wait(ctx.ch["tensor"], t0 + tc_ + 1)
                            key = f"xmt{tc_ % 2}"
                            ctx.dwait(v, key, ctx.d0.get(key, 0) + tc_ // 2 + 1)
                            if tc_ >= 2:
                                key2 = f"x2s{tc_ % 2}"
                                ctx.dwait(v, key2, ctx.d0.get(key2, 0) + tc_ // 2)
                            v.tensor_tensor(x2r[:, tc_ % 2, :], psO, xmt[:, tc_ % 2, :],
                                            ALU.add)
                    def c2_s(s_, half=half, dhalf=dhalf):
                        v0 = ctx.c0["vector"]
                        for tc_ in range(NT // 2):
                            tc = half * (NT // 2) + tc_
                            s_.wait(ctx.ch["vector"], v0 + tc_ + 1)
                            s_.dma(x2t_d[:, tc, dhalf * 1024:(dhalf + 1) * 1024],
                                   x2r[:, tc_ % 2, :], key=f"x2s{tc_ % 2}")
                    ctx.phase({"sync": c2_sy, "tensor": c2_t, "vector": c2_v,
                               "scalar": c2_s})

        if _L4_UPTO <= 4:
            return nc
        emit_ln_router(nc, ctx, st, x2_o, xn3_o, eidx, gate, wg, "q_")
        with contextlib.ExitStack() as stK:
            emit_compact(nc, ctx, stK, eidx, xn3_o, xcomp, lc, "k2_")
    return nc


def prep_ffn_arrays(inputs):
    W1 = np.ascontiguousarray(inputs["L0_ffn_W1"], dtype=np.float32)
    W3 = np.ascontiguousarray(inputs["L0_ffn_W3"], dtype=np.float32)
    W2 = np.ascontiguousarray(inputs["L0_ffn_W2"], dtype=np.float32)
    sw1 = np.ascontiguousarray(inputs["L0_ffn_sw1"], dtype=np.float32)
    sw3 = np.ascontiguousarray(inputs["L0_ffn_sw3"], dtype=np.float32)
    sw2 = np.ascontiguousarray(inputs["L0_ffn_sw2"], dtype=np.float32)
    w1r = np.ascontiguousarray(W1.reshape(E, KD, 128, HC, 128).transpose(0, 3, 2, 1, 4))
    w3r = np.ascontiguousarray(W3.reshape(E, KD, 128, HC, 128).transpose(0, 3, 2, 1, 4))
    w2r = np.ascontiguousarray(W2.reshape(E, HC, 128, D))
    sw1r = np.ascontiguousarray(sw1.reshape(KD, 128, HC, 128).transpose(2, 1, 0, 3))
    sw3r = np.ascontiguousarray(sw3.reshape(KD, 128, HC, 128).transpose(2, 1, 0, 3))
    sw2r = np.ascontiguousarray(sw2.reshape(HC, 128, D))
    return dict(w1r=w1r, w3r=w3r, w2r=w2r, sw1r=sw1r, sw3r=sw3r, sw2r=sw2r)


def ffn_in_maps(x_sl, xn_sl, eidx_cores, gate_cores, wf):
    eidx_pi = np.ascontiguousarray(
        np.concatenate([eidx_cores[c].T for c in range(NCORES)], axis=0))
    maps = []
    for c in range(NCORES):
        selv = (np.arange(128) < c * NT).astype(np.float32).reshape(128, 1)
        maps.append(dict(
            xin=x_sl[c], xn=xn_sl[c], eidx_pi=eidx_pi, selv=selv,
            eoq=np.ascontiguousarray(eidx_cores[c]),
            gateq=np.ascontiguousarray(gate_cores[c]), **wf))
    return maps


_CACHE = {}


def _prog(name, builder):
    if name not in _CACHE:
        _CACHE[name] = builder()
    return _CACHE[name]


def _run(nc, maps):
    return run_bass_kernel_spmd(nc, maps, core_ids=list(range(NCORES)))


def kernel(**inputs):
    f32 = np.float32
    ca = np.ascontiguousarray
    x = ca(inputs["L0_attn_hidden_states"], dtype=f32)
    xf = x.reshape(NCORES * T, D)
    x_sl = [ca(xf[c * T:(c + 1) * T]) for c in range(NCORES)]
    wg = ca(inputs["L0_ffn_Wg"], dtype=f32)
    wf = prep_ffn_arrays(inputs)

    W_in = ca(inputs["L0_attn_W_in"], dtype=f32)
    winp = np.zeros((D, CCW * 128), f32)
    winp[:, :OUT_IN] = W_in
    winr = ca(winp.reshape(KD, 128, CCW, 128).transpose(2, 1, 0, 3))
    conv_w = ca(inputs["L0_attn_conv_w"], dtype=f32)
    conv_b = ca(inputs["L0_attn_conv_b"], dtype=f32)
    convwr = ca(conv_w.T.reshape(CCI, 128, 4).transpose(1, 0, 2))
    convbr = ca(conv_b.reshape(CCI, 128).T)
    dtbr = ca(inputs["L0_attn_dt_bias"], dtype=f32).reshape(MH, 1)
    alogr = ca(inputs["L0_attn_A_log"], dtype=f32).reshape(MH, 1)
    dexp = ca(np.repeat(ca(inputs["L0_attn_D"], dtype=f32), MP).reshape(32, 128).T)
    nrmw = ca(ca(inputs["L0_attn_norm_weight"], dtype=f32).reshape(32, 128).T)
    wout = ca(inputs["L0_attn_W_out"], dtype=f32)

    r1 = _run(_prog("l1", build_l1),
              [{"x0": x_sl[c], "wg": wg} for c in range(NCORES)])
    xn1 = [r1.results[c]["xn1"] for c in range(NCORES)]
    ei1 = [r1.results[c]["eidx"] for c in range(NCORES)]
    ga1 = [r1.results[c]["gate"] for c in range(NCORES)]
    xc1 = [r1.results[c]["xcomp"] for c in range(NCORES)]
    lc1 = [r1.results[c]["lc"] for c in range(NCORES)]

    r2 = _run(_prog("fexp", build_ffn_exp), ffn_exp_in_maps(xn1, xc1, lc1, wf))
    oe1 = [r2.results[c]["oe_o"] for c in range(NCORES)]
    sh1 = [r2.results[c]["shro"] for c in range(NCORES)]

    oeall1 = np.concatenate(oe1, axis=0)
    eip1 = ca(np.concatenate([e.T for e in ei1], axis=0)).astype(np.uint32)
    z128f = np.zeros((128, D), f32)
    z128b = np.zeros((128, D), BF)
    ncol_u = np.full((128, 1), 255, np.uint32)
    zcol_f = np.zeros((128, 1), f32)
    maps3 = []
    for c in range(NCORES):
        odd = c % 2 == 1
        xblb = ca(xf[c * T - 128:c * T]) if odd else z128f
        shlb = ca(sh1[c - 1][T - 128:]) if odd else z128b
        lbq = ca(ei1[c - 1][:, NT - 1:NT]) if odd else ncol_u
        lbg = ca(ga1[c - 1][:, NT - 1:NT]) if odd else zcol_f
        selv = (np.arange(128) < (c * NT - 1 if odd else c * NT)
                ).astype(f32).reshape(128, 1)
        maps3.append(dict(
            xb=x_sl[c], xblb=xblb, shro=sh1[c], shlb=shlb, oeall=oeall1,
            eidx_pi=eip1, selvlb=selv,
            eoq17=ca(np.concatenate([lbq, ei1[c]], axis=1)),
            gq17=ca(np.concatenate([lbg, ga1[c]], axis=1)),
            winr=winr, convwr=convwr, convbr=convbr, dtbr=dtbr, alogr=alogr))
    r3 = _run(_prog("l3", build_l3), maps3)
    xmid = [r3.results[c]["xmid_o"] for c in range(NCORES)]

    maps4 = []
    zst = np.zeros((MH, NCH, MP, MN), BF)
    zce = np.zeros((NCH, MH), f32)
    mk_even = np.concatenate([np.zeros((8, 1)), np.ones((8, 1))]).astype(f32)
    mk_odd = np.ones((16, 1), f32)
    for c in range(NCORES):
        o = r3.results[c]
        own_st = ca(o["stn_o"].reshape(NCH, MN, MH, MP).transpose(2, 0, 3, 1))
        own_ce = ca(o["ce_o"].T)
        if c % 2 == 1:
            po = r3.results[c - 1]
            par_st = ca(po["stn_o"].reshape(NCH, MN, MH, MP).transpose(2, 0, 3, 1))
            par_ce = ca(po["ce_o"].T)
            msk = mk_odd
        else:
            par_st, par_ce, msk = zst.transpose(0, 1, 2, 3), zce, mk_even
        st16 = ca(np.concatenate([par_st, own_st], axis=1))
        ce16 = ca(np.concatenate([par_ce, own_ce], axis=0))
        maps4.append(dict(
            xm=xmid[c], zfm=o["z_o"], xfm=o["xfm_o"],
            xtk=ca(o["xfm_o"].T), bfm=o["bfm_o"], cfm=o["cfm_o"],
            dtt=ca(o["dtf_o"].T), acsf=o["acsf_o"], nactk=ca(o["nacsf_o"].T),
            ce16t=ce16, msk16=msk, st16=st16, dexp=dexp, nrmw=nrmw,
            wout=wout, wg=wg))
    r4 = _run(_prog("l4", build_l4), maps4)
    x2 = [r4.results[c]["x2_o"] for c in range(NCORES)]
    xn3 = [r4.results[c]["xn3_o"] for c in range(NCORES)]
    ei2 = [r4.results[c]["eidx"] for c in range(NCORES)]
    ga2 = [r4.results[c]["gate"] for c in range(NCORES)]
    xc2 = [r4.results[c]["xcomp"] for c in range(NCORES)]
    lc2 = [r4.results[c]["lc"] for c in range(NCORES)]

    r5 = _run(_prog("fexp", build_ffn_exp), ffn_exp_in_maps(xn3, xc2, lc2, wf))
    oe2 = [r5.results[c]["oe_o"] for c in range(NCORES)]
    sh2 = [r5.results[c]["shro"] for c in range(NCORES)]

    r6 = _run(_prog("fin", build_final), final_in_maps(x2, oe2, sh2, ei2, ga2))
    out = np.concatenate([r6.results[c]["out"] for c in range(NCORES)], axis=0)
    return out.reshape(4, 4096, 2048)
